# version 05
# baseline (speedup 1.0000x reference)
import math
import contextlib
import numpy as np
import ml_dtypes

import concourse.bass as bass
import concourse.mybir as mybir
from concourse.bass_utils import run_bass_kernel_spmd

F32 = mybir.dt.float32
BF16 = mybir.dt.bfloat16
I32 = mybir.dt.int32
AF = mybir.ActivationFunctionType
ALU = mybir.AluOpType
AX = mybir.AxisListType.X

TWO_PI = 6.283185307179586
PI = 3.141592653589793

D_MODEL = 1024
NCORES = 8


def build_consts(T):
    c = {}
    bf = ml_dtypes.bfloat16
    c["ident_f32"] = np.eye(128, dtype=np.float32)
    c["ones_f32"] = np.ones((128, 128), np.float32)
    tri = np.tril(np.ones((128, 128), np.float32))
    c["tri01_f32"] = tri
    mt = np.zeros((4, 128, 512), np.float32)
    for k in range(4):
        for c2 in range(128):
            t1 = np.arange(512)
            mt[k, c2, :] = (c2 + 128 * k <= t1).astype(np.float32)
    c["attnb_maskT"] = mt
    DH = 64
    inv_freq = (1.0 / 10000.0) ** (np.arange(0, DH, 2, dtype=np.float32) / DH)
    t = np.arange(T, dtype=np.float32)
    fr = np.outer(t, inv_freq)
    c["rope_cos"] = np.cos(fr).astype(np.float32)
    c["rope_sin"] = np.sin(fr).astype(np.float32)
    DEC = 0.9
    l = np.arange(128, dtype=np.float64)
    mT = np.zeros((128, 128), np.float64)
    for m in range(128):
        mT[m, m:] = DEC ** (l[m:] - m)
    c["l1b_maskT"] = mT.astype(np.float32)
    c["l1b_w_in"] = np.broadcast_to((DEC ** (l + 1.0))[None, :], (128, 128)).astype(np.float32).copy()
    c["l1b_w_out"] = (DEC ** (127.0 - l)).astype(np.float32).reshape(128, 1)
    dv = np.zeros((32, T), np.float32)
    dv[:28, :] = np.maximum(T - 1.0 - np.arange(T, dtype=np.float32), 0.0)[None, :]
    dv[28:, :] = np.arange(T, dtype=np.float32)[None, :]
    c["l1a_dvec"] = dv
    e3 = np.zeros((128, 1024), np.float32)
    for k in range(32):
        e3[:, 32 * k + k] = 1.0
    c["l1a_e3"] = e3
    return c



ENGS = ("tensor", "vector", "scalar", "gpsimd", "sync")
COMPUTE_SEMS = {"tensor": "T", "vector": "V", "scalar": "S", "gpsimd": "G"}


class EM:

    def __init__(self, nc):
        self.nc = nc
        self.recs = {e: [] for e in ENGS}
        self.ct = {s: 0 for s in COMPUTE_SEMS.values()}
        self.streams = {}
        self.stream_q = {}
        self.pending = {e: [] for e in ENGS}
        self.last_wait = {}
        self.nops = 0

    AUTO_SER = ("vector", "scalar", "gpsimd")

    def op(self, eng, meth, *args, **kw):
        if eng in self.AUTO_SER and self.recs[eng] and meth != "nop":
            prev = self.recs[eng][-1][1]
            if prev[0] != "dma_start" and prev[0] != "nop":
                tok = self.mark(eng)
                self.wait(eng, tok)
        rec = [meth, args, kw, []]
        self.recs[eng].append((self.pending[eng], rec))
        self.pending[eng] = []
        return rec

    def mark(self, eng):
        sem = COMPUTE_SEMS[eng]
        assert self.recs[eng], f"mark on empty engine {eng}"
        rec = self.recs[eng][-1][1]
        if rec[0] == "dma_start":
            rec = self.op(eng, "nop")
        for (s, n) in rec[3]:
            if s == sem:
                return (sem, rec[4])
        rec[3].append((sem, 1))
        self.ct[sem] += 1
        rec.append(self.ct[sem])
        return (sem, self.ct[sem])

    def dma(self, q, out, in_, stream="misc", **kw):
        sem = "D_" + stream
        qq = self.stream_q.setdefault(sem, q)
        assert qq == q, f"stream {stream} used from {qq} and {q}"
        self.streams[sem] = self.streams.get(sem, 0) + 16
        rec = self.op(q, "dma_start", out=out, in_=in_, **kw)
        rec[3].append((sem, 16))
        return (sem, self.streams[sem])

    def wait(self, eng, tok):
        if tok is None:
            return
        sem, val = tok
        if sem.startswith("D_") and val != self.streams.get(sem):
            raise AssertionError(
                f"unsafe partial DMA wait on {sem}: tok {val} != issued {self.streams.get(sem)}")
        key = (eng, sem)
        if self.last_wait.get(key, 0) >= val:
            return
        self.pending[eng].append((sem, val))
        self.last_wait[key] = val

    def nop(self, eng):
        self.nops += 1
        return self.op(eng, "nop")

    def barrier(self, engs=("tensor", "vector", "scalar", "gpsimd", "sync")):
        toks = []
        for e in ("tensor", "vector", "scalar", "gpsimd"):
            if not self.recs[e]:
                continue
            toks.append(self.mark(e))
        for s, v in self.streams.items():
            toks.append((s, v))
        for e in engs:
            for t in toks:
                self.wait(e, t)

    def flush(self, block, sems):
        for engname in ENGS:
            recs = self.recs[engname]
            tail = self.pending[engname]
            if not recs and not tail:
                continue

            def runner(eng, recs=recs, tail=tail):
                for waits, rec in recs:
                    for (s, v) in waits:
                        eng.wait_ge(sems[s], v)
                    meth, args, kw, incs = rec[0], rec[1], rec[2], rec[3]
                    instr = getattr(eng, meth)(*args, **kw)
                    for (s, n) in incs:
                        instr.then_inc(sems[s], n)
                for (s, v) in tail:
                    eng.wait_ge(sems[s], v)

            getattr(block, engname)(runner)



def ceil_div(a, b):
    return (a + b - 1) // b


class Prog:

    def __init__(self, nc, em, stack, consts_sb, ident_bf, ident_f32):
        self.nc = nc
        self.em = em
        self.stack = stack
        self.csb = consts_sb
        self.ident_bf = ident_bf
        self.ident_f32 = ident_f32
        self.uid = 0

    def name(self, p):
        self.uid += 1
        return f"{p}{self.uid}"

    def sbuf(self, shape, dtype, stack=None):
        st = stack or self.stack
        return st.enter_context(self.nc.sbuf_tensor(self.name("sb"), list(shape), dtype))

    def psum(self, shape, dtype=F32, stack=None):
        st = stack or self.stack
        return st.enter_context(self.nc.psum_tensor(self.name("ps"), list(shape), dtype))


def emit_transpose(P, dst_ap, src_ap, ps_ap, ident_ap, src_tok=None, ps_free_tok=None,
                   copy_eng="vector"):
    em = P.em
    if src_tok is not None:
        em.wait("tensor", src_tok)
    if ps_free_tok is not None:
        em.wait("tensor", ps_free_tok)
    em.op("tensor", "transpose", ps_ap, src_ap, ident_ap)
    t_tok = em.mark("tensor")
    em.wait(copy_eng, t_tok)
    em.op(copy_eng, "tensor_copy", dst_ap, ps_ap)
    c_tok = em.mark(copy_eng)
    return t_tok, c_tok


def emit_load_cols(P, dram_vec, n, dtype=F32, stack=None):
    em = P.em
    nt = ceil_div(n, 128)
    nfull = n // 128
    col = P.sbuf([128, nt], dtype, stack)
    ist = contextlib.ExitStack()
    stage = P.sbuf([128, 256], F32, ist)
    ps = P.psum([128, 32], F32, ist)
    em.op("vector", "memset", col[:, :], 0.0)
    mz = em.mark("vector")
    last_tok = mz
    if nfull > 0:
        src2d = dram_vec[0:nfull * 128].rearrange("(a b) -> a b", b=128)
        tok = em.dma("sync", stage[0:nfull, 0:128], src2d[0:nfull, :], stream="cols")
        em.wait("tensor", tok)
        em.op("tensor", "transpose", ps[:, 0:nfull], stage[0:nfull, 0:128],
              P.ident_f32[0:nfull, 0:nfull])
        ttok = em.mark("tensor")
        em.wait("vector", ttok)
        em.op("vector", "tensor_copy", col[:, 0:nfull], ps[:, 0:nfull])
        last_tok = em.mark("vector")
    rem = n - nfull * 128
    if rem > 0:
        tok = em.dma("sync", stage[0:1, 128:128 + rem],
                     dram_vec[nfull * 128:n].rearrange("(a b) -> a b", a=1), stream="cols2")
        em.wait("tensor", tok)
        em.wait("tensor", last_tok)
        em.op("tensor", "transpose", ps[0:rem, nfull:nfull + 1], stage[0:1, 128:128 + rem],
              P.ident_f32[0:1, 0:1])
        ttok = em.mark("tensor")
        em.wait("vector", ttok)
        em.op("vector", "tensor_copy", col[0:rem, nfull:nfull + 1],
              ps[0:rem, nfull:nfull + 1])
        last_tok = em.mark("vector")
    em.barrier()
    ist.close()
    return col, last_tok


def emit_weight_loadT(P, w_dram, rows, cols, dtype=BF16, stack=None):
    em = P.em
    rt = ceil_div(rows, 128)
    ct = ceil_div(cols, 128)
    out = P.sbuf([128, ct, rows], dtype, stack)
    ident = P.ident_bf if dtype == BF16 else P.ident_f32
    with contextlib.ExitStack() as st:
        stage = P.sbuf([128, 2, cols], dtype, st)
        ps = P.psum([128, 4096], dtype, st) if dtype == BF16 else P.psum([128, 2048], dtype, st)
        ps_busy = [None] * 4
        stage_busy = [None, None]
        tok = None
        for r in range(rt):
            r0, r1 = r * 128, min(rows, r * 128 + 128)
            sl = r % 2
            if stage_busy[sl] is not None:
                em.wait("gpsimd", stage_busy[sl])
            dtok = em.dma("gpsimd", stage[0:r1 - r0, sl, :], w_dram[r0:r1, :], stream=f"wl{sl}")
            em.wait("tensor", dtok)
            for cc in range(ct):
                c0, c1 = cc * 128, min(cols, cc * 128 + 128)
                slot = (r * ct + cc) % 4
                if ps_busy[slot] is not None:
                    em.wait("tensor", ps_busy[slot])
                off = slot * (1024 if dtype == BF16 else 512)
                em.op("tensor", "transpose",
                      ps[0:c1 - c0, off:off + (r1 - r0)],
                      stage[0:r1 - r0, sl, c0:c1], ident[0:r1 - r0, 0:r1 - r0])
                ttok = em.mark("tensor")
                em.wait("vector", ttok)
                em.op("vector", "tensor_copy", out[0:c1 - c0, cc, r0:r1],
                      ps[0:c1 - c0, off:off + (r1 - r0)])
                tok = em.mark("vector")
                ps_busy[slot] = tok
            stage_busy[sl] = tok
        em.barrier()
    return out, tok


DBG = {}


def emit_ln(P, x_dram, t0, t1, xlnT, xln_off, tm_out=None, eps=1e-5, rms_scale_col=None,
            rms=False, out_dram=None):
    if xlnT is None:
        return _emit_ln_impl(P, x_dram, t0, t1, None, 0, tm_out, eps, rms, out_dram)
    return _emit_ln_impl(P, x_dram, t0, t1, xlnT, xln_off, tm_out, eps, rms, out_dram)


def _emit_ln_impl(P, x_dram, t0, t1, xlnT, xln_off, tm_out, eps, rms, out_dram):
    em = P.em
    nt = (t1 - t0) // 128
    with contextlib.ExitStack() as st:
        xt = P.sbuf([128, 2, 1024], F32, st)
        sq = P.sbuf([128, 1024], F32, st)
        stats = P.sbuf([128, 8], F32, st)
        tmb = P.sbuf([128, 1024], BF16, st) if tm_out is None else None
        pst = P.psum([128, 2048], BF16, st)
        busy = {}
        ps_busy = [None, None]
        sq_war = None
        for i in range(nt):
            buf = i % 2
            if busy.get(buf) is not None:
                em.wait("sync", busy[buf])
            ld = em.dma("sync", xt[:, buf, :], x_dram[t0 + i * 128:t0 + (i + 1) * 128, :], stream=f"lnx{buf}")
            xtile = xt[:, buf, :]
            mu = stats[:, 0:1]
            msq = stats[:, 1:2]
            var = stats[:, 2:3]
            lnv = stats[:, 3:4]
            rstd = stats[:, 4:5]
            em.wait("vector", ld)
            em.wait("scalar", ld)
            em.wait("scalar", sq_war)
            em.op("scalar", "activation", sq[:, :], xtile, AF.Square)
            sq_tok = em.mark("scalar")
            if not rms:
                em.op("vector", "reduce_sum", mu, xtile, AX)
                em.mark("vector")
            em.wait("vector", sq_tok)
            em.op("vector", "reduce_sum", msq, sq[:, :], AX)
            sq_war = em.mark("vector")
            if rms:
                em.op("vector", "tensor_scalar", var, msq, 1.0 / 1024.0, None, ALU.mult)
            else:
                em.op("vector", "tensor_scalar", mu, mu, 1.0 / 1024.0, None, ALU.mult)
                em.op("vector", "tensor_scalar", msq, msq, 1.0 / 1024.0, None, ALU.mult)
                em.op("vector", "tensor_tensor", var, mu, mu, ALU.mult)
                em.op("vector", "tensor_tensor", var, msq, var, ALU.subtract)
            em.op("vector", "tensor_scalar", var, var, float(eps), None, ALU.add)
            v_tok = em.mark("vector")
            em.wait("scalar", v_tok)
            em.op("scalar", "activation", lnv, var, AF.Ln)
            em.mark("scalar")
            em.op("scalar", "activation", rstd, lnv, AF.Exp, scale=-0.5)
            s_tok = em.mark("scalar")
            tdst = tm_out[:, i, :] if tm_out is not None else tmb[:, :]
            em.wait("vector", s_tok)
            if rms:
                em.op("vector", "tensor_scalar", tdst, xtile, rstd, None, ALU.mult)
            else:
                em.op("vector", "tensor_scalar", tdst, xtile, mu, rstd, ALU.subtract, ALU.mult)
            n_tok = em.mark("vector")
            busy[buf] = n_tok
            if out_dram is not None:
                em.op("vector", "tensor_scalar", sq[:, :], xtile, mu, rstd, ALU.subtract, ALU.mult)
                f_tok = em.mark("vector")
                em.wait("sync", f_tok)
                sq_war = em.dma("sync", out_dram[t0 + i * 128:t0 + (i + 1) * 128, :], sq[:, :], stream="lnout")
                busy[buf] = f_tok
            for j in range((8 if xlnT is not None else 0) if not DBG.get("ln_notr") else 0):
                slot = j % 2
                if ps_busy[slot] is not None:
                    em.wait("tensor", ps_busy[slot])
                em.wait("tensor", n_tok)
                em.op("tensor", "transpose", pst[:, slot * 1024:slot * 1024 + 128],
                      tdst[:, j * 128:(j + 1) * 128], P.ident_bf)
                ttok = em.mark("tensor")
                em.wait("vector", ttok)
                em.op("vector", "tensor_copy",
                      xlnT[:, j, xln_off + i * 128:xln_off + (i + 1) * 128],
                      pst[:, slot * 1024:slot * 1024 + 128])
                ps_busy[slot] = em.mark("vector")
        em.barrier()




def ph_l0_attn_a(P, T, xin, xout, W, G=8):
    em, nc = P.em, P.nc
    if DBG.get("l0aa"):
        og_d = nc.dram_tensor("dbg_og", [T, 2048], BF16, kind="ExternalOutput").ap()
        p_d = nc.dram_tensor("dbg_p", [T, 1024], BF16, kind="ExternalOutput").ap()
    nC = T // 128
    G = min(G, nC)
    ngr = nC // G
    TG = G * 128
    SCALE = 1.0 / math.sqrt(2048.0)
    qT_d = nc.dram_tensor(P.name("l0aa_q"), [16, 128, T], BF16).ap()
    kT_d = nc.dram_tensor(P.name("l0aa_k"), [16, 128, T], BF16).ap()
    vT_d = nc.dram_tensor(P.name("l0aa_v"), [16, 128, T], BF16).ap()
    g_d = nc.dram_tensor(P.name("l0aa_g"), [T, 2048], BF16).ap()

    with contextlib.ExitStack() as stA:
        xlnT = P.sbuf([128, 8, T], BF16, stA)
        emit_ln(P, xin, 0, T, xlnT, 0)
        bq, _ = emit_load_cols(P, W["bq"], 2048, stack=stA)
        bk, _ = emit_load_cols(P, W["bk"], 2048, stack=stA)
        bv, tokb = emit_load_cols(P, W["bv"], 2048, stack=stA)
        em.barrier()
        ps = P.psum([128, 2048], F32, stA)
        wcol = P.sbuf([128, 2, 8, 128], BF16, stA)
        otile = P.sbuf([128, 2, T], BF16, stA)
        wbusy = [None, None]
        obusy = [None, None]
        psbusy = [None, None, None, None]
        for wi, (wname, bcol, dst) in enumerate(
                [("Wq", bq, qT_d), ("Wk", bk, kT_d), ("Wv", bv, vT_d)]):
            for ft in range(16):
                k = wi * 16 + ft
                sl = k % 2
                if wbusy[sl] is not None:
                    em.wait("gpsimd", wbusy[sl])
                for d in range(8):
                    wtok = em.dma("gpsimd", wcol[:, sl, d, :],
                                  W[wname][d * 128:(d + 1) * 128, ft * 128:(ft + 1) * 128],
                                  stream=f"wc{sl}")
                wtok = (f"D_wc{sl}", em.streams[f"D_wc{sl}"])
                em.wait("tensor", wtok)
                em.wait("vector", obusy[sl])
                mtok = None
                for tcp in range(max(1, T // 1024)):
                    ncc = min(2, T // 512)
                    for cc in range(ncc):
                        em.wait("tensor", psbusy[(tcp % 2) * 2 + cc])
                    for d in range(8):
                        for cc in range(ncc):
                            tc = tcp * 2 + cc
                            em.op("tensor", "matmul",
                                  ps[:, ((tcp % 2) * 2 + cc) * 512:((tcp % 2) * 2 + cc + 1) * 512],
                                  wcol[:, sl, d, :], xlnT[:, d, tc * 512:(tc + 1) * 512],
                                  start=(d == 0), stop=(d == 7))
                    mtok = em.mark("tensor")
                    em.wait("vector", mtok)
                    for cc in range(ncc):
                        tc = tcp * 2 + cc
                        em.op("vector", "tensor_scalar", otile[:, sl, tc * 512:(tc + 1) * 512],
                              ps[:, ((tcp % 2) * 2 + cc) * 512:((tcp % 2) * 2 + cc + 1) * 512],
                              bcol[:, ft:ft + 1], None, ALU.add)
                        psbusy[(tcp % 2) * 2 + cc] = em.mark("vector")
                wbusy[sl] = mtok
                for pb2 in range(4):
                    em.wait("sync", psbusy[pb2])
                obusy[sl] = em.dma("sync", dst[ft, :, :], otile[:, sl, :], stream=f"oo{sl}")
        wg_sb = P.sbuf([128, 8, 2048], BF16, stA)
        for d in range(8):
            gtok = em.dma("gpsimd", wg_sb[:, d, :], W["Wg"][d * 128:(d + 1) * 128, :], stream="wg")
        bg_row = P.sbuf([1, 2048], BF16, stA)
        gtok2 = em.dma("gpsimd", bg_row[0:1, :], W["bg"].rearrange("(a b) -> a b", a=1), stream="wg")
        em.wait("tensor", ("D_wg", em.streams["D_wg"]))
        psg = P.psum([128, 2048], F32, stA)
        gt = P.sbuf([128, 2, 2048], BF16, stA)
        gbusy = [None, None]
        psg_busy = None
        ones_bf = P.csb["ones_bf16"]
        for tt in range(nC):
            sl = tt % 2
            em.wait("tensor", psg_busy)
            for d in range(8):
                for fc in range(4):
                    em.op("tensor", "matmul", psg[:, fc * 512:(fc + 1) * 512],
                          xlnT[:, d, tt * 128:(tt + 1) * 128], wg_sb[:, d, fc * 512:(fc + 1) * 512],
                          start=(d == 0), stop=False)
            for fc in range(4):
                em.op("tensor", "matmul", psg[:, fc * 512:(fc + 1) * 512],
                      ones_bf[0:1, :], bg_row[0:1, fc * 512:(fc + 1) * 512],
                      start=False, stop=True)
            mtok = em.mark("tensor")
            em.wait("scalar", mtok)
            em.wait("scalar", gbusy[sl])
            em.op("scalar", "activation", gt[:, sl, :], psg[:, :], AF.Sigmoid)
            psg_busy = em.mark("scalar")
            em.wait("sync", psg_busy)
            gbusy[sl] = em.dma("sync", g_d[tt * 128:(tt + 1) * 128, :], gt[:, sl, :], stream=f"go{sl}")
        em.barrier()

    if DBG.get("stopA"):
        with contextlib.ExitStack() as stZ:
            cp = P.sbuf([128, 1024], F32, stZ)
            for tt in range(T // 128):
                ct = em.dma("sync", cp[:, :], xin[tt * 128:(tt + 1) * 128, :], stream="cpx")
                em.wait("sync", ct)
                em.dma("sync", xout[tt * 128:(tt + 1) * 128, :], cp[:, :], stream="cpx")
            em.barrier()
        return
    with contextlib.ExitStack() as stB:
        S = P.sbuf([128, 16, 2048], BF16, stB)
        kT_g = P.sbuf([128, 16, TG], BF16, stB)
        v_tm = P.sbuf([128, G, 2048], BF16, stB)
        wo_sb = P.sbuf([128, 16, 1024], BF16, stB)
        bo_row = P.sbuf([1, 1024], BF16, stB)
        for dt in range(16):
            wotok = em.dma("gpsimd", wo_sb[:, dt, :], W["Wo"][dt * 128:(dt + 1) * 128, :], stream="wo")
        botok = em.dma("gpsimd", bo_row[0:1, :], W["bo"].rearrange("(a b) -> a b", a=1), stream="wo")
        em.wait("tensor", ("D_wo", em.streams["D_wo"]))
        ps_pt = P.psum([128, 2048], BF16, stB)
        vtmp = P.sbuf([128, 16, 128], BF16, stB)
        for g in range(ngr):
            g0 = g * TG
            for dt in range(16):
                ktok = em.dma("sync", kT_g[:, dt, :], kT_d[dt, :, g0:g0 + TG], stream="kg")
            ktok = ("D_kg", em.streams["D_kg"])
            pt_busy = [None, None]
            vt_tok = None
            for c in range(G):
                em.wait("sync", vt_tok)
                for dt in range(16):
                    vtok = em.dma("sync", vtmp[:, dt, :],
                                  vT_d[dt, :, g0 + c * 128:g0 + (c + 1) * 128], stream="vt")
                vtok = ("D_vt", em.streams["D_vt"])
                em.wait("tensor", vtok)
                for dt in range(16):
                    sl = dt % 2
                    em.wait("tensor", pt_busy[sl])
                    em.op("tensor", "transpose", ps_pt[:, sl * 1024:sl * 1024 + 128],
                          vtmp[:, dt, :], P.ident_bf)
                    ttok = em.mark("tensor")
                    em.wait("vector", ttok)
                    em.op("vector", "tensor_copy", v_tm[:, c, dt * 128:(dt + 1) * 128],
                          ps_pt[:, sl * 1024:sl * 1024 + 128])
                    pt_busy[sl] = em.mark("vector")
                    vt_tok = pt_busy[sl]
            em.wait("tensor", ktok)
            with contextlib.ExitStack() as stC:
                Pbuf = P.sbuf([128, TG], BF16, stC)
                PTbuf = P.sbuf([128, TG], BF16, stC)
                qti = P.sbuf([128, 2, 16, 128], BF16, stC)
                gti = P.sbuf([128, 2, 2048], BF16, stC)
                og = P.sbuf([128, 2048], BF16, stC)
                ogT = P.sbuf([128, 16, 128], BF16, stC)
                xres2 = P.sbuf([128, 2, 1024], F32, stC)
                stats = P.sbuf([128, 4], F32, stC)
                ps_sc = P.psum([128, 1024], F32, stC)
                ps_o = P.psum([128, 2048], F32, stC)
                q_busy = [None, None]
                g_busy = [None, None]
                x_busy = [None, None]
                y_busy = [None, None]
                sc_free = None
                o_free = None
                qtoks = [None, None]
                gtoks = [None, None]
                xtoks = [None, None]

                def load_chunk(i2):
                    sl2 = i2 % 2
                    ci2 = g * G + i2
                    em.wait("sync", q_busy[sl2])
                    for dt in range(16):
                        em.dma("sync", qti[:, sl2, dt, :],
                               qT_d[dt, :, ci2 * 128:(ci2 + 1) * 128], stream=f"qt{sl2}")
                    qtoks[sl2] = ("D_qt" + str(sl2), em.streams["D_qt" + str(sl2)])
                    em.wait("sync", g_busy[sl2])
                    gtoks[sl2] = em.dma("sync", gti[:, sl2, :],
                                        g_d[ci2 * 128:(ci2 + 1) * 128, :], stream=f"gi{sl2}")
                    em.wait("sync", x_busy[sl2])
                    xtoks[sl2] = em.dma("sync", xres2[:, sl2, :],
                                        xin[ci2 * 128:(ci2 + 1) * 128, :], stream=f"xr{sl2}")

                load_chunk(0)
                for i in range(G):
                    ci = g * G + i
                    sl = i % 2
                    if i + 1 < G:
                        load_chunk(i + 1)
                    qtok = qtoks[sl]
                    gtok = gtoks[sl]
                    xtok = xtoks[sl]
                    ncols = (i + 1) * 128
                    em.wait("tensor", qtok)
                    em.wait("tensor", sc_free)
                    for cc in range(ceil_div(ncols, 512)):
                        c0, c1 = cc * 512, min(ncols, (cc + 1) * 512)
                        for d in range(16):
                            em.op("tensor", "matmul", ps_sc[:, c0:c1],
                                  qti[:, sl, d, :], kT_g[:, d, c0:c1],
                                  start=(d == 0), stop=(d == 15))
                    sc_tok = em.mark("tensor")
                    dm = stats[:, 0:1]
                    dmn = stats[:, 1:2]
                    den = stats[:, 2:3]
                    rden = stats[:, 3:4]
                    em.wait("vector", sc_tok)
                    em.op("vector", "reduce_max", dm, ps_sc[:, i * 128:ncols], AX)
                    em.op("vector", "tensor_scalar", dmn, dm, -SCALE, None, ALU.mult)
                    v_tok = em.mark("vector")
                    em.wait("scalar", v_tok)
                    em.op("scalar", "activation", Pbuf[:, i * 128:ncols],
                          ps_sc[:, i * 128:ncols], AF.Exp, bias=dmn, scale=SCALE)
                    e_tok = em.mark("scalar")
                    em.wait("vector", e_tok)
                    em.op("vector", "tensor_tensor", Pbuf[:, i * 128:ncols],
                          Pbuf[:, i * 128:ncols], P.csb["tri01_f32"][:, :], ALU.mult)
                    em.op("vector", "reduce_sum", den, Pbuf[:, i * 128:ncols], AX)
                    em.op("vector", "reciprocal", rden, den)
                    rtok = em.mark("vector")
                    em.wait("vector", rtok)
                    em.op("vector", "tensor_scalar", Pbuf[:, i * 128:ncols],
                          Pbuf[:, i * 128:ncols], rden, None, ALU.mult)
                    if i > 0:
                        em.op("vector", "tensor_copy", Pbuf[:, 0:i * 128],
                              ps_sc[:, 0:i * 128])
                    p_tok = em.mark("vector")
                    sc_free = p_tok
                    pt_busy2 = [None, None]
                    for j in range(i + 1):
                        slp = j % 2
                        em.wait("tensor", p_tok)
                        em.wait("tensor", pt_busy2[slp])
                        em.op("tensor", "transpose", ps_pt[:, slp * 1024:slp * 1024 + 128],
                              Pbuf[:, j * 128:(j + 1) * 128], P.ident_bf)
                        ttok = em.mark("tensor")
                        em.wait("vector", ttok)
                        em.op("vector", "tensor_copy", PTbuf[:, j * 128:(j + 1) * 128],
                              ps_pt[:, slp * 1024:slp * 1024 + 128])
                        pt_busy2[slp] = em.mark("vector")
                    em.wait("tensor", pt_busy2[0])
                    em.wait("tensor", pt_busy2[1])
                    em.wait("tensor", o_free)
                    nj = i + 1
                    for j in range(nj):
                        last = (j == nj - 1) and (g == 0)
                        for fc in range(4):
                            em.op("tensor", "matmul", ps_o[:, fc * 512:(fc + 1) * 512],
                                  PTbuf[:, j * 128:(j + 1) * 128], v_tm[:, j, fc * 512:(fc + 1) * 512],
                                  start=(j == 0), stop=last)
                    if g > 0:
                        for dt in range(16):
                            for fc in range(4):
                                em.op("tensor", "matmul", ps_o[:, fc * 512:(fc + 1) * 512],
                                      qti[:, sl, dt, :], S[:, dt, fc * 512:(fc + 1) * 512],
                                      start=False, stop=(dt == 15))
                    o_tok = em.mark("tensor")
                    em.wait("vector", o_tok)
                    em.wait("vector", gtok)
                    em.op("vector", "tensor_tensor", og[:, :], gti[:, sl, :], ps_o[:, :],
                          ALU.mult)
                    og_tok = em.mark("vector")
                    o_free = og_tok
                    if DBG.get("l0aa"):
                        em.wait("sync", og_tok)
                        dto = em.dma("sync", og_d[ci * 128:(ci + 1) * 128, :], og[:, :], stream="dbgo")
                        em.wait("sync", p_tok)
                        dtp = em.dma("sync", p_d[ci * 128:(ci + 1) * 128, 0:(i + 1) * 128],
                               Pbuf[:, 0:(i + 1) * 128], stream="dbgp")
                        for e2 in ("vector", "scalar"):
                            em.wait(e2, dto)
                            em.wait(e2, dtp)
                    pt_busy3 = [None, None]
                    for dt in range(16):
                        slp = dt % 2
                        em.wait("tensor", og_tok)
                        em.wait("tensor", pt_busy3[slp])
                        em.op("tensor", "transpose", ps_pt[:, slp * 1024:slp * 1024 + 128],
                              og[:, dt * 128:(dt + 1) * 128], P.ident_bf)
                        ttok = em.mark("tensor")
                        em.wait("vector", ttok)
                        em.op("vector", "tensor_copy", ogT[:, dt, :],
                              ps_pt[:, slp * 1024:slp * 1024 + 128])
                        pt_busy3[slp] = em.mark("vector")
                    em.wait("tensor", pt_busy3[0])
                    em.wait("tensor", pt_busy3[1])
                    em.wait("tensor", y_busy[sl])
                    em.wait("tensor", sc_free)
                    for dt in range(16):
                        for fc in range(2):
                            em.op("tensor", "matmul", ps_sc[:, fc * 512:(fc + 1) * 512],
                                  ogT[:, dt, :], wo_sb[:, dt, fc * 512:(fc + 1) * 512],
                                  start=(dt == 0), stop=False)
                    for fc in range(2):
                        em.op("tensor", "matmul", ps_sc[:, fc * 512:(fc + 1) * 512],
                              P.csb["ones_bf16"][0:1, :], bo_row[0:1, fc * 512:(fc + 1) * 512],
                              start=False, stop=True)
                    y_tok = em.mark("tensor")
                    em.wait("vector", y_tok)
                    em.wait("vector", xtok)
                    em.op("vector", "tensor_tensor", xres2[:, sl, :], xres2[:, sl, :],
                          ps_sc[:, :], ALU.add)
                    yv_tok = em.mark("vector")
                    sc_free = yv_tok
                    em.wait("sync", yv_tok)
                    y_busy[sl] = em.dma("sync", xout[ci * 128:(ci + 1) * 128, :],
                                        xres2[:, sl, :], stream=f"yo{sl}")
                    q_busy[sl] = o_tok
                    g_busy[sl] = og_tok
                    x_busy[sl] = y_busy[sl]
                em.barrier()
            if g < ngr - 1:
                with contextlib.ExitStack() as stD:
                    k_tm = P.sbuf([128, G, 2048], BF16, stD)
                    ps_st = P.psum([128, 1024], F32, stD)
                    ptb = [None, None]
                    for c in range(G):
                        for dt in range(16):
                            slp = dt % 2
                            em.wait("tensor", ptb[slp])
                            em.op("tensor", "transpose", ps_pt[:, slp * 1024:slp * 1024 + 128],
                                  kT_g[:, dt, c * 128:(c + 1) * 128], P.ident_bf)
                            ttok = em.mark("tensor")
                            em.wait("vector", ttok)
                            em.op("vector", "tensor_copy", k_tm[:, c, dt * 128:(dt + 1) * 128],
                                  ps_pt[:, slp * 1024:slp * 1024 + 128])
                            ptb[slp] = em.mark("vector")
                    em.wait("tensor", ptb[0])
                    em.wait("tensor", ptb[1])
                    st_free = [None, None]
                    for dt in range(16):
                        for ec in range(4):
                            k2 = dt * 4 + ec
                            slp = k2 % 2
                            em.wait("tensor", st_free[slp])
                            for c in range(G):
                                em.op("tensor", "matmul", ps_st[:, slp * 512:(slp + 1) * 512],
                                      k_tm[:, c, dt * 128:(dt + 1) * 128],
                                      v_tm[:, c, ec * 512:(ec + 1) * 512],
                                      start=(c == 0), stop=(c == G - 1))
                            mtok = em.mark("tensor")
                            em.wait("vector", mtok)
                            if g == 0:
                                em.op("vector", "tensor_copy", S[:, dt, ec * 512:(ec + 1) * 512],
                                      ps_st[:, slp * 512:(slp + 1) * 512])
                            else:
                                em.op("vector", "tensor_tensor", S[:, dt, ec * 512:(ec + 1) * 512],
                                      S[:, dt, ec * 512:(ec + 1) * 512],
                                      ps_st[:, slp * 512:(slp + 1) * 512], ALU.add)
                            st_free[slp] = em.mark("vector")
                    em.barrier()
        em.barrier()


PHASE_W = {
    "l0aa": ("L0_attn_a_", ["Wq", "bq", "Wk", "bk", "Wv", "bv", "Wg", "bg", "Wo", "bo"]),
    "l0ffn": ("L0_ffn_", ["gate_w", "base_wgu", "base_wd", "exp_wgu", "exp_wd"]),
    "l0ab": ("L0_attn_b_", ["W_abq", "W_abkv", "W_o"]),
    "l1aa": ("L1_attn_a_", ["W_in", "w_conv", "b_conv", "theta", "decay_slopes",
                             "anchor_slopes", "score_scale", "deriv_logits",
                             "norm_scale", "W_re", "W_im", "W_out"]),
    "l1ffn": ("L1_ffn_", ["gate_w", "w1", "w2", "w3", "ws1", "ws2", "ws3"]),
    "l1ab": ("L1_attn_b_", ["norm1_w", "Wr", "Wk", "Wv", "Wo", "norm2_w",
                             "w1", "w2", "w3"]),
}

INPUT_SHAPES = {
    "L0_attn_a_Wq": [1024, 2048], "L0_attn_a_bq": [2048],
    "L0_attn_a_Wk": [1024, 2048], "L0_attn_a_bk": [2048],
    "L0_attn_a_Wv": [1024, 2048], "L0_attn_a_bv": [2048],
    "L0_attn_a_Wg": [1024, 2048], "L0_attn_a_bg": [2048],
    "L0_attn_a_Wo": [2048, 1024], "L0_attn_a_bo": [1024],
    "L0_attn_b_W_abq": [480, 1024], "L0_attn_b_W_abkv": [320, 1024],
    "L0_attn_b_W_o": [1024, 1024],
    "L0_ffn_gate_w": [1024, 8], "L0_ffn_base_wgu": [1024, 5632],
    "L0_ffn_base_wd": [2816, 1024], "L0_ffn_exp_wgu": [8, 1024, 5632],
    "L0_ffn_exp_wd": [8, 2816, 1024],
    "L1_attn_a_W_in": [1024, 2080], "L1_attn_a_w_conv": [4, 1, 2080],
    "L1_attn_a_b_conv": [2080], "L1_attn_a_theta": [1, 1, 32, 32, 4],
    "L1_attn_a_decay_slopes": [28], "L1_attn_a_anchor_slopes": [4],
    "L1_attn_a_score_scale": [32], "L1_attn_a_deriv_logits": [3],
    "L1_attn_a_norm_scale": [256], "L1_attn_a_W_re": [128, 32],
    "L1_attn_a_W_im": [128, 32], "L1_attn_a_W_out": [1024, 1024],
    "L1_attn_b_norm1_w": [1024], "L1_attn_b_Wr": [1024, 1024],
    "L1_attn_b_Wk": [1024, 1024], "L1_attn_b_Wv": [1024, 1024],
    "L1_attn_b_Wo": [1024, 1024], "L1_attn_b_norm2_w": [1024],
    "L1_attn_b_w1": [1024, 4096], "L1_attn_b_w2": [1024, 4096],
    "L1_attn_b_w3": [4096, 1024],
    "L1_ffn_gate_w": [16, 1024], "L1_ffn_w1": [16, 1024, 512],
    "L1_ffn_w2": [16, 512, 1024], "L1_ffn_w3": [16, 1024, 512],
    "L1_ffn_ws1": [1024, 1024], "L1_ffn_ws2": [1024, 1024],
    "L1_ffn_ws3": [1024, 1024],
}

PHASE_FNS = {}


def build_program(T, phase_seq, g_l0aa=8):
    consts = build_consts(T)
    nc = bass.Bass("TRN2", target_bir_lowering=False, debug=False)
    x0 = nc.dram_tensor("x0", [T, 1024], F32, kind="ExternalInput").ap()
    y_out = nc.dram_tensor("y_out", [T, 1024], F32, kind="ExternalOutput").ap()
    need = set()
    for p in set(phase_seq):
        pref, names = PHASE_W[p]
        for n in names:
            need.add(pref + n)
    dr = {}
    for n in sorted(need):
        dr[n] = nc.dram_tensor(n, INPUT_SHAPES[n], F32, kind="ExternalInput").ap()
    cdr = {}
    for n, arr in consts.items():
        cdr[n] = nc.dram_tensor("c_" + n, list(arr.shape), F32, kind="ExternalInput").ap()
    xA = nc.dram_tensor("xA", [T, 1024], F32).ap()
    xB = nc.dram_tensor("xB", [T, 1024], F32).ap()
    with contextlib.ExitStack() as gstack:
        em = EM(nc)
        P = Prog(nc, em, gstack, {}, None, None)
        P.cdr = cdr
        ident_f32 = P.sbuf([128, 128], F32)
        ident_bf = P.sbuf([128, 128], BF16)
        ones_bf = P.sbuf([128, 128], BF16)
        ones_f = P.sbuf([128, 128], F32)
        tri01 = P.sbuf([128, 128], F32)
        t1 = em.dma("sync", ident_f32[:, :], cdr["ident_f32"][:, :], stream="init")
        t2 = em.dma("gpsimd", ident_bf[:, :], cdr["ident_f32"][:, :], stream="init2")
        t3 = em.dma("gpsimd", ones_bf[:, :], cdr["ones_f32"][:, :], stream="init2")
        t4 = em.dma("sync", tri01[:, :], cdr["tri01_f32"][:, :], stream="init")
        t5 = em.dma("sync", ones_f[:, :], cdr["ones_f32"][:, :], stream="init")
        for e in ("tensor", "vector", "scalar"):
            em.wait(e, ("D_init", em.streams["D_init"]))
            em.wait(e, ("D_init2", em.streams["D_init2"]))
        P.csb = {"ones_bf16": ones_bf, "tri01_f32": tri01, "ones_f32": ones_f}
        P.ident_bf = ident_bf[:, :]
        P.ident_f32 = ident_f32[:, :]
        cur = x0
        bufs = [xA, xB]
        for i, pk in enumerate(phase_seq):
            nxt = y_out if i == len(phase_seq) - 1 else bufs[i % 2]
            pref, names = PHASE_W[pk]
            Wm = {n: dr[pref + n] for n in names}
            kw = {}
            if pk == "l0aa":
                kw["G"] = g_l0aa
            PHASE_FNS[pk](P, T, cur, nxt, Wm, **kw)
            cur = nxt
        em.barrier()
        em.nop("gpsimd")
        sems = {}
        for s in set(COMPUTE_SEMS.values()) | set(em.streams.keys()):
            sems[s] = gstack.enter_context(nc.semaphore("sm" + s))
        with nc.Block() as block:
            em.flush(block, sems)
    return nc, consts, sorted(need)


PHASE_FNS["l0aa"] = ph_l0_attn_a

FULL_SEQ = ["l0aa", "l0ffn", "l0ab", "l0ffn", "l1aa", "l1ffn", "l1ab", "l1ffn"]


def kernel(**inputs):
    T = 4096
    nc, consts, need = build_program(T, FULL_SEQ)
    x_full = np.ascontiguousarray(inputs["L0_attn_a_hidden_states"], dtype=np.float32)
    in_maps = []
    for c in range(NCORES):
        m = {"x0": x_full[c]}
        for n in need:
            m[n] = np.ascontiguousarray(inputs[n], dtype=np.float32)
        for k, v in consts.items():
            m["c_" + k] = np.ascontiguousarray(v, dtype=np.float32)
        in_maps.append(m)
    res = run_bass_kernel_spmd(nc, in_maps, core_ids=list(range(NCORES)))
    out = np.stack([res.results[c]["y_out"] for c in range(NCORES)], axis=0)
    return out.astype(np.float32)


def ph_l0_ffn(P, T, xin, xout, W):
    em, nc = P.em, P.nc
    NE = 8
    INTER = 2816
    NJ = INTER // 128
    SW = min(512, T)
    NS = T // SW
    NTT = SW // 128
    xlnT_d = nc.dram_tensor(P.name("ffn_xln"), [8, 128, T], BF16).ap()

    with contextlib.ExitStack() as stA:
        cw_sb = P.sbuf([128, T // 128, NE], F32, stA)
        with contextlib.ExitStack() as stL:
            gw_sb = P.sbuf([128, 8, 8], BF16, stL)
            for d in range(8):
                em.dma("gpsimd", gw_sb[:, d, :], W["gate_w"][d * 128:(d + 1) * 128, :],
                       stream="gw")
            stats = P.sbuf([128, 8], F32, stL)
            prob = P.sbuf([128, 2, NE], F32, stL)
            ps_l = P.psum([128, 1024], F32, stL)
            psl_busy = [None, None]
            em.wait("tensor", ("D_gw", em.streams["D_gw"]))
            xlnT = P.sbuf([128, 8, 512], BF16, stL)
            nblk = T // 512
            for b in range(nblk):
                emit_ln(P, xin, b * 512, (b + 1) * 512, xlnT, 0)
                for d in range(8):
                    em.dma("sync", xlnT_d[d, :, b * 512:(b + 1) * 512], xlnT[:, d, :],
                           stream="xlw")
                for it in range(4):
                    tt = b * 4 + it
                    psl = ps_l[:, (it % 2) * 512:(it % 2) * 512 + NE]
                    em.wait("tensor", psl_busy[it % 2])
                    for d in range(8):
                        em.op("tensor", "matmul", psl, xlnT[:, d, it * 128:(it + 1) * 128],
                              gw_sb[:, d, :], start=(d == 0), stop=(d == 7))
                    mtok = em.mark("tensor")
                    pr = prob[:, it % 2, :]
                    m1 = stats[:, 0:1]
                    den = stats[:, 1:2]
                    rden = stats[:, 2:3]
                    m1p = stats[:, 3:4]
                    m2p = stats[:, 4:5]
                    tmp8 = prob[:, (it + 1) % 2, :]
                    em.wait("vector", mtok)
                    em.op("vector", "reduce_max", m1, psl, AX)
                    vt = em.mark("vector")
                    em.op("vector", "tensor_scalar", m1, m1, -1.0, None, ALU.mult)
                    em.wait("scalar", em.mark("vector"))
                    em.op("scalar", "activation", pr, psl, AF.Exp, bias=m1,
                          accum_out=den)
                    stok = em.mark("scalar")
                    psl_busy[it % 2] = stok
                    em.wait("vector", stok)
                    em.op("vector", "reciprocal", rden, den)
                    em.op("vector", "tensor_scalar", pr, pr, rden, None, ALU.mult)
                    em.op("vector", "reduce_max", m1p, pr, AX)
                    em.op("vector", "tensor_scalar", tmp8, pr, m1p, None, ALU.is_ge)
                    em.op("vector", "scalar_tensor_tensor", tmp8, tmp8, -1e9, pr,
                          ALU.mult, ALU.add)
                    em.op("vector", "reduce_max", m2p, tmp8, AX)
                    em.op("vector", "tensor_tensor", m1p, m1p, m2p, ALU.add)
                    em.op("vector", "reciprocal", m1p, m1p)
                    em.op("vector", "tensor_scalar", tmp8, pr, m2p, None, ALU.is_ge)
                    em.op("vector", "tensor_tensor", tmp8, tmp8, pr, ALU.mult)
                    em.op("vector", "tensor_scalar", cw_sb[:, tt, :], tmp8, m1p, None,
                          ALU.mult)
                    em.mark("vector")
            em.barrier()
        a_sb = P.sbuf([128, NJ, SW], BF16, stA)
        sg = P.sbuf([128, 512], BF16, stA)
        xlt = P.sbuf([128, 2, 8, SW], BF16, stA)
        xti = P.sbuf([128, 2, 1024], F32, stA)
        ystg = P.sbuf([128, 2, 1024], F32, stA)
        guA = P.sbuf([128, 8, 2 * INTER], BF16, stA)
        wdA = P.sbuf([128, NJ, 1024], BF16, stA)
        ps_g = P.psum([128, 1024], F32, stA)
        ps_u = P.psum([128, 1024], F32, stA)
        ps_y = P.psum([128, 2048], F32, stA)

        def load_gu(e):
            src = W["base_wgu"] if e == NE else W["exp_wgu"][e]
            for d in range(8):
                tok = em.dma("gpsimd", guA[:, d, :], src[d * 128:(d + 1) * 128, :],
                             stream="gu")
            return ("D_gu", em.streams["D_gu"])

        def load_wd(e):
            src = W["base_wd"] if e == NE else W["exp_wd"][e]
            for j in range(NJ):
                tok = em.dma("gpsimd", wdA[:, j, :], src[j * 128:(j + 1) * 128, :],
                             stream="wd")
            return ("D_wd", em.streams["D_wd"])

        order = [NE] + list(range(NE))
        gu_tok = load_gu(order[0])
        wd_tok = load_wd(order[0])
        xl_toks = [None, None]
        xl_guard = [None, None]

        def load_xl(s2):
            sl2 = s2 % 2
            em.wait("sync", xl_guard[sl2])
            for d in range(8):
                em.dma("sync", xlt[:, sl2, d, :],
                       xlnT_d[d, :, s2 * SW:(s2 + 1) * SW], stream=f"xl{sl2}")
            xl_toks[sl2] = (f"D_xl{sl2}", em.streams[f"D_xl{sl2}"])

        load_xl(0)
        xt_busy = [None, None]
        y_busy = [None, None]
        psg_busy = [None, None]
        psu_busy = [None, None]
        psy_busy = [None, None]
        a_busy = None
        sgw = None
        for ei, e in enumerate(order):
            em.wait("tensor", gu_tok)
            for s in range(NS):
                sl = s % 2
                em.wait("tensor", xl_toks[sl])
                em.wait("tensor", a_busy)
                for j in range(NJ):
                    pg = ps_g[:, (j % 2) * 512:(j % 2) * 512 + 512]
                    pu = ps_u[:, (j % 2) * 512:(j % 2) * 512 + 512]
                    em.wait("tensor", psg_busy[j % 2])
                    em.wait("tensor", psu_busy[j % 2])
                    for d in range(8):
                        em.op("tensor", "matmul", pg, guA[:, d, j * 128:(j + 1) * 128],
                              xlt[:, sl, d, :], start=(d == 0), stop=(d == 7))
                    for d in range(8):
                        em.op("tensor", "matmul", pu,
                              guA[:, d, INTER + j * 128:INTER + (j + 1) * 128],
                              xlt[:, sl, d, :], start=(d == 0), stop=(d == 7))
                    mtok = em.mark("tensor")
                    em.wait("scalar", mtok)
                    em.wait("scalar", sgw)
                    em.op("scalar", "activation", sg[:, :], pg, AF.Silu)
                    psg_busy[j % 2] = em.mark("scalar")
                    em.wait("vector", psg_busy[j % 2])
                    em.op("vector", "tensor_tensor", a_sb[:, j, :], sg[:, :], pu,
                          ALU.mult)
                    psu_busy[j % 2] = em.mark("vector")
                    sgw = psu_busy[j % 2]
                a_ready = sgw
                xl_guard[sl] = mtok
                if s + 1 < NS:
                    load_xl(s + 1)
                elif NS > 1:
                    load_xl(0)
                if s == 0:
                    em.wait("tensor", wd_tok)
                if s == NS - 1 and ei + 1 < len(order):
                    em.wait("gpsimd", mtok)
                    gu_tok = load_gu(order[ei + 1])
                em.wait("tensor", a_ready)
                for tt in range(NTT):
                    gtt = s * NTT + tt
                    psl2 = (tt % 2)
                    py = ps_y[:, psl2 * 1024:(psl2 + 1) * 1024]
                    em.wait("tensor", psy_busy[psl2])
                    for j in range(NJ):
                        for fc in range(2):
                            em.op("tensor", "matmul",
                                  py[:, fc * 512:(fc + 1) * 512],
                                  a_sb[:, j, tt * 128:(tt + 1) * 128],
                                  wdA[:, j, fc * 512:(fc + 1) * 512],
                                  start=(j == 0), stop=(j == NJ - 1))
                    ytok = em.mark("tensor")
                    ysl = gtt % 2
                    em.wait("vector", ytok)
                    em.wait("vector", y_busy[ysl])
                    if e == NE:
                        em.wait("sync", xt_busy[ysl])
                        xttok = em.dma("sync", xti[:, ysl, :],
                                       xin[gtt * 128:(gtt + 1) * 128, :],
                                       stream=f"xt{ysl}")
                        em.wait("vector", xttok)
                        em.op("vector", "tensor_tensor", ystg[:, ysl, :],
                              xti[:, ysl, :], py, ALU.add)
                        xt_busy[ysl] = em.mark("vector")
                        wtok = em.mark("vector")
                    else:
                        em.op("vector", "tensor_scalar", ystg[:, ysl, :], py,
                              cw_sb[:, gtt, e:e + 1], None, ALU.mult)
                        wtok = em.mark("vector")
                    psy_busy[psl2] = wtok
                    q = "gpsimd"
                    em.wait(q, wtok)
                    ystream = "ya%d_%d" % (s, ysl)
                    if tt < 2:
                        for st2 in ("ya%d_0" % s, "ya%d_1" % s):
                            if "D_" + st2 in em.streams:
                                em.wait(q, ("D_" + st2, em.streams["D_" + st2]))
                    if e == NE:
                        y_busy[ysl] = em.dma("gpsimd", xout[gtt * 128:(gtt + 1) * 128, :],
                                             ystg[:, ysl, :], stream=ystream)
                    else:
                        y_busy[ysl] = em.dma("gpsimd", xout[gtt * 128:(gtt + 1) * 128, :],
                                             ystg[:, ysl, :], stream=ystream,
                                             accum_op=ALU.add)
                a_busy = ytok
            if ei + 1 < len(order):
                em.wait("gpsimd", a_busy)
                wd_tok = load_wd(order[ei + 1])
        em.barrier()


PHASE_FNS["l0ffn"] = ph_l0_ffn


def ph_l0_attn_b(P, T, xin, xout, W):
    em, nc = P.em, P.nc
    NTT = T // 128
    NH, DH, CH = 16, 64, 80
    RQ, RK, RV = 6, 2, 2

    with contextlib.ExitStack() as stA:
        outT_all = P.sbuf([128, 8, T], BF16, stA)
        abq_tm = P.sbuf([128, NTT, 480], BF16, stA)
        abkv_tm = P.sbuf([128, NTT, 320], BF16, stA)
        rcos = P.sbuf([128, NTT, 32], BF16, stA)
        rsin = P.sbuf([128, NTT, 32], BF16, stA)
        for tt in range(NTT):
            em.dma("gpsimd", rcos[:, tt, :], P.cdr["rope_cos"][tt * 128:(tt + 1) * 128, :],
                   stream="rope")
            em.dma("gpsimd", rsin[:, tt, :], P.cdr["rope_sin"][tt * 128:(tt + 1) * 128, :],
                   stream="rope")
        with contextlib.ExitStack() as stB:
            HW2 = min(2048, T)
            xlnT = P.sbuf([128, 8, HW2], BF16, stB)
            wabqT, _ = emit_weight_loadT(P, W["W_abq"], 480, 1024, stack=stB)
            wabkvT, tok2 = emit_weight_loadT(P, W["W_abkv"], 320, 1024, stack=stB)
            em.barrier()
            ps = P.psum([128, 1024], F32, stB)
            ps_t = P.psum([128, 2048], BF16, stB)
            otile = P.sbuf([128, 512], BF16, stB)
            for hh in range(T // HW2):
                h0c = hh * HW2
                emit_ln(P, xin, h0c, h0c + HW2, xlnT, 0)
                ps_busy = [None, None]
                pt_busy = [None, None]
                ot_busy = None
                for which, wT, fdim, dst in [(0, wabqT, 480, abq_tm), (1, wabkvT, 320, abkv_tm)]:
                    nft = ceil_div(fdim, 128)
                    for ft in range(nft):
                        r0, r1 = ft * 128, min(fdim, ft * 128 + 128)
                        rows = r1 - r0
                        for tc in range(HW2 // 512):
                            slp = tc % 2
                            em.wait("tensor", ps_busy[slp])
                            for d in range(8):
                                em.op("tensor", "matmul", ps[0:rows, slp * 512:slp * 512 + 512],
                                      wT[:, d, r0:r1], xlnT[:, d, tc * 512:(tc + 1) * 512],
                                      start=(d == 0), stop=(d == 7))
                            mtok = em.mark("tensor")
                            em.wait("vector", mtok)
                            em.wait("vector", ot_busy)
                            em.op("vector", "tensor_copy", otile[0:rows, :],
                                  ps[0:rows, slp * 512:slp * 512 + 512])
                            ps_busy[slp] = em.mark("vector")
                            ot_busy = None
                            for q4 in range(4):
                                tt = (h0c // 128) + tc * 4 + q4
                                slq = tt % 2
                                em.wait("tensor", ps_busy[slp])
                                em.wait("tensor", pt_busy[slq])
                                em.op("tensor", "transpose",
                                      ps_t[0:128, slq * 1024:slq * 1024 + rows],
                                      otile[0:rows, q4 * 128:(q4 + 1) * 128],
                                      P.ident_bf[0:rows, 0:rows])
                                tttok = em.mark("tensor")
                                em.wait("vector", tttok)
                                em.op("vector", "tensor_copy", dst[:, tt, r0:r1],
                                      ps_t[0:128, slq * 1024:slq * 1024 + rows])
                                pt_busy[slq] = em.mark("vector")
                em.barrier()
        with contextlib.ExitStack() as stC:
            rb = P.sbuf([128, 4, 32], F32, stC)
            for tt in range(NTT):
                for r in range(4):
                    b0 = r * 80 + 16
                    x1 = abkv_tm[:, tt, b0:b0 + 32]
                    x2 = abkv_tm[:, tt, b0 + 32:b0 + 64]
                    em.op("vector", "tensor_tensor", rb[:, 0, :], x1, rcos[:, tt, :], ALU.mult)
                    em.op("vector", "tensor_tensor", rb[:, 1, :], x2, rsin[:, tt, :], ALU.mult)
                    em.op("vector", "tensor_tensor", rb[:, 2, :], x1, rsin[:, tt, :], ALU.mult)
                    em.op("vector", "tensor_tensor", rb[:, 3, :], x2, rcos[:, tt, :], ALU.mult)
                    em.op("vector", "tensor_tensor", x1, rb[:, 0, :], rb[:, 1, :], ALU.add)
                    em.op("vector", "tensor_tensor", x2, rb[:, 3, :], rb[:, 2, :], ALU.subtract)
            em.barrier()
        with contextlib.ExitStack() as stD:
            maskT = P.sbuf([128, 4, 512], BF16, stD)
            for k4 in range(4):
                em.dma("gpsimd", maskT[:, k4, :], P.cdr["attnb_maskT"][k4, :, :], stream="mskb")
            em.wait("vector", ("D_mskb", em.streams["D_mskb"]))
            em.barrier()
            q_tm = P.sbuf([128, 2, NTT, 64], BF16, stD)
            v_aug = P.sbuf([128, 2, NTT, 65], BF16, stD)
            qT_h = P.sbuf([64, 2, T], BF16, stD)
            kT_h = P.sbuf([64, 2, T], BF16, stD)
            kb = P.sbuf([128, 64], BF16, stD)
            Pj = P.sbuf([128, 3, 512], BF16, stD)
            rrow = P.sbuf([1, 512], F32, stD)
            acq = P.sbuf([128, RQ], F32, stD)
            ack = P.sbuf([128, 4], F32, stD)
            rb_sb = P.sbuf([64, 2, 512], F32, stD)
            ps_sc = P.psum([128, 1536], F32, stD)
            ps_av = P.psum([128, 1024], F32, stD)
            ps_rb = P.psum([128, 512], F32, stD)
            ps_t2 = P.psum([128, 2048], BF16, stD)
            sc_busy = [None] * 3
            pj_war = [None] * 3
            av_busy = [None, None]
            rb_busy = [None, None]
            t2_busy = [None, None]
            flash_done = [None, None]
            build_done = [None, None]

            def build_head(h):
                hs = h % 2
                em.wait("vector", flash_done[hs])
                em.op("vector", "memset", v_aug[:, hs, :, 64:65], 1.0)
                for tt in range(NTT):
                    aq = abq_tm[:, tt, :]
                    akv = abkv_tm[:, tt, :]
                    em.op("vector", "tensor_copy", acq[:, 0:RQ],
                          bass.AP(abq_tm, tt * 480 + h, [[480 * NTT, 128], [80, RQ]]))
                    em.op("vector", "tensor_copy", ack[:, 0:4],
                          bass.AP(abkv_tm, tt * 320 + h, [[320 * NTT, 128], [80, 4]]))
                    em.op("vector", "tensor_scalar", q_tm[:, hs, tt, :],
                          aq[:, 16:80], acq[:, 0:1], None, ALU.mult)
                    for r in range(1, RQ):
                        em.op("vector", "scalar_tensor_tensor", q_tm[:, hs, tt, :],
                              aq[:, r * 80 + 16:r * 80 + 80], acq[:, r:r + 1],
                              q_tm[:, hs, tt, :], ALU.mult, ALU.add)
                    em.op("vector", "tensor_scalar", kb[:, :],
                          akv[:, 16:80], ack[:, 0:1], None, ALU.mult)
                    em.op("vector", "scalar_tensor_tensor", kb[:, :],
                          akv[:, 96:160], ack[:, 1:2], kb[:, :], ALU.mult, ALU.add)
                    ktok = em.mark("vector")
                    em.wait("tensor", ktok)
                    em.wait("tensor", t2_busy[tt % 2])
                    em.op("tensor", "transpose",
                          ps_t2[0:64, (tt % 2) * 1024:(tt % 2) * 1024 + 128],
                          kb[:, :], P.ident_bf)
                    tt2 = em.mark("tensor")
                    em.wait("vector", tt2)
                    em.op("vector", "tensor_copy", kT_h[:, hs, tt * 128:(tt + 1) * 128],
                          ps_t2[0:64, (tt % 2) * 1024:(tt % 2) * 1024 + 128])
                    t2_busy[tt % 2] = em.mark("vector")
                    em.op("vector", "tensor_scalar", v_aug[:, hs, tt, 0:64],
                          akv[:, 176:240], ack[:, 2:3], None, ALU.mult)
                    em.op("vector", "scalar_tensor_tensor", v_aug[:, hs, tt, 0:64],
                          akv[:, 256:320], ack[:, 3:4],
                          v_aug[:, hs, tt, 0:64], ALU.mult, ALU.add)
                    qtok = em.mark("vector")
                    em.wait("tensor", qtok)
                    em.wait("tensor", t2_busy[(tt + 1) % 2])
                    em.op("tensor", "transpose",
                          ps_t2[0:64, ((tt + 1) % 2) * 1024:((tt + 1) % 2) * 1024 + 128],
                          q_tm[:, hs, tt, :], P.ident_bf)
                    tt3 = em.mark("tensor")
                    em.wait("vector", tt3)
                    em.op("vector", "tensor_copy", qT_h[:, hs, tt * 128:(tt + 1) * 128],
                          ps_t2[0:64, ((tt + 1) % 2) * 1024:((tt + 1) % 2) * 1024 + 128])
                    t2_busy[(tt + 1) % 2] = em.mark("vector")
                    yield
                build_done[hs] = em.mark("vector")

            for _ in build_head(0):
                pass
            for h in range(16):
                hs = h % 2
                gnext = build_head(h + 1) if h + 1 < 16 else None
                em.wait("tensor", build_done[hs])
                for t1b in range(T // 512):
                    if gnext is not None:
                        for _ in range(4):
                            next(gnext, None)
                    avsl = t1b % 2
                    pav = ps_av[0:65, avsl * 512:avsl * 512 + 512]
                    em.wait("tensor", av_busy[avsl])
                    njq = 4 * t1b + 4
                    DEPTH = 2

                    def emit_scores(j):
                        ssl = j % 3
                        em.wait("tensor", sc_busy[ssl])
                        em.op("tensor", "matmul", ps_sc[:, ssl * 512:ssl * 512 + 512],
                              kT_h[:, hs, j * 128:(j + 1) * 128],
                              qT_h[:, hs, t1b * 512:(t1b + 1) * 512], start=True, stop=True)
                        return em.mark("tensor")

                    def emit_exp(j, mtok):
                        ssl = j % 3
                        em.wait("scalar", mtok)
                        em.wait("scalar", pj_war[ssl])
                        em.op("scalar", "activation", Pj[:, ssl, :],
                              ps_sc[:, ssl * 512:ssl * 512 + 512], AF.Exp, scale=1.0 / 96.0)
                        etok = em.mark("scalar")
                        sc_busy[ssl] = etok
                        if j >= 4 * t1b:
                            em.wait("vector", etok)
                            em.op("vector", "tensor_tensor", Pj[:, ssl, :], Pj[:, ssl, :],
                                  maskT[:, j - 4 * t1b, :], ALU.mult)
                            return em.mark("vector")
                        return etok

                    stoks = {}
                    ptoks = {}
                    for j in range(min(DEPTH, njq)):
                        stoks[j] = emit_scores(j)
                        ptoks[j] = emit_exp(j, stoks[j])
                    for j in range(njq):
                        if j + DEPTH < njq:
                            stoks[j + DEPTH] = emit_scores(j + DEPTH)
                            ptoks[j + DEPTH] = emit_exp(j + DEPTH, stoks[j + DEPTH])
                        em.wait("tensor", ptoks[j])
                        em.op("tensor", "matmul", pav, v_aug[:, hs, j, :], Pj[:, j % 3, :],
                              start=(j == 0), stop=(j == njq - 1))
                        pj_war[j % 3] = em.mark("tensor")
                    avtok = em.mark("tensor")
                    em.wait("vector", avtok)
                    em.op("vector", "reciprocal", rrow[0:1, :], pav[64:65, :])
                    rtok = em.mark("vector")
                    em.wait("tensor", rtok)
                    em.wait("tensor", rb_busy[avsl])
                    em.op("tensor", "matmul", ps_rb[0:64, 0:512],
                          P.csb["ones_f32"][0:1, 0:64], rrow[0:1, :], start=True, stop=True)
                    rbtok = em.mark("tensor")
                    em.wait("vector", rbtok)
                    em.op("vector", "tensor_copy", rb_sb[:, avsl, :],
                          ps_rb[0:64, 0:512])
                    cptok = em.mark("vector")
                    rb_busy[avsl] = cptok
                    em.op("vector", "scalar_tensor_tensor",
                          outT_all[(h % 2) * 64:(h % 2) * 64 + 64, h // 2,
                                   t1b * 512:(t1b + 1) * 512],
                          pav[0:64, :], 0.5, rb_sb[:, avsl, :], ALU.mult, ALU.mult)
                    av_busy[avsl] = em.mark("vector")
                if gnext is not None:
                    for _ in gnext:
                        pass
                flash_done[hs] = av_busy[(T // 512 - 1) % 2]
            em.barrier()
        with contextlib.ExitStack() as stE:
            woT, _ = emit_weight_loadT(P, W["W_o"], 1024, 1024, stack=stE)
            em.barrier()
            if True:
                xres = P.sbuf([128, 2, 1024], F32, stE)
                yout = P.sbuf([128, 2, 1024], F32, stE)
                ps_y = P.psum([128, 2048], F32, stE)
                x_busy = [None, None]
                y_busy = [None, None]
                psy_busy = [None, None]
                for tt in range(NTT):
                    sl = tt % 2
                    em.wait("sync", x_busy[sl])
                    xtok = em.dma("sync", xres[:, sl, :], xin[tt * 128:(tt + 1) * 128, :],
                                  stream=f"xr{sl}")
                    em.wait("tensor", psy_busy[sl])
                    for dt in range(8):
                        for fc in range(2):
                            em.op("tensor", "matmul",
                                  ps_y[:, sl * 1024 + fc * 512:sl * 1024 + (fc + 1) * 512],
                                  outT_all[:, dt, tt * 128:(tt + 1) * 128],
                                  woT[:, dt, fc * 512:(fc + 1) * 512],
                                  start=(dt == 0), stop=(dt == 7))
                    ytok = em.mark("tensor")
                    em.wait("vector", ytok)
                    em.wait("vector", xtok)
                    em.wait("vector", y_busy[sl])
                    em.op("vector", "tensor_tensor", yout[:, sl, :], xres[:, sl, :],
                          ps_y[:, sl * 1024:(sl + 1) * 1024], ALU.add)
                    vtok = em.mark("vector")
                    psy_busy[sl] = vtok
                    x_busy[sl] = vtok
                    em.wait("sync", vtok)
                    y_busy[sl] = em.dma("sync", xout[tt * 128:(tt + 1) * 128, :],
                                        yout[:, sl, :], stream=f"yo{sl}")
                em.barrier()


PHASE_FNS["l0ab"] = ph_l0_attn_b


def ph_l1_ffn(P, T, xin, xout, W):
    em, nc = P.em, P.nc
    NE = 16
    SW = min(512, T)
    NS = T // SW
    NTT = SW // 128
    xlnT_d = nc.dram_tensor(P.name("ffn1_xln"), [8, 128, T], BF16).ap()

    with contextlib.ExitStack() as stA:
        c_sb = P.sbuf([128, T // 128, NE], F32, stA)
        with contextlib.ExitStack() as stL:
            gwT, _ = emit_weight_loadT(P, W["gate_w"], NE, 1024, stack=stL)
            stats = P.sbuf([128, 8], F32, stL)
            prob = P.sbuf([128, 2, NE], F32, stL)
            gsb = P.sbuf([128, 8], F32, stL)
            ps_l = P.psum([128, 1024], F32, stL)
            psl_busy = [None, None]
            em.barrier()
            xlnT = P.sbuf([128, 8, 512], BF16, stL)
            nblk = T // 512
            for b in range(nblk):
                emit_ln(P, xin, b * 512, (b + 1) * 512, xlnT, 0)
                for d in range(8):
                    em.dma("sync", xlnT_d[d, :, b * 512:(b + 1) * 512], xlnT[:, d, :],
                           stream="xlw")
                for it in range(4):
                    tt = b * 4 + it
                    psl = ps_l[:, (it % 2) * 512:(it % 2) * 512 + NE]
                    em.wait("tensor", psl_busy[it % 2])
                    for d in range(8):
                        em.op("tensor", "matmul", psl, xlnT[:, d, it * 128:(it + 1) * 128],
                              gwT[:, d, :], start=(d == 0), stop=(d == 7))
                    mtok = em.mark("tensor")
                    pr = prob[:, it % 2, :]
                    m1 = stats[:, 0:1]
                    den = stats[:, 1:2]
                    rden = stats[:, 2:3]
                    mg = stats[:, 3:4]
                    thr = stats[:, 4:5]
                    gs = gsb[:, 0:4]
                    gmask = gsb[:, 4:8]
                    tmp16 = prob[:, (it + 1) % 2, :]
                    em.wait("vector", mtok)
                    em.op("vector", "reduce_max", m1, psl, AX)
                    em.op("vector", "tensor_scalar", m1, m1, -1.0, None, ALU.mult)
                    em.wait("scalar", em.mark("vector"))
                    em.op("scalar", "activation", pr, psl, AF.Exp, bias=m1, accum_out=den)
                    stok = em.mark("scalar")
                    psl_busy[it % 2] = stok
                    em.wait("vector", stok)
                    em.op("vector", "reciprocal", rden, den)
                    em.op("vector", "tensor_scalar", pr, pr, rden, None, ALU.mult)
                    em.op("vector", "reduce_max", gs,
                          pr.rearrange("p (g q) -> p g q", g=4), AX)
                    em.op("vector", "reduce_max", mg, gs, AX)
                    em.op("vector", "tensor_scalar", gmask, gs, mg, None, ALU.is_ge)
                    em.op("vector", "scalar_tensor_tensor", gmask, gmask, -1e9, gs,
                          ALU.mult, ALU.add)
                    em.op("vector", "reduce_max", mg, gmask, AX)
                    em.op("vector", "tensor_scalar", gmask, gs, mg, None, ALU.is_ge)
                    em.op("vector", "tensor_tensor",
                          tmp16.rearrange("p (g q) -> p g q", g=4),
                          pr.rearrange("p (g q) -> p g q", g=4),
                          bass.AP(gsb, 4, [[8, 128], [1, 4], [0, 4]]), ALU.mult)
                    c_out = c_sb[:, tt, :]
                    em.op("vector", "tensor_copy", c_out, tmp16)
                    for _ in range(3):
                        em.op("vector", "reduce_max", thr, tmp16, AX)
                        em.op("vector", "tensor_scalar", tmp16,
                              tmp16, thr, None, ALU.is_ge, accum_out=None)
                        em.op("vector", "scalar_tensor_tensor", tmp16, tmp16, -1e9,
                              c_out, ALU.mult, ALU.add)
                        em.op("vector", "tensor_copy", c_out, tmp16)
                    em.op("vector", "reduce_max", thr, tmp16, AX)
                    em.op("vector", "tensor_tensor",
                          tmp16.rearrange("p (g q) -> p g q", g=4),
                          pr.rearrange("p (g q) -> p g q", g=4),
                          bass.AP(gsb, 4, [[8, 128], [1, 4], [0, 4]]), ALU.mult)
                    em.op("vector", "tensor_scalar", tmp16, tmp16, thr, None, ALU.is_ge)
                    em.op("vector", "tensor_tensor", c_sb[:, tt, :], tmp16, pr, ALU.mult)
                    em.mark("vector")
            em.barrier()
        NP = max(1, NS // 2)
        a_sb = P.sbuf([128, 8, 2, SW], BF16, stA)
        sg = P.sbuf([128, 512], BF16, stA)
        xlt = P.sbuf([128, NS, 8, SW], BF16, stA)
        xti = P.sbuf([128, 2, 1024], F32, stA)
        ystg = P.sbuf([128, 2, 1024], F32, stA)
        guA = P.sbuf([128, 8, 2048], BF16, stA)
        wdA = P.sbuf([128, 8, 1024], BF16, stA)
        ps_all = P.psum([128, 4096], F32, stA)

        def load_w(e):
            if e == NE:
                inter = 1024
                for d in range(8):
                    em.dma("gpsimd", guA[:, d, 0:1024], W["ws1"][d * 128:(d + 1) * 128, :],
                           stream="gu")
                    em.dma("gpsimd", guA[:, d, 1024:2048], W["ws3"][d * 128:(d + 1) * 128, :],
                           stream="gu")
            else:
                inter = 512
                for d in range(8):
                    em.dma("gpsimd", guA[:, d, 0:512], W["w1"][e, d * 128:(d + 1) * 128, :],
                           stream="gu")
                    em.dma("gpsimd", guA[:, d, 512:1024], W["w3"][e, d * 128:(d + 1) * 128, :],
                           stream="gu")
            return inter, ("D_gu", em.streams["D_gu"])

        def load_wd2(e):
            if e == NE:
                for j in range(8):
                    em.dma("gpsimd", wdA[:, j, :], W["ws2"][j * 128:(j + 1) * 128, :],
                           stream="wd")
            else:
                for j in range(4):
                    em.dma("gpsimd", wdA[:, j, :], W["w2"][e, j * 128:(j + 1) * 128, :],
                           stream="wd")
            return ("D_wd", em.streams["D_wd"])

        order = [NE] + list(range(NE))
        inter, gu_tok = load_w(order[0])
        wd_tok = load_wd2(order[0])
        for s0 in range(NS):
            for d in range(8):
                em.dma("sync", xlt[:, s0, d, :],
                       xlnT_d[d, :, s0 * SW:(s0 + 1) * SW], stream="xla1")
        xl_tok = ("D_xla1", em.streams["D_xla1"])
        xt_busy = [None, None]
        y_busy = [None, None]
        bank_busy = {}
        a_busy = None
        sgw = None
        for ei, e in enumerate(order):
            nj = inter // 128
            uoff = inter
            em.wait("tensor", gu_tok)
            em.wait("tensor", xl_tok)
            for sp in range(NP):
                s0 = sp * 2
                sweeps = [s0] if NS == 1 else [s0, s0 + 1]
                em.wait("tensor", a_busy)
                for j in range(nj):
                    jp = j % 2
                    gbase = jp * 1024
                    ubase = 2048 + jp * 1024
                    for si in range(len(sweeps)):
                        em.wait("tensor", bank_busy.get(("g", jp, si)))
                        em.wait("tensor", bank_busy.get(("u", jp, si)))
                    for d in range(8):
                        for si, s in enumerate(sweeps):
                            em.op("tensor", "matmul",
                                  ps_all[:, gbase + si * 512:gbase + (si + 1) * 512],
                                  guA[:, d, j * 128:(j + 1) * 128],
                                  xlt[:, s, d, :], start=(d == 0), stop=(d == 7))
                    for d in range(8):
                        for si, s in enumerate(sweeps):
                            em.op("tensor", "matmul",
                                  ps_all[:, ubase + si * 512:ubase + (si + 1) * 512],
                                  guA[:, d, uoff + j * 128:uoff + (j + 1) * 128],
                                  xlt[:, s, d, :], start=(d == 0), stop=(d == 7))
                    mtok = em.mark("tensor")
                    for si, s in enumerate(sweeps):
                        em.wait("scalar", mtok)
                        em.wait("scalar", sgw)
                        em.op("scalar", "activation", sg[:, :],
                              ps_all[:, gbase + si * 512:gbase + (si + 1) * 512], AF.Silu)
                        stk = em.mark("scalar")
                        bank_busy[("g", jp, si)] = stk
                        em.wait("vector", stk)
                        em.op("vector", "tensor_tensor", a_sb[:, j, si, :], sg[:, :],
                              ps_all[:, ubase + si * 512:ubase + (si + 1) * 512],
                              ALU.mult)
                        vtk = em.mark("vector")
                        bank_busy[("u", jp, si)] = vtk
                        sgw = vtk
                a_ready = sgw
                if sp == 0:
                    em.wait("tensor", wd_tok)
                if sp == NP - 1 and ei + 1 < len(order):
                    em.wait("gpsimd", mtok)
                    inter_next, gu_tok = load_w(order[ei + 1])
                else:
                    inter_next = None
                em.wait("tensor", a_ready)
                for si, s in enumerate(sweeps):
                    for tt in range(NTT):
                        gtt = s * NTT + tt
                        ysl2 = (si * NTT + tt) % 2
                        py = ps_all[:, ysl2 * 1024:(ysl2 + 1) * 1024]
                        em.wait("tensor", bank_busy.get(("y", ysl2)))
                        for jpar in range(2):
                            em.wait("tensor", bank_busy.get(("g", jpar, 0)))
                            em.wait("tensor", bank_busy.get(("g", jpar, 1)))
                        for j in range(nj):
                            for fc in range(2):
                                em.op("tensor", "matmul",
                                      py[:, fc * 512:(fc + 1) * 512],
                                      a_sb[:, j, si, tt * 128:(tt + 1) * 128],
                                      wdA[:, j, fc * 512:(fc + 1) * 512],
                                      start=(j == 0), stop=(j == nj - 1))
                        ytok = em.mark("tensor")
                        ysl = gtt % 2
                        em.wait("vector", ytok)
                        em.wait("vector", y_busy[ysl])
                        if e == NE:
                            em.wait("sync", xt_busy[ysl])
                            xttok = em.dma("sync", xti[:, ysl, :],
                                           xin[gtt * 128:(gtt + 1) * 128, :],
                                           stream=f"xt{ysl}")
                            em.wait("vector", xttok)
                            em.op("vector", "tensor_tensor", ystg[:, ysl, :],
                                  xti[:, ysl, :], py, ALU.add)
                            xt_busy[ysl] = em.mark("vector")
                            wtok = em.mark("vector")
                        else:
                            em.op("vector", "tensor_scalar", ystg[:, ysl, :], py,
                                  c_sb[:, gtt, e:e + 1], None, ALU.mult)
                            wtok = em.mark("vector")
                        bank_busy[("y", ysl2)] = wtok
                        bank_busy[("g", ysl2, 0)] = wtok
                        bank_busy[("g", ysl2, 1)] = wtok
                        em.wait("gpsimd", wtok)
                        ystream = "ya%d_%d" % (s, ysl)
                        if tt < 2:
                            for st2 in ("ya%d_0" % s, "ya%d_1" % s):
                                if "D_" + st2 in em.streams:
                                    em.wait("gpsimd", ("D_" + st2, em.streams["D_" + st2]))
                        if e == NE:
                            y_busy[ysl] = em.dma("gpsimd", xout[gtt * 128:(gtt + 1) * 128, :],
                                                 ystg[:, ysl, :], stream=ystream)
                        else:
                            y_busy[ysl] = em.dma("gpsimd", xout[gtt * 128:(gtt + 1) * 128, :],
                                                 ystg[:, ysl, :], stream=ystream,
                                                 accum_op=ALU.add)
                    a_busy = ytok
            if inter_next is not None:
                inter = inter_next
            if ei + 1 < len(order):
                em.wait("gpsimd", a_busy)
                wd_tok = load_wd2(order[ei + 1])
        em.barrier()


PHASE_FNS["l1ffn"] = ph_l1_ffn


def ph_l1_attn_b(P, T, xin, xout, W):
    em, nc = P.em, P.nc
    NTT = T // 128
    W_L = 0.9 ** 128
    xb_d = nc.dram_tensor(P.name("l1bh0_"), [T, 1024], F32).ap()
    if DBG.get("l1ab"):
        dbg_kk = nc.dram_tensor("dbg_kk", [T, 1024], BF16, kind="ExternalOutput").ap()
        dbg_v = nc.dram_tensor("dbg_v", [T, 1024], BF16, kind="ExternalOutput").ap()
        dbg_r = nc.dram_tensor("dbg_r", [T, 1024], BF16, kind="ExternalOutput").ap()
        dbg_y = nc.dram_tensor("dbg_y", [T, 1024], BF16, kind="ExternalOutput").ap()
    xb1_d = nc.dram_tensor(P.name("l1bout_"), [T, 1024], F32).ap()
    xr2_d = nc.dram_tensor(P.name("l1brr_"), [8, 128, T], BF16).ap()

    emit_ln(P, xin, 0, T, None, 0, out_dram=xb_d)

    with contextlib.ExitStack() as stA:
        w1col, _ = emit_load_cols(P, W["norm1_w"], 1024, stack=stA)
        WrA = P.sbuf([128, 8, 1024], BF16, stA)
        WkA = P.sbuf([128, 8, 1024], F32, stA)
        WvA = P.sbuf([128, 8, 1024], BF16, stA)
        WoA = P.sbuf([128, 8, 1024], BF16, stA)
        for d in range(8):
            em.dma("gpsimd", WrA[:, d, :], W["Wr"][d * 128:(d + 1) * 128, :], stream="wls")
            em.dma("sync", WkA[:, d, :], W["Wk"][d * 128:(d + 1) * 128, :], stream="wlsf")
            em.dma("gpsimd", WvA[:, d, :], W["Wv"][d * 128:(d + 1) * 128, :], stream="wls")
            em.dma("gpsimd", WoA[:, d, :], W["Wo"][d * 128:(d + 1) * 128, :], stream="wls")
        em.wait("vector", ("D_wls", em.streams["D_wls"]))
        em.wait("vector", ("D_wlsf", em.streams["D_wlsf"]))
        em.barrier()
        for Wx in (WrA, WkA, WvA):
            for d in range(8):
                em.op("vector", "tensor_scalar", Wx[:, d, :], Wx[:, d, :],
                      w1col[:, d:d + 1], None, ALU.mult)
        em.mark("vector")
        em.barrier()
        mskT = P.sbuf([128, 128], F32, stA)
        winb = P.sbuf([128, 128], F32, stA)
        woutc = P.sbuf([128, 1], F32, stA)
        em.dma("sync", mskT[:, :], P.cdr["l1b_maskT"][:, :], stream="mskc")
        em.dma("sync", winb[:, :], P.cdr["l1b_w_in"][:, :], stream="mskc")
        em.dma("sync", woutc[:, :], P.cdr["l1b_w_out"][:, :], stream="mskc")
        for e in ("tensor", "vector", "scalar"):
            em.wait(e, ("D_mskc", em.streams["D_mskc"]))
        S_sb = P.sbuf([64, 16, 64], BF16, stA)
        h0t = P.sbuf([128, 2, 1024], F32, stA)
        stats = P.sbuf([128, 8], F32, stA)
        sq = P.sbuf([128, 1024], F32, stA)
        xr = P.sbuf([128, 1024], BF16, stA)
        xr32 = P.sbuf([128, 1024], F32, stA)
        xrT = P.sbuf([128, 8, 128], BF16, stA)
        xrT32 = P.sbuf([128, 8, 128], F32, stA)
        rT = P.sbuf([128, 8, 128], BF16, stA)
        kkf = P.sbuf([128, 1024], F32, stA)
        cur = P.sbuf([128, 1024], F32, stA)
        thr = P.sbuf([128, 16], F32, stA)
        kk = P.sbuf([128, 1024], BF16, stA)
        kkT = P.sbuf([128, 8, 128], BF16, stA)
        v_tm = P.sbuf([128, 1024], BF16, stA)
        atm = P.sbuf([128, 2, 128], BF16, stA)
        rw = P.sbuf([64, 2, 128], BF16, stA)
        kw = P.sbuf([128, 2, 64], BF16, stA)
        yT = P.sbuf([128, 8, 128], BF16, stA)
        yo = P.sbuf([128, 2, 1024], F32, stA)
        psA = P.psum([128, 1024], F32, stA)
        psB = P.psum([128, 2048], F32, stA)
        ps_t = P.psum([128, 2048], BF16, stA)
        h_busy = [None, None]
        yo_busy = [None, None]
        pst_busy = [None, None]
        psA_free = None
        at_busy = {}
        atm_busy = {}
        y_busy = {}
        ds_busy = {}
        wo_evac = None
        prev_head_tok = None
        for ci in range(NTT):
            sl = ci % 2
            em.wait("sync", h_busy[sl])
            htok = em.dma("sync", h0t[:, sl, :], xb_d[ci * 128:(ci + 1) * 128, :],
                          stream=f"h0{sl}")
            h0 = h0t[:, sl, :]
            em.wait("scalar", htok)
            em.wait("scalar", prev_head_tok)
            em.op("scalar", "activation", sq[:, :], h0, AF.Square)
            sqtok = em.mark("scalar")
            msq = stats[:, 0:1]
            lnv = stats[:, 1:2]
            rstd = stats[:, 2:3]
            em.wait("vector", sqtok)
            em.op("vector", "reduce_sum", msq, sq[:, :], AX)
            em.op("vector", "tensor_scalar", msq, msq, 1.0 / 1024.0, 1e-5, ALU.mult,
                  ALU.add)
            vt = em.mark("vector")
            em.wait("scalar", vt)
            em.op("scalar", "activation", lnv, msq, AF.Ln)
            em.op("scalar", "activation", rstd, lnv, AF.Exp, scale=-0.5)
            stok = em.mark("scalar")
            em.wait("vector", stok)
            em.wait("vector", htok)
            em.op("vector", "tensor_scalar", xr32[:, :], h0, rstd, None, ALU.mult)
            em.op("vector", "tensor_copy", xr[:, :], xr32[:, :])
            xrtok = em.mark("vector")
            for d in range(8):
                slp = d % 2
                em.wait("tensor", xrtok)
                em.wait("tensor", pst_busy[slp])
                em.op("tensor", "transpose", ps_t[:, slp * 1024:slp * 1024 + 128],
                      xr[:, d * 128:(d + 1) * 128], P.ident_bf)
                tk = em.mark("tensor")
                em.wait("vector", tk)
                em.op("vector", "tensor_copy", xrT[:, d, :],
                      ps_t[:, slp * 1024:slp * 1024 + 128])
                pst_busy[slp] = em.mark("vector")
            xrT_tok = pst_busy[1]
            em.wait("tensor", pst_busy[0])
            em.wait("tensor", pst_busy[1])
            em.wait("tensor", psA_free)
            for h2 in ds_busy.values():
                em.wait("tensor", h2)
            ps32_busy = [None, None]
            for d in range(8):
                slp = d % 2
                em.wait("tensor", xrtok)
                em.wait("tensor", ps32_busy[slp])
                em.op("tensor", "transpose", psA[:, slp * 512:slp * 512 + 128],
                      xr32[:, d * 128:(d + 1) * 128], P.ident_f32)
                tk = em.mark("tensor")
                em.wait("vector", tk)
                em.op("vector", "tensor_copy", xrT32[:, d, :],
                      psA[:, slp * 512:slp * 512 + 128])
                ps32_busy[slp] = em.mark("vector")
            em.wait("tensor", ps32_busy[0])
            em.wait("tensor", ps32_busy[1])
            rsig = None
            for ft in range(8):
                slp = ft % 2
                em.wait("tensor", rsig_busy[slp] if False else None)
                for d in range(8):
                    em.op("tensor", "matmul", psA[:, slp * 512:slp * 512 + 128],
                          WrA[:, d, ft * 128:(ft + 1) * 128], xrT[:, d, :],
                          start=(d == 0), stop=(d == 7))
                mt = em.mark("tensor")
                if ft >= 2:
                    pass
                em.wait("scalar", mt)
                em.wait("scalar", rsig)
                em.op("scalar", "activation", rT[:, ft, :],
                      psA[:, slp * 512:slp * 512 + 128], AF.Sigmoid)
                rsig = em.mark("scalar")
                if ft < 6:
                    em.wait("tensor", rsig)
            em.wait("tensor", rsig)
            for d in range(8):
                for fc in range(2):
                    em.op("tensor", "matmul", psA[:, fc * 512:(fc + 1) * 512],
                          xrT32[:, d, :], WkA[:, d, fc * 512:(fc + 1) * 512],
                          start=(d == 0), stop=(d == 7))
            ktok = em.mark("tensor")
            kps = psA[:, :]
            em.wait("vector", ktok)
            em.op("vector", "tensor_scalar", kkf[:, :], kps, 0.5, None, ALU.is_gt)
            em.op("vector", "tensor_tensor", kkf[:, :], kkf[:, :], kps, ALU.mult)
            em.op("vector", "tensor_copy", cur[:, :], kkf[:, :])
            kv = kkf[:, :].rearrange("p (h q) -> p h q", h=16)
            cv = cur[:, :].rearrange("p (h q) -> p h q", h=16)
            for it in range(3):
                em.op("vector", "reduce_max", thr[:, :], cv, AX)
                em.op("vector", "tensor_tensor", sq[:, :].rearrange("p (h q) -> p h q", h=16),
                      cv, bass.AP(thr, 0, [[16, 128], [1, 16], [0, 64]]), ALU.is_ge)
                em.op("vector", "scalar_tensor_tensor", cur[:, :], sq[:, :], -1e9,
                      cur[:, :], ALU.mult, ALU.add)
            em.op("vector", "reduce_max", thr[:, :], cv, AX)
            em.op("vector", "tensor_tensor", sq[:, :].rearrange("p (h q) -> p h q", h=16),
                  kv, bass.AP(thr, 0, [[16, 128], [1, 16], [0, 64]]), ALU.is_ge)
            em.op("vector", "tensor_tensor", kk[:, :], sq[:, :], kkf[:, :], ALU.mult)
            kk_tok = em.mark("vector")
            em.wait("tensor", kk_tok)
            for d in range(8):
                for fc in range(2):
                    em.op("tensor", "matmul", psA[:, fc * 512:(fc + 1) * 512],
                          xrT[:, d, :], WvA[:, d, fc * 512:(fc + 1) * 512],
                          start=(d == 0), stop=(d == 7))
            vtok = em.mark("tensor")
            em.wait("vector", vtok)
            em.op("vector", "tensor_copy", v_tm[:, :], psA[:, :])
            v_tok = em.mark("vector")
            psA_free = v_tok
            for d in range(8):
                slp = d % 2
                em.wait("tensor", kk_tok)
                em.wait("tensor", pst_busy[slp])
                em.op("tensor", "transpose", ps_t[:, slp * 1024:slp * 1024 + 128],
                      kk[:, d * 128:(d + 1) * 128], P.ident_bf)
                tk = em.mark("tensor")
                em.wait("vector", tk)
                em.op("vector", "tensor_copy", kkT[:, d, :],
                      ps_t[:, slp * 1024:slp * 1024 + 128])
                pst_busy[slp] = em.mark("vector")
            em.wait("tensor", pst_busy[0])
            em.wait("tensor", pst_busy[1])
            em.wait("tensor", v_tok)
            ds_busy = {}
            for h in range(16):
                hp = (h % 2) * 64
                ht = h // 2
                hsl = h % 2
                rcT = rT[hp:hp + 64, ht, :]
                kcT = kkT[hp:hp + 64, ht, :]
                vh = v_tm[:, h * 64:(h + 1) * 64]
                kh = kk[:, h * 64:(h + 1) * 64]
                pat = psB[:, hsl * 512:hsl * 512 + 128]
                em.wait("tensor", atm_busy.get(hsl))
                em.op("tensor", "matmul", pat, kcT, rcT, start=True, stop=True)
                at_tok = em.mark("tensor")
                em.wait("vector", at_tok)
                em.op("vector", "tensor_tensor", atm[:, hsl, :], pat, mskT[:, :], ALU.mult)
                em.op("vector", "tensor_tensor", rw[:, hsl, :], rcT, winb[hp:hp + 64, :], ALU.mult)
                em.op("vector", "tensor_scalar", kw[:, hsl, :], kh, woutc[:, 0:1], None,
                      ALU.mult)
                a_tok = em.mark("vector")
                pyh = psB[:, 1024 + hsl * 512:1024 + hsl * 512 + 128]
                em.wait("tensor", a_tok)
                em.wait("tensor", y_busy.get(hsl))
                em.wait("tensor", wo_evac)
                em.op("tensor", "matmul", pyh[0:64, :], vh, atm[:, hsl, :],
                      start=True, stop=(ci == 0))
                if ci > 0:
                    em.op("tensor", "matmul", pyh[0:64, :], S_sb[:, h, :], rw[:, hsl, :],
                          start=False, stop=True)
                y_tok = em.mark("tensor")
                atm_busy[hsl] = y_tok
                em.wait("vector", y_tok)
                em.op("vector", "tensor_copy", yT[hp:hp + 64, ht, :], pyh[0:64, :])
                yc_tok = em.mark("vector")
                y_busy[hsl] = yc_tok
                pds = psA[0:64, hsl * 512:hsl * 512 + 64]
                em.wait("tensor", ds_busy.get(hsl))
                em.wait("tensor", psA_free)
                em.op("tensor", "matmul", pds, kw[:, hsl, :], vh,
                      start=True, stop=True)
                ds_tok = em.mark("tensor")
                em.wait("vector", ds_tok)
                if ci == 0:
                    em.op("vector", "tensor_copy", S_sb[:, h, :], pds)
                else:
                    em.op("vector", "scalar_tensor_tensor", S_sb[:, h, :],
                          S_sb[:, h, :], W_L, pds, ALU.mult, ALU.add)
                ds_busy[hsl] = em.mark("vector")
            head_tok = em.mark("vector")
            prev_head_tok = head_tok
            if DBG.get("l1ab"):
                em.wait("sync", kk_tok)
                em.wait("sync", v_tok)
                em.wait("sync", head_tok)
                d1 = em.dma("sync", dbg_kk[ci * 128:(ci + 1) * 128, :], kk[:, :], stream="dbg1")
                d2 = em.dma("sync", dbg_v[ci * 128:(ci + 1) * 128, :], v_tm[:, :], stream="dbg1")
                for dd in range(8):
                    d3 = em.dma("sync", dbg_r[ci * 128:(ci + 1) * 128, dd * 128:(dd + 1) * 128].rearrange("a b -> b a"), rT[:, dd, :], stream="dbg1", allow_slow_non_contiguous=True)
                    d4 = em.dma("sync", dbg_y[ci * 128:(ci + 1) * 128, dd * 128:(dd + 1) * 128].rearrange("a b -> b a"), yT[:, dd, :], stream="dbg1", allow_slow_non_contiguous=True)
                for e2 in ("vector", "scalar", "tensor"):
                    em.wait(e2, ("D_dbg1", em.streams["D_dbg1"]))
            em.wait("tensor", head_tok)
            for dt in range(8):
                for fc in range(2):
                    em.op("tensor", "matmul",
                          psB[:, 1024 + fc * 512:1024 + (fc + 1) * 512],
                          yT[:, dt, :], WoA[:, dt, fc * 512:(fc + 1) * 512],
                          start=(dt == 0), stop=(dt == 7))
            wo_tok = em.mark("tensor")
            wsl = ci % 2
            em.wait("vector", wo_tok)
            em.wait("vector", yo_busy[wsl])
            em.op("vector", "tensor_tensor", yo[:, wsl, :], h0,
                  psB[:, 1024:2048], ALU.add)
            yov = em.mark("vector")
            wo_evac = yov
            y_busy = {}
            h_busy[sl] = yov
            em.wait("sync", yov)
            yo_busy[wsl] = em.dma("sync", xb1_d[ci * 128:(ci + 1) * 128, :],
                                  yo[:, wsl, :], stream=f"x1{wsl}")
        em.barrier()

    with contextlib.ExitStack() as stB:
        xlnT = P.sbuf([128, 8, 512], BF16, stB)
        for b in range(T // 512):
            emit_ln(P, xb1_d, b * 512, (b + 1) * 512, xlnT, 0, rms=True)
            for d in range(8):
                em.dma("sync", xr2_d[d, :, b * 512:(b + 1) * 512], xlnT[:, d, :],
                       stream="xlw2")
        em.barrier()

    with contextlib.ExitStack() as stC:
        w2col, _ = emit_load_cols(P, W["norm2_w"], 1024, stack=stC)
        NJ = 32
        SW = min(1024, T)
        NQ = T // SW
        a_sb = P.sbuf([128, NJ, SW], BF16, stC)
        w12A = P.sbuf([128, 2, 8, 2, 1024], BF16, stC)
        w3A = P.sbuf([128, NJ, 512], BF16, stC)
        xlt = P.sbuf([128, 8, SW], BF16, stC)
        sgb = P.sbuf([128, 512], BF16, stC)
        x0t = P.sbuf([128, 2, 1024], F32, stC)
        x1t = P.sbuf([128, 2, 1024], F32, stC)
        ystg = P.sbuf([128, 2, 1024], F32, stC)
        ps_g = P.psum([128, 1024], F32, stC)
        ps_u = P.psum([128, 1024], F32, stC)
        ps_y = P.psum([128, 2048], F32, stC)
        psg_busy = [None, None]
        psu_busy = [None, None]
        psy_busy = {}
        sgw = None
        x0_busy = [None, None]
        x1_busy = [None, None]
        yo_busy = [None, None]
        a_busy = None
        xl_busy = None
        w12_fold = [None, None]
        fold_done = [None, None]
        w3_busy = None
        for q in range(NQ):
            em.wait("sync", xl_busy)
            for d in range(8):
                em.dma("sync", xlt[:, d, :], xr2_d[d, :, q * SW:(q + 1) * SW], stream="xl2")
            xltok = ("D_xl2", em.streams["D_xl2"])
            em.wait("tensor", xltok)
            em.wait("tensor", a_busy)
            def load_w12(fh2, slot):
                for d in range(8):
                    em.dma("gpsimd", w12A[:, slot, d, 0, :],
                           W["w1"][d * 128:(d + 1) * 128, fh2 * 1024:(fh2 + 1) * 1024],
                           stream=f"w12{slot}")
                    em.dma("gpsimd", w12A[:, slot, d, 1, :],
                           W["w2"][d * 128:(d + 1) * 128, fh2 * 1024:(fh2 + 1) * 1024],
                           stream=f"w12{slot}b")
                return (("D_w12" + str(slot), em.streams["D_w12" + str(slot)]),
                        ("D_w12" + str(slot) + "b", em.streams["D_w12" + str(slot) + "b"]))

            def fold_w12(slot, wtoks):
                em.wait("vector", wtoks[0])
                em.wait("vector", wtoks[1])
                for d in range(8):
                    for h2 in range(2):
                        em.op("vector", "tensor_scalar", w12A[:, slot, d, h2, :],
                              w12A[:, slot, d, h2, :], w2col[:, d:d + 1], None, ALU.mult)
                        yield
                fold_done[slot] = em.mark("vector")

            if q == 0:
                for _ in fold_w12(0, load_w12(0, 0)):
                    pass
            for fh in range(4):
                wsl2 = fh % 2
                em.wait("tensor", fold_done[wsl2])
                nxt = fh + 1 if fh + 1 < 4 else (0 if q + 1 < NQ else None)
                pend_fold = None
                if nxt is not None:
                    osl = (fh + 1) % 2
                    em.wait("gpsimd", w12_fold[osl])
                    pend_fold = fold_w12(osl, load_w12(nxt, osl))
                for jj in range(8):
                    j = fh * 8 + jj
                    for tc in range(SW // 512):
                        gslot = (jj * (SW // 512) + tc) % 2
                        pg = ps_g[:, gslot * 512:gslot * 512 + 512]
                        pu = ps_u[:, gslot * 512:gslot * 512 + 512]
                        em.wait("tensor", psg_busy[gslot])
                        em.wait("tensor", psu_busy[gslot])
                        for d in range(8):
                            em.op("tensor", "matmul", pg[:, :],
                                  w12A[:, wsl2, d, 0, jj * 128:(jj + 1) * 128],
                                  xlt[:, d, tc * 512:(tc + 1) * 512],
                                  start=(d == 0), stop=(d == 7))
                        for d in range(8):
                            em.op("tensor", "matmul", pu[:, :],
                                  w12A[:, wsl2, d, 1, jj * 128:(jj + 1) * 128],
                                  xlt[:, d, tc * 512:(tc + 1) * 512],
                                  start=(d == 0), stop=(d == 7))
                        mt = em.mark("tensor")
                        em.wait("scalar", mt)
                        em.wait("scalar", sgw)
                        em.op("scalar", "activation", sgb[:, :], pg, AF.Silu)
                        psg_busy[gslot] = em.mark("scalar")
                        em.wait("vector", psg_busy[gslot])
                        em.op("vector", "tensor_tensor",
                              a_sb[:, j, tc * 512:(tc + 1) * 512], sgb[:, :], pu,
                              ALU.mult)
                        psu_busy[gslot] = em.mark("vector")
                        sgw = psu_busy[gslot]
                    if pend_fold is not None and jj >= 2:
                        for _ in range(3):
                            next(pend_fold, None)
                if pend_fold is not None:
                    for _ in pend_fold:
                        pass
                w12_fold[wsl2] = mt
            a_ready = sgw
            em.wait("tensor", a_ready)
            for oh in range(2):
                em.wait("gpsimd", w3_busy)
                for j in range(NJ):
                    em.dma("gpsimd", w3A[:, j, :],
                           W["w3"][j * 128:(j + 1) * 128, oh * 512:(oh + 1) * 512],
                           stream="w3")
                w3tok = ("D_w3", em.streams["D_w3"])
                em.wait("tensor", w3tok)
                for tt in range(SW // 128):
                    gtt = q * (SW // 128) + tt
                    ysl = tt % 2
                    py = ps_y[:, ysl * 1024 + oh * 512:ysl * 1024 + (oh + 1) * 512]
                    em.wait("tensor", psy_busy.get((ysl, oh)))
                    for j in range(NJ):
                        em.op("tensor", "matmul", py,
                              a_sb[:, j, tt * 128:(tt + 1) * 128],
                              w3A[:, j, :], start=(j == 0), stop=(j == NJ - 1))
                    ytok = em.mark("tensor")
                    em.wait("sync", x0_busy[ysl])
                    x0tok = em.dma("sync", x0t[:, ysl, oh * 512:(oh + 1) * 512],
                                   xin[gtt * 128:(gtt + 1) * 128, oh * 512:(oh + 1) * 512],
                                   stream=f"xr{ysl}")
                    em.wait("sync", x1_busy[ysl])
                    x1tok = em.dma("sync", x1t[:, ysl, oh * 512:(oh + 1) * 512],
                                   xb1_d[gtt * 128:(gtt + 1) * 128, oh * 512:(oh + 1) * 512],
                                   stream=f"x1b{ysl}")
                    em.wait("vector", ytok)
                    em.wait("vector", x0tok)
                    em.wait("vector", x1tok)
                    em.wait("vector", yo_busy[ysl])
                    em.op("vector", "tensor_tensor", ystg[:, ysl, oh * 512:(oh + 1) * 512],
                          x1t[:, ysl, oh * 512:(oh + 1) * 512], py, ALU.add)
                    em.op("vector", "tensor_tensor", ystg[:, ysl, oh * 512:(oh + 1) * 512],
                          ystg[:, ysl, oh * 512:(oh + 1) * 512],
                          x0t[:, ysl, oh * 512:(oh + 1) * 512], ALU.add)
                    ev = em.mark("vector")
                    psy_busy[(ysl, oh)] = ev
                    x0_busy[ysl] = ev
                    x1_busy[ysl] = ev
                    em.wait("sync", ev)
                    yo_busy[ysl] = em.dma("sync",
                                          xout[gtt * 128:(gtt + 1) * 128, oh * 512:(oh + 1) * 512],
                                          ystg[:, ysl, oh * 512:(oh + 1) * 512],
                                          stream=f"yo{ysl}")
                w3_busy = ytok
            a_busy = ytok
            xl_busy = a_ready
        em.barrier()


PHASE_FNS["l1ab"] = ph_l1_attn_b


def ph_l1_attn_a(P, T, xin, xout, W):
    em, nc = P.em, P.nc
    HALF_PI = PI / 2.0
    INV2PI = 1.0 / TWO_PI
    QW = min(1024, T)
    NQ = T // QW
    NTTQ = QW // 128

    if DBG.get("l1aa"):
        dbg_zc = nc.dram_tensor("dbg_zc", [17, 128, T], BF16, kind="ExternalOutput").ap()
        dbg_pw = nc.dram_tensor("dbg_pw", [32, T], F32, kind="ExternalOutput").ap()
        dbg_den = nc.dram_tensor("dbg_den", [32, T], F32, kind="ExternalOutput").ap()
        dbg_cre = nc.dram_tensor("dbg_cre", [128, T], BF16, kind="ExternalOutput").ap()
        dbg_mrg = nc.dram_tensor("dbg_mrg", [128, T], BF16, kind="ExternalOutput").ap()
        dbg_yr = nc.dram_tensor("dbg_yr", [T, 1024], BF16, kind="ExternalOutput").ap()
        dbg_s = nc.dram_tensor("dbg_s", [32, T], F32, kind="ExternalOutput").ap()
    with contextlib.ExitStack() as stA:
        winA = P.sbuf([128, 8, 2080], BF16, stA)
        for d in range(8):
            em.dma("gpsimd", winA[:, d, :], W["W_in"][d * 128:(d + 1) * 128, :], stream="wia")
        WoutA = P.sbuf([128, 8, 1024], BF16, stA)
        for d in range(8):
            em.dma("gpsimd", WoutA[:, d, :], W["W_out"][d * 128:(d + 1) * 128, :], stream="wia")
        wre = P.sbuf([128, 32], BF16, stA)
        wim = P.sbuf([128, 32], BF16, stA)
        em.dma("gpsimd", wre[:, :], W["W_re"][:, :], stream="wia")
        em.dma("gpsimd", wim[:, :], W["W_im"][:, :], stream="wia")
        e3b = P.sbuf([128, 1024], BF16, stA)
        em.dma("gpsimd", e3b[:, :], P.cdr["l1a_e3"][:, :], stream="wia")
        dvec = P.sbuf([32, 1024], F32, stA)
        bconv_col, _ = emit_load_cols(P, W["b_conv"], 2080, stack=stA)
        ns_col, _ = emit_load_cols(P, W["norm_scale"], 256, stack=stA)
        ssc_col, _ = emit_load_cols(P, W["score_scale"], 32, stack=stA)
        em.barrier()
        em.op("vector", "tensor_scalar", wre[:, :], wre[:, :], ns_col[:, 0:1], None, ALU.mult)
        em.op("vector", "tensor_scalar", wim[:, :], wim[:, :], ns_col[:, 1:2], None, ALU.mult)
        wc_col = P.sbuf([128, 17, 4], F32, stA)
        sp_col = P.sbuf([32, 1], F32, stA)
        wpoly = P.sbuf([128, 4], F32, stA)
        theta_bc = P.sbuf([128, 4096], F32, stA)
        with contextlib.ExitStack() as st0:
            stg = P.sbuf([4, 2080], F32, st0)
            row = P.sbuf([1, 4096], F32, st0)
            ps0 = P.psum([128, 1024], F32, st0)
            t1 = em.dma("sync", stg[:, :], W["w_conv"].rearrange("a b c -> (a b) c"), stream="wib_wc")
            em.wait("tensor", t1)
            pb = [None, None]
            for ft in range(17):
                r0, r1 = ft * 128, min(2080, ft * 128 + 128)
                slp = ft % 2
                em.wait("tensor", pb[slp])
                em.op("tensor", "transpose", ps0[0:r1 - r0, slp * 512:slp * 512 + 4],
                      stg[:, r0:r1], P.ident_f32[0:4, 0:4])
                tk = em.mark("tensor")
                em.wait("vector", tk)
                em.op("vector", "tensor_copy", wc_col[0:r1 - r0, ft, :],
                      ps0[0:r1 - r0, slp * 512:slp * 512 + 4])
                pb[slp] = em.mark("vector")
            t2 = em.dma("sync", row[0:1, 0:28],
                        W["decay_slopes"].rearrange("(a b) -> a b", a=1), stream="wib_sl")
            em.wait("sync", t2)
            t3 = em.dma("sync", row[0:1, 28:32],
                        W["anchor_slopes"].rearrange("(a b) -> a b", a=1), stream="wib_sl")
            em.wait("tensor", t3)
            em.wait("tensor", pb[0])
            em.wait("tensor", pb[1])
            em.op("tensor", "transpose", ps0[0:32, 0:1], row[0:1, 0:32],
                  P.ident_f32[0:1, 0:1])
            tk = em.mark("tensor")
            slopes_tr = tk
            em.wait("vector", tk)
            em.op("vector", "tensor_copy", sp_col[:, 0:1], ps0[0:32, 0:1])
            sv = em.mark("vector")
            em.wait("scalar", sv)
            em.op("scalar", "activation", dvec[0:32, 0:1] if False else sp_col[:, 0:1],
                  sp_col[:, 0:1], AF.Abs)
            em.mark("scalar")
            em.op("vector", "tensor_copy", wpoly[0:32, 3:4], ps0[0:32, 0:1])
            sv2 = em.mark("vector")
            em.wait("scalar", sv2)
            em.op("scalar", "activation", sp_col[:, 0:1], wpoly[0:32, 3:4], AF.Abs)
            ab = em.mark("scalar")
            em.op("scalar", "activation", sp_col[:, 0:1], sp_col[:, 0:1], AF.Exp, scale=-1.0)
            em.op("scalar", "activation", sp_col[:, 0:1], sp_col[:, 0:1], AF.Ln, bias=1.0)
            lg = em.mark("scalar")
            em.wait("vector", lg)
            em.op("vector", "tensor_scalar", wpoly[0:32, 3:4], wpoly[0:32, 3:4], 0.0,
                  None, ALU.max)
            em.op("vector", "tensor_tensor", sp_col[:, 0:1], sp_col[:, 0:1],
                  wpoly[0:32, 3:4], ALU.add)
            em.mark("vector")
            em.wait("sync", slopes_tr)
            t4 = em.dma("sync", row[0:1, 0:3],
                        W["deriv_logits"].rearrange("(a b) -> a b", a=1), stream="wib_dl")
            em.wait("vector", t4)
            em.op("vector", "reduce_max", row[0:1, 8:9], row[0:1, 0:3], AX)
            rv = em.mark("vector")
            em.wait("vector", rv)
            em.op("vector", "tensor_scalar", row[0:1, 4:7], row[0:1, 0:3], row[0:1, 8:9],
                  None, ALU.subtract)
            rv = em.mark("vector")
            em.wait("scalar", rv)
            em.op("scalar", "activation", row[0:1, 4:7], row[0:1, 4:7], AF.Exp,
                  accum_out=row[0:1, 9:10])
            se = em.mark("scalar")
            em.wait("vector", se)
            em.op("vector", "reciprocal", row[0:1, 10:11], row[0:1, 9:10])
            em.op("vector", "tensor_scalar", row[0:1, 4:7], row[0:1, 4:7],
                  row[0:1, 10:11], None, ALU.mult)
            em.op("vector", "tensor_scalar", row[0:1, 6:7], row[0:1, 6:7], -1.0, None,
                  ALU.mult)
            rv = em.mark("vector")
            em.wait("tensor", rv)
            em.op("tensor", "matmul", ps0[:, 0:3], P.csb["ones_f32"][0:1, :],
                  row[0:1, 4:7], start=True, stop=True)
            tk = em.mark("tensor")
            em.wait("vector", tk)
            em.op("vector", "tensor_copy", wpoly[:, 0:3], ps0[:, 0:3])
            em.mark("vector")
            em.wait("sync", tk)
            t5 = em.dma("sync", row[0:1, :],
                        W["theta"].rearrange("a b c d e -> (a b) (c d e)"), stream="wib_th")
            em.wait("tensor", t5)
            for cc in range(8):
                em.op("tensor", "matmul", ps0[:, 512:1024], P.csb["ones_f32"][0:1, :],
                      row[0:1, cc * 512:(cc + 1) * 512], start=True, stop=True)
                tk = em.mark("tensor")
                em.wait("vector", tk)
                em.op("vector", "tensor_copy", theta_bc[:, cc * 512:(cc + 1) * 512],
                      ps0[:, 512:1024])
                vv = em.mark("vector")
                em.wait("tensor", vv)
            em.barrier()
        carry_re = P.sbuf([128, 32], F32, stA)
        carry_im = P.sbuf([128, 32], F32, stA)
        carry_den = P.sbuf([32, 1], F32, stA)
        em.op("vector", "memset", carry_re[:, :], 0.0)
        em.op("vector", "memset", carry_im[:, :], 0.0)
        em.op("vector", "memset", carry_den[:, :], 0.0)
        ones_f = P.sbuf([128, QW], F32, stA)
        em.op("vector", "memset", ones_f[:, :], 1.0)
        em.mark("vector")
        em.barrier()
        for q in range(NQ):
            E = 128 if q > 0 else 0
            q0 = q * QW
            with contextlib.ExitStack() as stQ:
                dv_tok = em.dma("sync", dvec[:, 0:QW], P.cdr["l1a_dvec"][:, q0:q0 + QW],
                                stream="wib_dv")
                for e2 in ("vector", "scalar"):
                    em.wait(e2, dv_tok)
                zc = P.sbuf([128, 17, QW], BF16, stQ)
                stP = contextlib.ExitStack()
                xlnT = P.sbuf([128, 8, E + QW], BF16, stP)
                emit_ln(P, xin, q0 - E, q0 + QW, xlnT, 0)
                ztile = P.sbuf([128, 2, E + QW], F32, stP)
                ps = P.psum([128, 1024], F32, stP)
                psb = [None, None]
                zt_busy = [None, None]
                for ft in range(17):
                    r0, r1 = ft * 128, min(2080, ft * 128 + 128)
                    rows = r1 - r0
                    sl = ft % 2
                    em.wait("vector", zt_busy[sl])
                    for tc in range(ceil_div(E + QW, 512)):
                        c0, c1 = tc * 512, min(E + QW, (tc + 1) * 512)
                        slp = tc % 2
                        em.wait("tensor", psb[slp])
                        for d in range(8):
                            em.op("tensor", "matmul", ps[0:rows, slp * 512:slp * 512 + (c1 - c0)],
                                  winA[:, d, r0:r1], xlnT[:, d, c0:c1],
                                  start=(d == 0), stop=(d == 7))
                        mt = em.mark("tensor")
                        em.wait("vector", mt)
                        em.op("vector", "tensor_copy", ztile[0:rows, sl, c0:c1],
                              ps[0:rows, slp * 512:slp * 512 + (c1 - c0)])
                        psb[slp] = em.mark("vector")
                    em.op("vector", "tensor_scalar", zc[0:rows, ft, :],
                          ztile[0:rows, sl, E:E + QW], wc_col[0:rows, ft, 3:4],
                          bconv_col[0:rows, ft:ft + 1], ALU.mult, ALU.add)
                    for dd in range(1, 4):
                        lo = E - dd
                        if lo >= 0:
                            em.op("vector", "scalar_tensor_tensor", zc[0:rows, ft, :],
                                  ztile[0:rows, sl, lo:lo + QW], wc_col[0:rows, ft, 3 - dd:4 - dd],
                                  zc[0:rows, ft, :], ALU.mult, ALU.add)
                        else:
                            em.op("vector", "scalar_tensor_tensor", zc[0:rows, ft, dd:],
                                  ztile[0:rows, sl, 0:QW - dd], wc_col[0:rows, ft, 3 - dd:4 - dd],
                                  zc[0:rows, ft, dd:], ALU.mult, ALU.add)
                    zt_busy[sl] = em.mark("vector")
                zc_tok = em.mark("vector")
                em.barrier()
                stP.close()
                pw = P.sbuf([32, QW], F32, stQ)
                tw = P.sbuf([32, QW], F32, stQ)
                denc = P.sbuf([32, QW], F32, stQ)
                em.op("vector", "tensor_scalar", pw[:, :], zc[0:32, 16, :],
                      ssc_col[0:32, 0:1], None, ALU.mult)
                em.op("vector", "tensor_scalar", pw[:, :], pw[:, :], 20.0, None, ALU.min)
                em.op("vector", "tensor_scalar", pw[:, :], pw[:, :], -20.0, None, ALU.max)
                em.op("vector", "tensor_scalar", tw[:, :], dvec[0:32, 0:QW],
                      sp_col[0:32, 0:1], None, ALU.mult)
                pv = em.mark("vector")
                em.wait("scalar", pv)
                em.op("scalar", "activation", pw[:, :], pw[:, :], AF.Exp)
                em.op("scalar", "activation", tw[:, :], tw[:, :], AF.Exp, scale=-1.0)
                pe = em.mark("scalar")
                em.wait("vector", pe)
                em.op("vector", "tensor_tensor", pw[:, :], pw[:, :], tw[:, :], ALU.mult)
                em.op("vector", "tensor_tensor_scan", denc[:, :], ones_f[0:32, :],
                      pw[:, :], carry_den[:, 0:1], ALU.mult, ALU.add)
                em.op("vector", "tensor_copy", carry_den[:, 0:1], denc[:, QW - 1:QW])
                em.op("vector", "tensor_scalar", tw[:, :], denc[:, :], 1e-4, None, ALU.max)
                em.op("vector", "reciprocal", tw[:, :], tw[:, :])
                pw_tok = em.mark("vector")
                pwT = P.sbuf([128, NTTQ, 32], F32, stQ)
                ps_f = P.psum([128, 1024], F32, stQ)
                pfb = [None, None]
                for tt in range(NTTQ):
                    slp = tt % 2
                    em.wait("tensor", pw_tok)
                    em.wait("tensor", pfb[slp])
                    em.op("tensor", "transpose", ps_f[0:128, slp * 512:slp * 512 + 32],
                          pw[:, tt * 128:(tt + 1) * 128], P.ident_f32[0:32, 0:32])
                    tk = em.mark("tensor")
                    em.wait("vector", tk)
                    em.op("vector", "tensor_copy", pwT[:, tt, :],
                          ps_f[0:128, slp * 512:slp * 512 + 32])
                    pfb[slp] = em.mark("vector")
                em.wait("tensor", pfb[0])
                em.wait("tensor", pfb[1])
                if DBG.get("l1aa"):
                    em.wait("sync", zc_tok)
                    em.wait("sync", pw_tok)
                    for ft in range(17):
                        rz = 128 if ft < 16 else 32
                        dz = em.dma("sync", dbg_zc[ft, 0:rz, q0:q0 + QW], zc[0:rz, ft, :], stream="dbgz")
                    dz = em.dma("sync", dbg_pw[:, q0:q0 + QW], pw[:, :], stream="dbgz")
                    dz = em.dma("sync", dbg_den[:, q0:q0 + QW], denc[:, :], stream="dbgz")
                    for e2 in ("vector", "scalar", "tensor", "gpsimd"):
                        em.wait(e2, dz)
                ps_t = P.psum([128, 2048], BF16, stQ)
                yraw = P.sbuf([128, NTTQ, 1024], BF16, stQ)
                ps_sn = P.psum([128, 1024], F32, stQ)
                ps_y = P.psum([128, 1024], F32, stQ)
                stK = contextlib.ExitStack()
                xv_tm = P.sbuf([128, NTTQ, 8, 128], BF16, stK)
                ptb = [None, None]
                for tt in range(NTTQ):
                    for ft in range(8):
                        slp = (tt * 8 + ft) % 2
                        em.wait("tensor", zc_tok)
                        em.wait("tensor", ptb[slp])
                        em.op("tensor", "transpose",
                              ps_t[0:128, slp * 1024:slp * 1024 + 128],
                              zc[:, ft, tt * 128:(tt + 1) * 128], P.ident_bf)
                        tk = em.mark("tensor")
                        em.wait("vector", tk)
                        em.op("vector", "tensor_copy", xv_tm[:, tt, ft, :],
                              ps_t[0:128, slp * 1024:slp * 1024 + 128])
                        ptb[slp] = em.mark("vector")
                em.wait("tensor", ptb[0])
                em.wait("tensor", ptb[1])
                phi = P.sbuf([128, QW], F32, stK)
                phc = P.sbuf([128, QW], F32, stK)
                rrf = P.sbuf([128, QW], F32, stK)
                sinv = P.sbuf([128, QW], F32, stK)
                cosv = P.sbuf([128, QW], F32, stK)
                poly = P.sbuf([128, NTTQ, 32], F32, stK)
                pp = P.sbuf([128, NTTQ, 32], F32, stK)
                mre = P.sbuf([128, QW], BF16, stK)
                mim = P.sbuf([128, QW], BF16, stK)
                cre = P.sbuf([128, QW], BF16, stK)
                cim = P.sbuf([128, QW], BF16, stK)
                sqb = P.sbuf([128, QW], BF16, stK)
                mg_prev = None
                sin_prev = None
                scan_prev = None
                yb = [None, None]
                sn_last = None
                for k in range(32):
                    off = (k // 4) * 128 + (k % 4) * 32
                    xv3 = bass.AP(xv_tm, off, [[NTTQ * 1024, 128], [1024, NTTQ], [1, 32]])
                    xv_bc = bass.AP(xv_tm, off,
                                    [[NTTQ * 1024, 128], [1024, NTTQ], [1, 32], [0, 4]])
                    th_sl = bass.AP(theta_bc, k * 128,
                                    [[4096, 128], [0, NTTQ], [4, 32], [1, 4]])
                    em.wait("vector", mg_prev)
                    em.wait("vector", sin_prev)
                    em.op("vector", "tensor_tensor",
                          phi[:, :].rearrange("p (a b c) -> p a b c", a=NTTQ, c=4),
                          xv_bc, th_sl, ALU.mult)
                    em.op("vector", "tensor_scalar", phc[:, :], phi[:, :], HALF_PI, None,
                          ALU.add)
                    BIGRND = 12582912.0
                    for srcb in (phi, phc):
                        em.op("vector", "tensor_scalar", rrf[:, :], srcb[:, :], INV2PI,
                              BIGRND, ALU.mult, ALU.add)
                        em.op("vector", "tensor_scalar", rrf[:, :], rrf[:, :], -BIGRND,
                              None, ALU.add)
                        em.op("vector", "scalar_tensor_tensor", srcb[:, :], rrf[:, :],
                              -TWO_PI, srcb[:, :], ALU.mult, ALU.add)
                    tr = em.mark("vector")
                    em.wait("scalar", tr)
                    em.wait("scalar", mg_prev)
                    em.op("scalar", "activation", sinv[:, :], phi[:, :], AF.Sin)
                    em.op("scalar", "activation", cosv[:, :], phc[:, :], AF.Sin)
                    sc_tok = em.mark("scalar")
                    sin_prev = sc_tok
                    em.op("vector", "tensor_tensor", poly[:, :, :], xv3, xv3, ALU.mult)
                    em.op("vector", "tensor_scalar", poly[:, :, :], poly[:, :, :],
                          wpoly[:, 2:3], None, ALU.mult)
                    em.op("vector", "scalar_tensor_tensor", pp[:, :, :], xv3,
                          wpoly[:, 1:2], poly[:, :, :], ALU.mult, ALU.add)
                    em.op("vector", "tensor_scalar", poly[:, :, :], pp[:, :, :],
                          wpoly[:, 0:1], None, ALU.add)
                    em.op("vector", "tensor_tensor", pp[:, :, :], poly[:, :, :],
                          bass.AP(pwT, k, [[NTTQ * 32, 128], [32, NTTQ], [0, 32]]),
                          ALU.mult)
                    pp_tok = em.mark("vector")
                    pp_bc = bass.AP(pp, 0, [[NTTQ * 32, 128], [32, NTTQ], [1, 32], [0, 4]])
                    em.wait("gpsimd", pp_tok)
                    em.wait("gpsimd", sc_tok)
                    em.wait("gpsimd", scan_prev)
                    em.op("gpsimd", "tensor_tensor",
                          mre[:, :].rearrange("p (a b c) -> p a b c", a=NTTQ, c=4),
                          cosv[:, :].rearrange("p (a b c) -> p a b c", a=NTTQ, c=4),
                          pp_bc, ALU.mult)
                    em.op("gpsimd", "tensor_tensor",
                          mim[:, :].rearrange("p (a b c) -> p a b c", a=NTTQ, c=4),
                          sinv[:, :].rearrange("p (a b c) -> p a b c", a=NTTQ, c=4),
                          pp_bc, ALU.mult)
                    mg = em.mark("gpsimd")
                    mg_prev = mg
                    for tt in range(NTTQ):
                        for which, mt_, ct_ in ((0, mre, cre), (1, mim, cim)):
                            slp = which
                            em.wait("tensor", mg)
                            em.wait("tensor", ptb[slp])
                            em.op("tensor", "transpose",
                                  ps_t[0:128, slp * 1024 + 128:slp * 1024 + 256],
                                  mt_[:, tt * 128:(tt + 1) * 128], P.ident_bf)
                            tk = em.mark("tensor")
                            em.wait("scalar", tk)
                            em.op("scalar", "copy", ct_[:, tt * 128:(tt + 1) * 128],
                                  ps_t[0:128, slp * 1024 + 128:slp * 1024 + 256])
                            ptb[slp] = em.mark("scalar")
                    mtr = em.mark("scalar")
                    em.wait("vector", mtr)
                    em.op("vector", "tensor_tensor_scan", mre[:, :], ones_f[:, :],
                          cre[:, :], carry_re[:, k:k + 1], ALU.mult, ALU.add)
                    em.op("vector", "tensor_copy", carry_re[:, k:k + 1], mre[:, QW - 1:QW])
                    em.op("vector", "tensor_tensor_scan", mim[:, :], ones_f[:, :],
                          cim[:, :], carry_im[:, k:k + 1], ALU.mult, ALU.add)
                    em.op("vector", "tensor_copy", carry_im[:, k:k + 1], mim[:, QW - 1:QW])
                    scan_prev = em.mark("vector")
                    em.op("vector", "tensor_tensor", sqb[:, :], mre[:, :], mre[:, :],
                          ALU.mult)
                    sq1 = em.mark("vector")
                    em.wait("tensor", sq1)
                    for cc in range(QW // 512):
                        em.op("tensor", "matmul", ps_sn[0:32, cc * 512:(cc + 1) * 512],
                              e3b[:, k * 32:(k + 1) * 32], sqb[:, cc * 512:(cc + 1) * 512],
                              start=(k == 0), stop=False)
                    sqm = em.mark("tensor")
                    em.wait("vector", sqm)
                    em.op("vector", "tensor_tensor", sqb[:, :], mim[:, :], mim[:, :],
                          ALU.mult)
                    sq2 = em.mark("vector")
                    em.wait("tensor", sq2)
                    for cc in range(QW // 512):
                        em.op("tensor", "matmul", ps_sn[0:32, cc * 512:(cc + 1) * 512],
                              e3b[:, k * 32:(k + 1) * 32], sqb[:, cc * 512:(cc + 1) * 512],
                              start=False, stop=(k == 31))
                    sn_last = em.mark("tensor")
                    ysl = k % 2
                    em.wait("tensor", yb[ysl])
                    for tt in range(NTTQ):
                        em.op("tensor", "matmul",
                              ps_y[:, ysl * 512 + tt * 32:ysl * 512 + (tt + 1) * 32],
                              mre[:, tt * 128:(tt + 1) * 128], wre[:, :],
                              start=True, stop=False)
                        em.op("tensor", "matmul",
                              ps_y[:, ysl * 512 + tt * 32:ysl * 512 + (tt + 1) * 32],
                              mim[:, tt * 128:(tt + 1) * 128], wim[:, :],
                              start=False, stop=True)
                    ym = em.mark("tensor")
                    em.wait("vector", ym)
                    em.op("vector", "tensor_copy",
                          bass.AP(yraw, k * 32, [[NTTQ * 1024, 128], [1024, NTTQ], [1, 32]]),
                          ps_y[:, ysl * 512:ysl * 512 + NTTQ * 32]
                          .rearrange("p (a b) -> p a b", b=32))
                    yb[ysl] = em.mark("vector")
                    if DBG.get("l1aa") and k == 0:
                        em.wait("sync", scan_prev)
                        dz = em.dma("sync", dbg_cre[:, q0:q0 + QW], mre[:, :], stream="dbgz")
                        dz = em.dma("sync", dbg_mrg[:, q0:q0 + QW], cre[:, :], stream="dbgz")
                        for e2 in ("vector", "scalar", "tensor", "gpsimd"):
                            em.wait(e2, dz)
                em.barrier()
                if not DBG.get("l1aa"):
                    stK.close()
                if DBG.get("l1aa"):
                    for tt in range(NTTQ):
                        dz = em.dma("sync", dbg_yr[(q * NTTQ + tt) * 128:(q * NTTQ + tt + 1) * 128, :],
                                    yraw[:, tt, :], stream="dbgz")
                    for e2 in ("vector", "scalar", "tensor", "gpsimd"):
                        em.wait(e2, dz)
                sT = P.sbuf([128, NTTQ, 32], F32, stQ)
                em.op("vector", "tensor_tensor", denc[:, :], tw[:, :], tw[:, :], ALU.mult)
                em.op("vector", "tensor_tensor", denc[:, :], ps_sn[0:32, 0:QW], denc[:, :],
                      ALU.mult)
                em.op("vector", "tensor_scalar", denc[:, :], denc[:, :], 1.0 / 256.0,
                      1e-5, ALU.mult, ALU.add)
                sv = em.mark("vector")
                em.wait("scalar", sv)
                em.op("scalar", "activation", denc[:, :], denc[:, :], AF.Ln)
                em.op("scalar", "activation", denc[:, :], denc[:, :], AF.Exp, scale=-0.5)
                sv = em.mark("scalar")
                em.wait("vector", sv)
                em.op("vector", "tensor_tensor", denc[:, :], denc[:, :], tw[:, :], ALU.mult)
                s_tok = em.mark("vector")
                if DBG.get("l1aa"):
                    em.wait("sync", s_tok)
                    dz = em.dma("sync", dbg_s[:, q0:q0 + QW], denc[:, :], stream="dbgz")
                    for e2 in ("vector", "scalar", "tensor", "gpsimd"):
                        em.wait(e2, dz)
                pfb = [None, None]
                for tt in range(NTTQ):
                    slp = tt % 2
                    em.wait("tensor", s_tok)
                    em.wait("tensor", pfb[slp])
                    em.op("tensor", "transpose", ps_f[0:128, slp * 512:slp * 512 + 32],
                          denc[:, tt * 128:(tt + 1) * 128], P.ident_f32[0:32, 0:32])
                    tk = em.mark("tensor")
                    em.wait("vector", tk)
                    em.op("vector", "tensor_copy", sT[:, tt, :],
                          ps_f[0:128, slp * 512:slp * 512 + 32])
                    pfb[slp] = em.mark("vector")
                ygT = P.sbuf([128, 8, QW], BF16, stQ)
                ytmp = P.sbuf([128, 1024], BF16, stQ)
                gtile = P.sbuf([128, 2, 128], BF16, stQ)
                ptb2 = [None, None]
                yt_tok = None
                for tt in range(NTTQ):
                    s_bc = bass.AP(sT, tt * 32, [[NTTQ * 32, 128], [1, 32], [0, 32]])
                    em.wait("vector", pfb[0])
                    em.wait("vector", pfb[1])
                    em.wait("vector", yt_tok)
                    em.op("vector", "tensor_tensor",
                          ytmp[:, :].rearrange("p (a b) -> p a b", b=32),
                          yraw[:, tt, :].rearrange("p (a b) -> p a b", b=32),
                          s_bc, ALU.mult)
                    yv = em.mark("vector")
                    for d in range(8):
                        slp = d % 2
                        em.wait("tensor", yv)
                        em.wait("tensor", ptb2[slp])
                        em.op("tensor", "transpose",
                              ps_t[0:128, slp * 1024 + 256:slp * 1024 + 384],
                              ytmp[:, d * 128:(d + 1) * 128], P.ident_bf)
                        tk = em.mark("tensor")
                        em.wait("scalar", ptb2[slp])
                        em.op("scalar", "activation", gtile[:, slp, :],
                              zc[:, 8 + d, tt * 128:(tt + 1) * 128], AF.Silu)
                        gtk = em.mark("scalar")
                        em.wait("vector", tk)
                        em.wait("vector", gtk)
                        em.op("vector", "tensor_tensor", ygT[:, d, tt * 128:(tt + 1) * 128],
                              gtile[:, slp, :],
                              ps_t[0:128, slp * 1024 + 256:slp * 1024 + 384], ALU.mult)
                        ptb2[slp] = em.mark("vector")
                    yt_tok = ptb2[1] if ptb2[1] is not None else ptb2[0]
                em.wait("tensor", ptb2[0])
                em.wait("tensor", ptb2[1])
                xres = P.sbuf([128, 2, 1024], F32, stQ)
                yout = P.sbuf([128, 2, 1024], F32, stQ)
                x_busy = [None, None]
                y_busy = [None, None]
                psy_busy = None
                for tt in range(NTTQ):
                    gtt = q * NTTQ + tt
                    sl = tt % 2
                    em.wait("sync", x_busy[sl])
                    xtok = em.dma("sync", xres[:, sl, :], xin[gtt * 128:(gtt + 1) * 128, :],
                                  stream=f"xr{sl}")
                    em.wait("tensor", psy_busy)
                    for fc in range(2):
                        for dt in range(8):
                            em.op("tensor", "matmul", ps_f[:, fc * 512:(fc + 1) * 512],
                                  ygT[:, dt, tt * 128:(tt + 1) * 128],
                                  WoutA[:, dt, fc * 512:(fc + 1) * 512],
                                  start=(dt == 0), stop=(dt == 7))
                    ym = em.mark("tensor")
                    em.wait("vector", ym)
                    em.wait("vector", xtok)
                    em.wait("vector", y_busy[sl])
                    em.op("vector", "tensor_tensor", yout[:, sl, :], xres[:, sl, :],
                          ps_f[:, :], ALU.add)
                    ev = em.mark("vector")
                    psy_busy = ev
                    x_busy[sl] = ev
                    em.wait("sync", ev)
                    y_busy[sl] = em.dma("sync", xout[gtt * 128:(gtt + 1) * 128, :],
                                        yout[:, sl, :], stream=f"yo{sl}")
                em.barrier()


PHASE_FNS["l1aa"] = ph_l1_attn_a


# version 06
# speedup vs baseline: 1.0302x; 1.0302x over previous
import math
import contextlib
import numpy as np
import ml_dtypes

import concourse.bass as bass
import concourse.mybir as mybir
from concourse.bass_utils import run_bass_kernel_spmd

F32 = mybir.dt.float32
BF16 = mybir.dt.bfloat16
I32 = mybir.dt.int32
AF = mybir.ActivationFunctionType
ALU = mybir.AluOpType
AX = mybir.AxisListType.X

TWO_PI = 6.283185307179586
PI = 3.141592653589793

D_MODEL = 1024
NCORES = 8


def build_consts(T):
    c = {}
    bf = ml_dtypes.bfloat16
    c["ident_f32"] = np.eye(128, dtype=np.float32)
    c["ones_f32"] = np.ones((128, 128), np.float32)
    tri = np.tril(np.ones((128, 128), np.float32))
    c["tri01_f32"] = tri
    mt = np.zeros((4, 128, 512), np.float32)
    for k in range(4):
        for c2 in range(128):
            t1 = np.arange(512)
            mt[k, c2, :] = (c2 + 128 * k <= t1).astype(np.float32)
    c["attnb_maskT"] = mt
    DH = 64
    inv_freq = (1.0 / 10000.0) ** (np.arange(0, DH, 2, dtype=np.float32) / DH)
    t = np.arange(T, dtype=np.float32)
    fr = np.outer(t, inv_freq)
    c["rope_cos"] = np.cos(fr).astype(np.float32)
    c["rope_sin"] = np.sin(fr).astype(np.float32)
    DEC = 0.9
    l = np.arange(128, dtype=np.float64)
    mT = np.zeros((128, 128), np.float64)
    for m in range(128):
        mT[m, m:] = DEC ** (l[m:] - m)
    c["l1b_maskT"] = mT.astype(np.float32)
    c["l1b_w_in"] = np.broadcast_to((DEC ** (l + 1.0))[None, :], (128, 128)).astype(np.float32).copy()
    c["l1b_w_out"] = (DEC ** (127.0 - l)).astype(np.float32).reshape(128, 1)
    dv = np.zeros((32, T), np.float32)
    dv[:28, :] = np.maximum(T - 1.0 - np.arange(T, dtype=np.float32), 0.0)[None, :]
    dv[28:, :] = np.arange(T, dtype=np.float32)[None, :]
    c["l1a_dvec"] = dv
    e3 = np.zeros((128, 1024), np.float32)
    for k in range(32):
        e3[:, 32 * k + k] = 1.0
    c["l1a_e3"] = e3
    return c



ENGS = ("tensor", "vector", "scalar", "gpsimd", "sync")
COMPUTE_SEMS = {"tensor": "T", "vector": "V", "scalar": "S", "gpsimd": "G"}


class EM:

    def __init__(self, nc):
        self.nc = nc
        self.recs = {e: [] for e in ENGS}
        self.ct = {s: 0 for s in COMPUTE_SEMS.values()}
        self.streams = {}
        self.stream_q = {}
        self.pending = {e: [] for e in ENGS}
        self.last_wait = {}
        self.nops = 0

    AUTO_SER = ("vector", "scalar", "gpsimd")

    def op(self, eng, meth, *args, **kw):
        if eng in self.AUTO_SER and self.recs[eng] and meth != "nop":
            prev = self.recs[eng][-1][1]
            if prev[0] != "dma_start" and prev[0] != "nop":
                tok = self.mark(eng)
                self.wait(eng, tok)
        rec = [meth, args, kw, []]
        self.recs[eng].append((self.pending[eng], rec))
        self.pending[eng] = []
        return rec

    def mark(self, eng):
        sem = COMPUTE_SEMS[eng]
        assert self.recs[eng], f"mark on empty engine {eng}"
        rec = self.recs[eng][-1][1]
        if rec[0] == "dma_start":
            rec = self.op(eng, "nop")
        for (s, n) in rec[3]:
            if s == sem:
                return (sem, rec[4])
        rec[3].append((sem, 1))
        self.ct[sem] += 1
        rec.append(self.ct[sem])
        return (sem, self.ct[sem])

    def dma(self, q, out, in_, stream="misc", **kw):
        sem = "D_" + stream
        qq = self.stream_q.setdefault(sem, q)
        assert qq == q, f"stream {stream} used from {qq} and {q}"
        self.streams[sem] = self.streams.get(sem, 0) + 16
        rec = self.op(q, "dma_start", out=out, in_=in_, **kw)
        rec[3].append((sem, 16))
        return (sem, self.streams[sem])

    def wait(self, eng, tok):
        if tok is None:
            return
        sem, val = tok
        if sem.startswith("D_") and val != self.streams.get(sem):
            raise AssertionError(
                f"unsafe partial DMA wait on {sem}: tok {val} != issued {self.streams.get(sem)}")
        key = (eng, sem)
        if self.last_wait.get(key, 0) >= val:
            return
        self.pending[eng].append((sem, val))
        self.last_wait[key] = val

    def nop(self, eng):
        self.nops += 1
        return self.op(eng, "nop")

    def barrier(self, engs=("tensor", "vector", "scalar", "gpsimd", "sync")):
        toks = []
        for e in ("tensor", "vector", "scalar", "gpsimd"):
            if not self.recs[e]:
                continue
            toks.append(self.mark(e))
        for s, v in self.streams.items():
            toks.append((s, v))
        for e in engs:
            for t in toks:
                self.wait(e, t)

    def flush(self, block, sems):
        for engname in ENGS:
            recs = self.recs[engname]
            tail = self.pending[engname]
            if not recs and not tail:
                continue

            def runner(eng, recs=recs, tail=tail):
                for waits, rec in recs:
                    for (s, v) in waits:
                        eng.wait_ge(sems[s], v)
                    meth, args, kw, incs = rec[0], rec[1], rec[2], rec[3]
                    instr = getattr(eng, meth)(*args, **kw)
                    for (s, n) in incs:
                        instr.then_inc(sems[s], n)
                for (s, v) in tail:
                    eng.wait_ge(sems[s], v)

            getattr(block, engname)(runner)



def ceil_div(a, b):
    return (a + b - 1) // b


class Prog:

    def __init__(self, nc, em, stack, consts_sb, ident_bf, ident_f32):
        self.nc = nc
        self.em = em
        self.stack = stack
        self.csb = consts_sb
        self.ident_bf = ident_bf
        self.ident_f32 = ident_f32
        self.uid = 0

    def name(self, p):
        self.uid += 1
        return f"{p}{self.uid}"

    def sbuf(self, shape, dtype, stack=None):
        st = stack or self.stack
        return st.enter_context(self.nc.sbuf_tensor(self.name("sb"), list(shape), dtype))

    def psum(self, shape, dtype=F32, stack=None):
        st = stack or self.stack
        return st.enter_context(self.nc.psum_tensor(self.name("ps"), list(shape), dtype))


def emit_transpose(P, dst_ap, src_ap, ps_ap, ident_ap, src_tok=None, ps_free_tok=None,
                   copy_eng="vector"):
    em = P.em
    if src_tok is not None:
        em.wait("tensor", src_tok)
    if ps_free_tok is not None:
        em.wait("tensor", ps_free_tok)
    em.op("tensor", "transpose", ps_ap, src_ap, ident_ap)
    t_tok = em.mark("tensor")
    em.wait(copy_eng, t_tok)
    em.op(copy_eng, "tensor_copy", dst_ap, ps_ap)
    c_tok = em.mark(copy_eng)
    return t_tok, c_tok


def emit_load_cols(P, dram_vec, n, dtype=F32, stack=None):
    em = P.em
    nt = ceil_div(n, 128)
    nfull = n // 128
    col = P.sbuf([128, nt], dtype, stack)
    ist = contextlib.ExitStack()
    stage = P.sbuf([128, 256], F32, ist)
    ps = P.psum([128, 32], F32, ist)
    em.op("vector", "memset", col[:, :], 0.0)
    mz = em.mark("vector")
    last_tok = mz
    if nfull > 0:
        src2d = dram_vec[0:nfull * 128].rearrange("(a b) -> a b", b=128)
        tok = em.dma("sync", stage[0:nfull, 0:128], src2d[0:nfull, :], stream="cols")
        em.wait("tensor", tok)
        em.op("tensor", "transpose", ps[:, 0:nfull], stage[0:nfull, 0:128],
              P.ident_f32[0:nfull, 0:nfull])
        ttok = em.mark("tensor")
        em.wait("vector", ttok)
        em.op("vector", "tensor_copy", col[:, 0:nfull], ps[:, 0:nfull])
        last_tok = em.mark("vector")
    rem = n - nfull * 128
    if rem > 0:
        tok = em.dma("sync", stage[0:1, 128:128 + rem],
                     dram_vec[nfull * 128:n].rearrange("(a b) -> a b", a=1), stream="cols2")
        em.wait("tensor", tok)
        em.wait("tensor", last_tok)
        em.op("tensor", "transpose", ps[0:rem, nfull:nfull + 1], stage[0:1, 128:128 + rem],
              P.ident_f32[0:1, 0:1])
        ttok = em.mark("tensor")
        em.wait("vector", ttok)
        em.op("vector", "tensor_copy", col[0:rem, nfull:nfull + 1],
              ps[0:rem, nfull:nfull + 1])
        last_tok = em.mark("vector")
    em.barrier()
    ist.close()
    return col, last_tok


def emit_weight_loadT(P, w_dram, rows, cols, dtype=BF16, stack=None):
    em = P.em
    rt = ceil_div(rows, 128)
    ct = ceil_div(cols, 128)
    out = P.sbuf([128, ct, rows], dtype, stack)
    ident = P.ident_bf if dtype == BF16 else P.ident_f32
    with contextlib.ExitStack() as st:
        stage = P.sbuf([128, 2, cols], dtype, st)
        ps = P.psum([128, 4096], dtype, st) if dtype == BF16 else P.psum([128, 2048], dtype, st)
        ps_busy = [None] * 4
        stage_busy = [None, None]
        tok = None
        for r in range(rt):
            r0, r1 = r * 128, min(rows, r * 128 + 128)
            sl = r % 2
            if stage_busy[sl] is not None:
                em.wait("gpsimd", stage_busy[sl])
            dtok = em.dma("gpsimd", stage[0:r1 - r0, sl, :], w_dram[r0:r1, :], stream=f"wl{sl}")
            em.wait("tensor", dtok)
            for cc in range(ct):
                c0, c1 = cc * 128, min(cols, cc * 128 + 128)
                slot = (r * ct + cc) % 4
                if ps_busy[slot] is not None:
                    em.wait("tensor", ps_busy[slot])
                off = slot * (1024 if dtype == BF16 else 512)
                em.op("tensor", "transpose",
                      ps[0:c1 - c0, off:off + (r1 - r0)],
                      stage[0:r1 - r0, sl, c0:c1], ident[0:r1 - r0, 0:r1 - r0])
                ttok = em.mark("tensor")
                em.wait("vector", ttok)
                em.op("vector", "tensor_copy", out[0:c1 - c0, cc, r0:r1],
                      ps[0:c1 - c0, off:off + (r1 - r0)])
                tok = em.mark("vector")
                ps_busy[slot] = tok
            stage_busy[sl] = tok
        em.barrier()
    return out, tok


DBG = {}


def emit_ln(P, x_dram, t0, t1, xlnT, xln_off, tm_out=None, eps=1e-5, rms_scale_col=None,
            rms=False, out_dram=None):
    if xlnT is None:
        return _emit_ln_impl(P, x_dram, t0, t1, None, 0, tm_out, eps, rms, out_dram)
    return _emit_ln_impl(P, x_dram, t0, t1, xlnT, xln_off, tm_out, eps, rms, out_dram)


def _emit_ln_impl(P, x_dram, t0, t1, xlnT, xln_off, tm_out, eps, rms, out_dram):
    em = P.em
    nt = (t1 - t0) // 128
    with contextlib.ExitStack() as st:
        xt = P.sbuf([128, 2, 1024], F32, st)
        sq = P.sbuf([128, 1024], F32, st)
        stats = P.sbuf([128, 8], F32, st)
        tmb = P.sbuf([128, 1024], BF16, st) if tm_out is None else None
        pst = P.psum([128, 2048], BF16, st)
        busy = {}
        ps_busy = [None, None]
        sq_war = None
        for i in range(nt):
            buf = i % 2
            if busy.get(buf) is not None:
                em.wait("sync", busy[buf])
            ld = em.dma("sync", xt[:, buf, :], x_dram[t0 + i * 128:t0 + (i + 1) * 128, :], stream=f"lnx{buf}")
            xtile = xt[:, buf, :]
            mu = stats[:, 0:1]
            msq = stats[:, 1:2]
            var = stats[:, 2:3]
            lnv = stats[:, 3:4]
            rstd = stats[:, 4:5]
            em.wait("vector", ld)
            em.wait("scalar", ld)
            em.wait("scalar", sq_war)
            em.op("scalar", "activation", sq[:, :], xtile, AF.Square)
            sq_tok = em.mark("scalar")
            if not rms:
                em.op("vector", "reduce_sum", mu, xtile, AX)
                em.mark("vector")
            em.wait("vector", sq_tok)
            em.op("vector", "reduce_sum", msq, sq[:, :], AX)
            sq_war = em.mark("vector")
            if rms:
                em.op("vector", "tensor_scalar", var, msq, 1.0 / 1024.0, None, ALU.mult)
            else:
                em.op("vector", "tensor_scalar", mu, mu, 1.0 / 1024.0, None, ALU.mult)
                em.op("vector", "tensor_scalar", msq, msq, 1.0 / 1024.0, None, ALU.mult)
                em.op("vector", "tensor_tensor", var, mu, mu, ALU.mult)
                em.op("vector", "tensor_tensor", var, msq, var, ALU.subtract)
            em.op("vector", "tensor_scalar", var, var, float(eps), None, ALU.add)
            v_tok = em.mark("vector")
            em.wait("scalar", v_tok)
            em.op("scalar", "activation", lnv, var, AF.Ln)
            em.mark("scalar")
            em.op("scalar", "activation", rstd, lnv, AF.Exp, scale=-0.5)
            s_tok = em.mark("scalar")
            tdst = tm_out[:, i, :] if tm_out is not None else tmb[:, :]
            em.wait("vector", s_tok)
            if rms:
                em.op("vector", "tensor_scalar", tdst, xtile, rstd, None, ALU.mult)
            else:
                em.op("vector", "tensor_scalar", tdst, xtile, mu, rstd, ALU.subtract, ALU.mult)
            n_tok = em.mark("vector")
            busy[buf] = n_tok
            if out_dram is not None:
                em.op("vector", "tensor_scalar", sq[:, :], xtile, mu, rstd, ALU.subtract, ALU.mult)
                f_tok = em.mark("vector")
                em.wait("sync", f_tok)
                sq_war = em.dma("sync", out_dram[t0 + i * 128:t0 + (i + 1) * 128, :], sq[:, :], stream="lnout")
                busy[buf] = f_tok
            for j in range((8 if xlnT is not None else 0) if not DBG.get("ln_notr") else 0):
                slot = j % 2
                if ps_busy[slot] is not None:
                    em.wait("tensor", ps_busy[slot])
                em.wait("tensor", n_tok)
                em.op("tensor", "transpose", pst[:, slot * 1024:slot * 1024 + 128],
                      tdst[:, j * 128:(j + 1) * 128], P.ident_bf)
                ttok = em.mark("tensor")
                em.wait("vector", ttok)
                em.op("vector", "tensor_copy",
                      xlnT[:, j, xln_off + i * 128:xln_off + (i + 1) * 128],
                      pst[:, slot * 1024:slot * 1024 + 128])
                ps_busy[slot] = em.mark("vector")
        em.barrier()




def ph_l0_attn_a(P, T, xin, xout, W, G=8):
    em, nc = P.em, P.nc
    if DBG.get("l0aa"):
        og_d = nc.dram_tensor("dbg_og", [T, 2048], BF16, kind="ExternalOutput").ap()
        p_d = nc.dram_tensor("dbg_p", [T, 1024], BF16, kind="ExternalOutput").ap()
    nC = T // 128
    G = min(G, nC)
    ngr = nC // G
    TG = G * 128
    SCALE = 1.0 / math.sqrt(2048.0)
    qT_d = nc.dram_tensor(P.name("l0aa_q"), [16, 128, T], BF16).ap()
    kT_d = nc.dram_tensor(P.name("l0aa_k"), [16, 128, T], BF16).ap()
    vT_d = nc.dram_tensor(P.name("l0aa_v"), [16, 128, T], BF16).ap()
    g_d = nc.dram_tensor(P.name("l0aa_g"), [T, 2048], BF16).ap()

    with contextlib.ExitStack() as stA:
        xlnT = P.sbuf([128, 8, T], BF16, stA)
        emit_ln(P, xin, 0, T, xlnT, 0)
        bq, _ = emit_load_cols(P, W["bq"], 2048, stack=stA)
        bk, _ = emit_load_cols(P, W["bk"], 2048, stack=stA)
        bv, tokb = emit_load_cols(P, W["bv"], 2048, stack=stA)
        em.barrier()
        ps = P.psum([128, 2048], F32, stA)
        wcol = P.sbuf([128, 2, 8, 128], BF16, stA)
        otile = P.sbuf([128, 2, T], BF16, stA)
        wbusy = [None, None]
        obusy = [None, None]
        psbusy = [None, None, None, None]
        for wi, (wname, bcol, dst) in enumerate(
                [("Wq", bq, qT_d), ("Wk", bk, kT_d), ("Wv", bv, vT_d)]):
            for ft in range(16):
                k = wi * 16 + ft
                sl = k % 2
                if wbusy[sl] is not None:
                    em.wait("gpsimd", wbusy[sl])
                for d in range(8):
                    wtok = em.dma("gpsimd", wcol[:, sl, d, :],
                                  W[wname][d * 128:(d + 1) * 128, ft * 128:(ft + 1) * 128],
                                  stream=f"wc{sl}")
                wtok = (f"D_wc{sl}", em.streams[f"D_wc{sl}"])
                em.wait("tensor", wtok)
                em.wait("vector", obusy[sl])
                mtok = None
                for tcp in range(max(1, T // 1024)):
                    ncc = min(2, T // 512)
                    for cc in range(ncc):
                        em.wait("tensor", psbusy[(tcp % 2) * 2 + cc])
                    for d in range(8):
                        for cc in range(ncc):
                            tc = tcp * 2 + cc
                            em.op("tensor", "matmul",
                                  ps[:, ((tcp % 2) * 2 + cc) * 512:((tcp % 2) * 2 + cc + 1) * 512],
                                  wcol[:, sl, d, :], xlnT[:, d, tc * 512:(tc + 1) * 512],
                                  start=(d == 0), stop=(d == 7))
                    mtok = em.mark("tensor")
                    em.wait("vector", mtok)
                    for cc in range(ncc):
                        tc = tcp * 2 + cc
                        em.op("vector", "tensor_scalar", otile[:, sl, tc * 512:(tc + 1) * 512],
                              ps[:, ((tcp % 2) * 2 + cc) * 512:((tcp % 2) * 2 + cc + 1) * 512],
                              bcol[:, ft:ft + 1], None, ALU.add)
                        psbusy[(tcp % 2) * 2 + cc] = em.mark("vector")
                wbusy[sl] = mtok
                for pb2 in range(4):
                    em.wait("sync", psbusy[pb2])
                obusy[sl] = em.dma("sync", dst[ft, :, :], otile[:, sl, :], stream=f"oo{sl}")
        wg_sb = P.sbuf([128, 8, 2048], BF16, stA)
        for d in range(8):
            gtok = em.dma("gpsimd", wg_sb[:, d, :], W["Wg"][d * 128:(d + 1) * 128, :], stream="wg")
        bg_row = P.sbuf([1, 2048], BF16, stA)
        gtok2 = em.dma("gpsimd", bg_row[0:1, :], W["bg"].rearrange("(a b) -> a b", a=1), stream="wg")
        em.wait("tensor", ("D_wg", em.streams["D_wg"]))
        psg = P.psum([128, 2048], F32, stA)
        gt = P.sbuf([128, 2, 2048], BF16, stA)
        gbusy = [None, None]
        psg_busy = None
        ones_bf = P.csb["ones_bf16"]
        for tt in range(nC):
            sl = tt % 2
            em.wait("tensor", psg_busy)
            for d in range(8):
                for fc in range(4):
                    em.op("tensor", "matmul", psg[:, fc * 512:(fc + 1) * 512],
                          xlnT[:, d, tt * 128:(tt + 1) * 128], wg_sb[:, d, fc * 512:(fc + 1) * 512],
                          start=(d == 0), stop=False)
            for fc in range(4):
                em.op("tensor", "matmul", psg[:, fc * 512:(fc + 1) * 512],
                      ones_bf[0:1, :], bg_row[0:1, fc * 512:(fc + 1) * 512],
                      start=False, stop=True)
            mtok = em.mark("tensor")
            em.wait("scalar", mtok)
            em.wait("scalar", gbusy[sl])
            em.op("scalar", "activation", gt[:, sl, :], psg[:, :], AF.Sigmoid)
            psg_busy = em.mark("scalar")
            em.wait("sync", psg_busy)
            gbusy[sl] = em.dma("sync", g_d[tt * 128:(tt + 1) * 128, :], gt[:, sl, :], stream=f"go{sl}")
        em.barrier()

    if DBG.get("stopA"):
        with contextlib.ExitStack() as stZ:
            cp = P.sbuf([128, 1024], F32, stZ)
            for tt in range(T // 128):
                ct = em.dma("sync", cp[:, :], xin[tt * 128:(tt + 1) * 128, :], stream="cpx")
                em.wait("sync", ct)
                em.dma("sync", xout[tt * 128:(tt + 1) * 128, :], cp[:, :], stream="cpx")
            em.barrier()
        return
    with contextlib.ExitStack() as stB:
        S = P.sbuf([128, 16, 2048], BF16, stB)
        kT_g = P.sbuf([128, 16, TG], BF16, stB)
        v_tm = P.sbuf([128, G, 2048], BF16, stB)
        wo_sb = P.sbuf([128, 16, 1024], BF16, stB)
        bo_row = P.sbuf([1, 1024], BF16, stB)
        for dt in range(16):
            wotok = em.dma("gpsimd", wo_sb[:, dt, :], W["Wo"][dt * 128:(dt + 1) * 128, :], stream="wo")
        botok = em.dma("gpsimd", bo_row[0:1, :], W["bo"].rearrange("(a b) -> a b", a=1), stream="wo")
        em.wait("tensor", ("D_wo", em.streams["D_wo"]))
        ps_pt = P.psum([128, 2048], BF16, stB)
        vtmp = P.sbuf([128, 16, 128], BF16, stB)
        for g in range(ngr):
            g0 = g * TG
            for dt in range(16):
                ktok = em.dma("sync", kT_g[:, dt, :], kT_d[dt, :, g0:g0 + TG], stream="kg")
            ktok = ("D_kg", em.streams["D_kg"])
            pt_busy = [None, None]
            vt_tok = None
            for c in range(G):
                em.wait("sync", vt_tok)
                for dt in range(16):
                    vtok = em.dma("sync", vtmp[:, dt, :],
                                  vT_d[dt, :, g0 + c * 128:g0 + (c + 1) * 128], stream="vt")
                vtok = ("D_vt", em.streams["D_vt"])
                em.wait("tensor", vtok)
                for dt in range(16):
                    sl = dt % 2
                    em.wait("tensor", pt_busy[sl])
                    em.op("tensor", "transpose", ps_pt[:, sl * 1024:sl * 1024 + 128],
                          vtmp[:, dt, :], P.ident_bf)
                    ttok = em.mark("tensor")
                    em.wait("vector", ttok)
                    em.op("vector", "tensor_copy", v_tm[:, c, dt * 128:(dt + 1) * 128],
                          ps_pt[:, sl * 1024:sl * 1024 + 128])
                    pt_busy[sl] = em.mark("vector")
                    vt_tok = pt_busy[sl]
            em.wait("tensor", ktok)
            with contextlib.ExitStack() as stC:
                Pbuf = P.sbuf([128, TG], BF16, stC)
                PTbuf = P.sbuf([128, TG], BF16, stC)
                qti = P.sbuf([128, 2, 16, 128], BF16, stC)
                gti = P.sbuf([128, 2, 2048], BF16, stC)
                og = P.sbuf([128, 2048], BF16, stC)
                ogT = P.sbuf([128, 16, 128], BF16, stC)
                xres2 = P.sbuf([128, 2, 1024], F32, stC)
                stats = P.sbuf([128, 4], F32, stC)
                ps_sc = P.psum([128, 1024], F32, stC)
                ps_o = P.psum([128, 2048], F32, stC)
                q_busy = [None, None]
                g_busy = [None, None]
                x_busy = [None, None]
                y_busy = [None, None]
                sc_free = None
                o_free = None
                qtoks = [None, None]
                gtoks = [None, None]
                xtoks = [None, None]

                def load_chunk(i2):
                    sl2 = i2 % 2
                    ci2 = g * G + i2
                    em.wait("sync", q_busy[sl2])
                    for dt in range(16):
                        em.dma("sync", qti[:, sl2, dt, :],
                               qT_d[dt, :, ci2 * 128:(ci2 + 1) * 128], stream=f"qt{sl2}")
                    qtoks[sl2] = ("D_qt" + str(sl2), em.streams["D_qt" + str(sl2)])
                    em.wait("sync", g_busy[sl2])
                    gtoks[sl2] = em.dma("sync", gti[:, sl2, :],
                                        g_d[ci2 * 128:(ci2 + 1) * 128, :], stream=f"gi{sl2}")
                    em.wait("sync", x_busy[sl2])
                    xtoks[sl2] = em.dma("sync", xres2[:, sl2, :],
                                        xin[ci2 * 128:(ci2 + 1) * 128, :], stream=f"xr{sl2}")

                load_chunk(0)
                for i in range(G):
                    ci = g * G + i
                    sl = i % 2
                    if i + 1 < G:
                        load_chunk(i + 1)
                    qtok = qtoks[sl]
                    gtok = gtoks[sl]
                    xtok = xtoks[sl]
                    ncols = (i + 1) * 128
                    em.wait("tensor", qtok)
                    em.wait("tensor", sc_free)
                    for cc in range(ceil_div(ncols, 512)):
                        c0, c1 = cc * 512, min(ncols, (cc + 1) * 512)
                        for d in range(16):
                            em.op("tensor", "matmul", ps_sc[:, c0:c1],
                                  qti[:, sl, d, :], kT_g[:, d, c0:c1],
                                  start=(d == 0), stop=(d == 15))
                    sc_tok = em.mark("tensor")
                    dm = stats[:, 0:1]
                    dmn = stats[:, 1:2]
                    den = stats[:, 2:3]
                    rden = stats[:, 3:4]
                    em.wait("vector", sc_tok)
                    em.op("vector", "reduce_max", dm, ps_sc[:, i * 128:ncols], AX)
                    em.op("vector", "tensor_scalar", dmn, dm, -SCALE, None, ALU.mult)
                    v_tok = em.mark("vector")
                    em.wait("scalar", v_tok)
                    em.op("scalar", "activation", Pbuf[:, i * 128:ncols],
                          ps_sc[:, i * 128:ncols], AF.Exp, bias=dmn, scale=SCALE)
                    e_tok = em.mark("scalar")
                    em.wait("vector", e_tok)
                    em.op("vector", "tensor_tensor", Pbuf[:, i * 128:ncols],
                          Pbuf[:, i * 128:ncols], P.csb["tri01_f32"][:, :], ALU.mult)
                    em.op("vector", "reduce_sum", den, Pbuf[:, i * 128:ncols], AX)
                    em.op("vector", "reciprocal", rden, den)
                    rtok = em.mark("vector")
                    em.wait("vector", rtok)
                    em.op("vector", "tensor_scalar", Pbuf[:, i * 128:ncols],
                          Pbuf[:, i * 128:ncols], rden, None, ALU.mult)
                    if i > 0:
                        em.op("vector", "tensor_copy", Pbuf[:, 0:i * 128],
                              ps_sc[:, 0:i * 128])
                    p_tok = em.mark("vector")
                    sc_free = p_tok
                    pt_busy2 = [None, None]
                    for j in range(i + 1):
                        slp = j % 2
                        em.wait("tensor", p_tok)
                        em.wait("tensor", pt_busy2[slp])
                        em.op("tensor", "transpose", ps_pt[:, slp * 1024:slp * 1024 + 128],
                              Pbuf[:, j * 128:(j + 1) * 128], P.ident_bf)
                        ttok = em.mark("tensor")
                        em.wait("vector", ttok)
                        em.op("vector", "tensor_copy", PTbuf[:, j * 128:(j + 1) * 128],
                              ps_pt[:, slp * 1024:slp * 1024 + 128])
                        pt_busy2[slp] = em.mark("vector")
                    em.wait("tensor", pt_busy2[0])
                    em.wait("tensor", pt_busy2[1])
                    em.wait("tensor", o_free)
                    nj = i + 1
                    for j in range(nj):
                        last = (j == nj - 1) and (g == 0)
                        for fc in range(4):
                            em.op("tensor", "matmul", ps_o[:, fc * 512:(fc + 1) * 512],
                                  PTbuf[:, j * 128:(j + 1) * 128], v_tm[:, j, fc * 512:(fc + 1) * 512],
                                  start=(j == 0), stop=last)
                    if g > 0:
                        for dt in range(16):
                            for fc in range(4):
                                em.op("tensor", "matmul", ps_o[:, fc * 512:(fc + 1) * 512],
                                      qti[:, sl, dt, :], S[:, dt, fc * 512:(fc + 1) * 512],
                                      start=False, stop=(dt == 15))
                    o_tok = em.mark("tensor")
                    em.wait("vector", o_tok)
                    em.wait("vector", gtok)
                    em.op("vector", "tensor_tensor", og[:, :], gti[:, sl, :], ps_o[:, :],
                          ALU.mult)
                    og_tok = em.mark("vector")
                    o_free = og_tok
                    if DBG.get("l0aa"):
                        em.wait("sync", og_tok)
                        dto = em.dma("sync", og_d[ci * 128:(ci + 1) * 128, :], og[:, :], stream="dbgo")
                        em.wait("sync", p_tok)
                        dtp = em.dma("sync", p_d[ci * 128:(ci + 1) * 128, 0:(i + 1) * 128],
                               Pbuf[:, 0:(i + 1) * 128], stream="dbgp")
                        for e2 in ("vector", "scalar"):
                            em.wait(e2, dto)
                            em.wait(e2, dtp)
                    pt_busy3 = [None, None]
                    for dt in range(16):
                        slp = dt % 2
                        em.wait("tensor", og_tok)
                        em.wait("tensor", pt_busy3[slp])
                        em.op("tensor", "transpose", ps_pt[:, slp * 1024:slp * 1024 + 128],
                              og[:, dt * 128:(dt + 1) * 128], P.ident_bf)
                        ttok = em.mark("tensor")
                        em.wait("vector", ttok)
                        em.op("vector", "tensor_copy", ogT[:, dt, :],
                              ps_pt[:, slp * 1024:slp * 1024 + 128])
                        pt_busy3[slp] = em.mark("vector")
                    em.wait("tensor", pt_busy3[0])
                    em.wait("tensor", pt_busy3[1])
                    em.wait("tensor", y_busy[sl])
                    em.wait("tensor", sc_free)
                    for dt in range(16):
                        for fc in range(2):
                            em.op("tensor", "matmul", ps_sc[:, fc * 512:(fc + 1) * 512],
                                  ogT[:, dt, :], wo_sb[:, dt, fc * 512:(fc + 1) * 512],
                                  start=(dt == 0), stop=False)
                    for fc in range(2):
                        em.op("tensor", "matmul", ps_sc[:, fc * 512:(fc + 1) * 512],
                              P.csb["ones_bf16"][0:1, :], bo_row[0:1, fc * 512:(fc + 1) * 512],
                              start=False, stop=True)
                    y_tok = em.mark("tensor")
                    em.wait("vector", y_tok)
                    em.wait("vector", xtok)
                    em.op("vector", "tensor_tensor", xres2[:, sl, :], xres2[:, sl, :],
                          ps_sc[:, :], ALU.add)
                    yv_tok = em.mark("vector")
                    sc_free = yv_tok
                    em.wait("sync", yv_tok)
                    y_busy[sl] = em.dma("sync", xout[ci * 128:(ci + 1) * 128, :],
                                        xres2[:, sl, :], stream=f"yo{sl}")
                    q_busy[sl] = o_tok
                    g_busy[sl] = og_tok
                    x_busy[sl] = y_busy[sl]
                em.barrier()
            if g < ngr - 1:
                with contextlib.ExitStack() as stD:
                    k_tm = P.sbuf([128, G, 2048], BF16, stD)
                    ps_st = P.psum([128, 1024], F32, stD)
                    ptb = [None, None]
                    for c in range(G):
                        for dt in range(16):
                            slp = dt % 2
                            em.wait("tensor", ptb[slp])
                            em.op("tensor", "transpose", ps_pt[:, slp * 1024:slp * 1024 + 128],
                                  kT_g[:, dt, c * 128:(c + 1) * 128], P.ident_bf)
                            ttok = em.mark("tensor")
                            em.wait("vector", ttok)
                            em.op("vector", "tensor_copy", k_tm[:, c, dt * 128:(dt + 1) * 128],
                                  ps_pt[:, slp * 1024:slp * 1024 + 128])
                            ptb[slp] = em.mark("vector")
                    em.wait("tensor", ptb[0])
                    em.wait("tensor", ptb[1])
                    st_free = [None, None]
                    for dt in range(16):
                        for ec in range(4):
                            k2 = dt * 4 + ec
                            slp = k2 % 2
                            em.wait("tensor", st_free[slp])
                            for c in range(G):
                                em.op("tensor", "matmul", ps_st[:, slp * 512:(slp + 1) * 512],
                                      k_tm[:, c, dt * 128:(dt + 1) * 128],
                                      v_tm[:, c, ec * 512:(ec + 1) * 512],
                                      start=(c == 0), stop=(c == G - 1))
                            mtok = em.mark("tensor")
                            em.wait("vector", mtok)
                            if g == 0:
                                em.op("vector", "tensor_copy", S[:, dt, ec * 512:(ec + 1) * 512],
                                      ps_st[:, slp * 512:(slp + 1) * 512])
                            else:
                                em.op("vector", "tensor_tensor", S[:, dt, ec * 512:(ec + 1) * 512],
                                      S[:, dt, ec * 512:(ec + 1) * 512],
                                      ps_st[:, slp * 512:(slp + 1) * 512], ALU.add)
                            st_free[slp] = em.mark("vector")
                    em.barrier()
        em.barrier()


PHASE_W = {
    "l0aa": ("L0_attn_a_", ["Wq", "bq", "Wk", "bk", "Wv", "bv", "Wg", "bg", "Wo", "bo"]),
    "l0ffn": ("L0_ffn_", ["gate_w", "base_wgu", "base_wd", "exp_wgu", "exp_wd"]),
    "l0ab": ("L0_attn_b_", ["W_abq", "W_abkv", "W_o"]),
    "l1aa": ("L1_attn_a_", ["W_in", "w_conv", "b_conv", "theta", "decay_slopes",
                             "anchor_slopes", "score_scale", "deriv_logits",
                             "norm_scale", "W_re", "W_im", "W_out"]),
    "l1ffn": ("L1_ffn_", ["gate_w", "w1", "w2", "w3", "ws1", "ws2", "ws3"]),
    "l1ab": ("L1_attn_b_", ["norm1_w", "Wr", "Wk", "Wv", "Wo", "norm2_w",
                             "w1", "w2", "w3"]),
}

INPUT_SHAPES = {
    "L0_attn_a_Wq": [1024, 2048], "L0_attn_a_bq": [2048],
    "L0_attn_a_Wk": [1024, 2048], "L0_attn_a_bk": [2048],
    "L0_attn_a_Wv": [1024, 2048], "L0_attn_a_bv": [2048],
    "L0_attn_a_Wg": [1024, 2048], "L0_attn_a_bg": [2048],
    "L0_attn_a_Wo": [2048, 1024], "L0_attn_a_bo": [1024],
    "L0_attn_b_W_abq": [480, 1024], "L0_attn_b_W_abkv": [320, 1024],
    "L0_attn_b_W_o": [1024, 1024],
    "L0_ffn_gate_w": [1024, 8], "L0_ffn_base_wgu": [1024, 5632],
    "L0_ffn_base_wd": [2816, 1024], "L0_ffn_exp_wgu": [8, 1024, 5632],
    "L0_ffn_exp_wd": [8, 2816, 1024],
    "L1_attn_a_W_in": [1024, 2080], "L1_attn_a_w_conv": [4, 1, 2080],
    "L1_attn_a_b_conv": [2080], "L1_attn_a_theta": [1, 1, 32, 32, 4],
    "L1_attn_a_decay_slopes": [28], "L1_attn_a_anchor_slopes": [4],
    "L1_attn_a_score_scale": [32], "L1_attn_a_deriv_logits": [3],
    "L1_attn_a_norm_scale": [256], "L1_attn_a_W_re": [128, 32],
    "L1_attn_a_W_im": [128, 32], "L1_attn_a_W_out": [1024, 1024],
    "L1_attn_b_norm1_w": [1024], "L1_attn_b_Wr": [1024, 1024],
    "L1_attn_b_Wk": [1024, 1024], "L1_attn_b_Wv": [1024, 1024],
    "L1_attn_b_Wo": [1024, 1024], "L1_attn_b_norm2_w": [1024],
    "L1_attn_b_w1": [1024, 4096], "L1_attn_b_w2": [1024, 4096],
    "L1_attn_b_w3": [4096, 1024],
    "L1_ffn_gate_w": [16, 1024], "L1_ffn_w1": [16, 1024, 512],
    "L1_ffn_w2": [16, 512, 1024], "L1_ffn_w3": [16, 1024, 512],
    "L1_ffn_ws1": [1024, 1024], "L1_ffn_ws2": [1024, 1024],
    "L1_ffn_ws3": [1024, 1024],
}

PHASE_FNS = {}


def build_program(T, phase_seq, g_l0aa=8):
    consts = build_consts(T)
    nc = bass.Bass("TRN2", target_bir_lowering=False, debug=False)
    x0 = nc.dram_tensor("x0", [T, 1024], F32, kind="ExternalInput").ap()
    y_out = nc.dram_tensor("y_out", [T, 1024], F32, kind="ExternalOutput").ap()
    need = set()
    for p in set(phase_seq):
        pref, names = PHASE_W[p]
        for n in names:
            need.add(pref + n)
    dr = {}
    for n in sorted(need):
        dr[n] = nc.dram_tensor(n, INPUT_SHAPES[n], F32, kind="ExternalInput").ap()
    cdr = {}
    for n, arr in consts.items():
        cdr[n] = nc.dram_tensor("c_" + n, list(arr.shape), F32, kind="ExternalInput").ap()
    xA = nc.dram_tensor("xA", [T, 1024], F32).ap()
    xB = nc.dram_tensor("xB", [T, 1024], F32).ap()
    with contextlib.ExitStack() as gstack:
        em = EM(nc)
        P = Prog(nc, em, gstack, {}, None, None)
        P.cdr = cdr
        ident_f32 = P.sbuf([128, 128], F32)
        ident_bf = P.sbuf([128, 128], BF16)
        ones_bf = P.sbuf([128, 128], BF16)
        ones_f = P.sbuf([128, 128], F32)
        tri01 = P.sbuf([128, 128], F32)
        t1 = em.dma("sync", ident_f32[:, :], cdr["ident_f32"][:, :], stream="init")
        t2 = em.dma("gpsimd", ident_bf[:, :], cdr["ident_f32"][:, :], stream="init2")
        t3 = em.dma("gpsimd", ones_bf[:, :], cdr["ones_f32"][:, :], stream="init2")
        t4 = em.dma("sync", tri01[:, :], cdr["tri01_f32"][:, :], stream="init")
        t5 = em.dma("sync", ones_f[:, :], cdr["ones_f32"][:, :], stream="init")
        for e in ("tensor", "vector", "scalar"):
            em.wait(e, ("D_init", em.streams["D_init"]))
            em.wait(e, ("D_init2", em.streams["D_init2"]))
        P.csb = {"ones_bf16": ones_bf, "tri01_f32": tri01, "ones_f32": ones_f}
        P.ident_bf = ident_bf[:, :]
        P.ident_f32 = ident_f32[:, :]
        cur = x0
        bufs = [xA, xB]
        for i, pk in enumerate(phase_seq):
            nxt = y_out if i == len(phase_seq) - 1 else bufs[i % 2]
            pref, names = PHASE_W[pk]
            Wm = {n: dr[pref + n] for n in names}
            kw = {}
            if pk == "l0aa":
                kw["G"] = g_l0aa
            PHASE_FNS[pk](P, T, cur, nxt, Wm, **kw)
            cur = nxt
        em.barrier()
        em.nop("gpsimd")
        sems = {}
        for s in set(COMPUTE_SEMS.values()) | set(em.streams.keys()):
            sems[s] = gstack.enter_context(nc.semaphore("sm" + s))
        with nc.Block() as block:
            em.flush(block, sems)
    return nc, consts, sorted(need)


PHASE_FNS["l0aa"] = ph_l0_attn_a

FULL_SEQ = ["l0aa", "l0ffn", "l0ab", "l0ffn", "l1aa", "l1ffn", "l1ab", "l1ffn"]


def kernel(**inputs):
    T = 4096
    nc, consts, need = build_program(T, FULL_SEQ)
    x_full = np.ascontiguousarray(inputs["L0_attn_a_hidden_states"], dtype=np.float32)
    in_maps = []
    for c in range(NCORES):
        m = {"x0": x_full[c]}
        for n in need:
            m[n] = np.ascontiguousarray(inputs[n], dtype=np.float32)
        for k, v in consts.items():
            m["c_" + k] = np.ascontiguousarray(v, dtype=np.float32)
        in_maps.append(m)
    res = run_bass_kernel_spmd(nc, in_maps, core_ids=list(range(NCORES)))
    out = np.stack([res.results[c]["y_out"] for c in range(NCORES)], axis=0)
    return out.astype(np.float32)


def ph_l0_ffn(P, T, xin, xout, W):
    em, nc = P.em, P.nc
    NE = 8
    INTER = 2816
    NJ = INTER // 128
    SW = min(512, T)
    NS = T // SW
    NTT = SW // 128
    xlnT_d = nc.dram_tensor(P.name("ffn_xln"), [8, 128, T], BF16).ap()

    with contextlib.ExitStack() as stA:
        cw_sb = P.sbuf([128, T // 128, NE], F32, stA)
        with contextlib.ExitStack() as stL:
            gw_sb = P.sbuf([128, 8, 8], BF16, stL)
            for d in range(8):
                em.dma("gpsimd", gw_sb[:, d, :], W["gate_w"][d * 128:(d + 1) * 128, :],
                       stream="gw")
            stats = P.sbuf([128, 8], F32, stL)
            prob = P.sbuf([128, 2, NE], F32, stL)
            ps_l = P.psum([128, 1024], F32, stL)
            psl_busy = [None, None]
            em.wait("tensor", ("D_gw", em.streams["D_gw"]))
            xlnT = P.sbuf([128, 8, 512], BF16, stL)
            nblk = T // 512
            for b in range(nblk):
                emit_ln(P, xin, b * 512, (b + 1) * 512, xlnT, 0)
                for d in range(8):
                    em.dma("sync", xlnT_d[d, :, b * 512:(b + 1) * 512], xlnT[:, d, :],
                           stream="xlw")
                for it in range(4):
                    tt = b * 4 + it
                    psl = ps_l[:, (it % 2) * 512:(it % 2) * 512 + NE]
                    em.wait("tensor", psl_busy[it % 2])
                    for d in range(8):
                        em.op("tensor", "matmul", psl, xlnT[:, d, it * 128:(it + 1) * 128],
                              gw_sb[:, d, :], start=(d == 0), stop=(d == 7))
                    mtok = em.mark("tensor")
                    pr = prob[:, it % 2, :]
                    m1 = stats[:, 0:1]
                    den = stats[:, 1:2]
                    rden = stats[:, 2:3]
                    m1p = stats[:, 3:4]
                    m2p = stats[:, 4:5]
                    tmp8 = prob[:, (it + 1) % 2, :]
                    em.wait("vector", mtok)
                    em.op("vector", "reduce_max", m1, psl, AX)
                    vt = em.mark("vector")
                    em.op("vector", "tensor_scalar", m1, m1, -1.0, None, ALU.mult)
                    em.wait("scalar", em.mark("vector"))
                    em.op("scalar", "activation", pr, psl, AF.Exp, bias=m1,
                          accum_out=den)
                    stok = em.mark("scalar")
                    psl_busy[it % 2] = stok
                    em.wait("vector", stok)
                    em.op("vector", "reciprocal", rden, den)
                    em.op("vector", "tensor_scalar", pr, pr, rden, None, ALU.mult)
                    em.op("vector", "reduce_max", m1p, pr, AX)
                    em.op("vector", "tensor_scalar", tmp8, pr, m1p, None, ALU.is_ge)
                    em.op("vector", "scalar_tensor_tensor", tmp8, tmp8, -1e9, pr,
                          ALU.mult, ALU.add)
                    em.op("vector", "reduce_max", m2p, tmp8, AX)
                    em.op("vector", "tensor_tensor", m1p, m1p, m2p, ALU.add)
                    em.op("vector", "reciprocal", m1p, m1p)
                    em.op("vector", "tensor_scalar", tmp8, pr, m2p, None, ALU.is_ge)
                    em.op("vector", "tensor_tensor", tmp8, tmp8, pr, ALU.mult)
                    em.op("vector", "tensor_scalar", cw_sb[:, tt, :], tmp8, m1p, None,
                          ALU.mult)
                    em.mark("vector")
            em.barrier()
        a_sb = P.sbuf([128, NJ, SW], BF16, stA)
        sg = P.sbuf([128, 512], BF16, stA)
        xlt = P.sbuf([128, 2, 8, SW], BF16, stA)
        xti = P.sbuf([128, 2, 1024], F32, stA)
        ystg = P.sbuf([128, 2, 1024], F32, stA)
        guA = P.sbuf([128, 8, 2 * INTER], BF16, stA)
        wdA = P.sbuf([128, NJ, 1024], BF16, stA)
        ps_g = P.psum([128, 1024], F32, stA)
        ps_u = P.psum([128, 1024], F32, stA)
        ps_y = P.psum([128, 2048], F32, stA)

        def load_gu(e):
            src = W["base_wgu"] if e == NE else W["exp_wgu"][e]
            for d in range(8):
                tok = em.dma("gpsimd", guA[:, d, :], src[d * 128:(d + 1) * 128, :],
                             stream="gu")
            return ("D_gu", em.streams["D_gu"])

        def load_wd(e):
            src = W["base_wd"] if e == NE else W["exp_wd"][e]
            for j in range(NJ):
                tok = em.dma("gpsimd", wdA[:, j, :], src[j * 128:(j + 1) * 128, :],
                             stream="wd")
            return ("D_wd", em.streams["D_wd"])

        order = [NE] + list(range(NE))
        gu_tok = load_gu(order[0])
        wd_tok = load_wd(order[0])
        xl_toks = [None, None]
        xl_guard = [None, None]

        def load_xl(s2):
            sl2 = s2 % 2
            em.wait("sync", xl_guard[sl2])
            for d in range(8):
                em.dma("sync", xlt[:, sl2, d, :],
                       xlnT_d[d, :, s2 * SW:(s2 + 1) * SW], stream=f"xl{sl2}")
            xl_toks[sl2] = (f"D_xl{sl2}", em.streams[f"D_xl{sl2}"])

        load_xl(0)
        xt_busy = [None, None]
        y_busy = [None, None]
        psg_busy = [None, None]
        psu_busy = [None, None]
        psy_busy = [None, None]
        a_busy = None
        sgw = None
        for ei, e in enumerate(order):
            em.wait("tensor", gu_tok)
            for s in range(NS):
                sl = s % 2
                em.wait("tensor", xl_toks[sl])
                em.wait("tensor", a_busy)
                for j in range(NJ):
                    pg = ps_g[:, (j % 2) * 512:(j % 2) * 512 + 512]
                    pu = ps_u[:, (j % 2) * 512:(j % 2) * 512 + 512]
                    em.wait("tensor", psg_busy[j % 2])
                    em.wait("tensor", psu_busy[j % 2])
                    for d in range(8):
                        em.op("tensor", "matmul", pg, guA[:, d, j * 128:(j + 1) * 128],
                              xlt[:, sl, d, :], start=(d == 0), stop=(d == 7))
                    for d in range(8):
                        em.op("tensor", "matmul", pu,
                              guA[:, d, INTER + j * 128:INTER + (j + 1) * 128],
                              xlt[:, sl, d, :], start=(d == 0), stop=(d == 7))
                    mtok = em.mark("tensor")
                    em.wait("scalar", mtok)
                    em.wait("scalar", sgw)
                    em.op("scalar", "activation", sg[:, :], pg, AF.Silu)
                    psg_busy[j % 2] = em.mark("scalar")
                    em.wait("vector", psg_busy[j % 2])
                    em.op("vector", "tensor_tensor", a_sb[:, j, :], sg[:, :], pu,
                          ALU.mult)
                    psu_busy[j % 2] = em.mark("vector")
                    sgw = psu_busy[j % 2]
                a_ready = sgw
                xl_guard[sl] = mtok
                if s + 1 < NS:
                    load_xl(s + 1)
                elif NS > 1:
                    load_xl(0)
                if s == 0:
                    em.wait("tensor", wd_tok)
                if s == NS - 1 and ei + 1 < len(order):
                    em.wait("gpsimd", mtok)
                    gu_tok = load_gu(order[ei + 1])
                em.wait("tensor", a_ready)
                for tt in range(NTT):
                    gtt = s * NTT + tt
                    psl2 = (tt % 2)
                    py = ps_y[:, psl2 * 1024:(psl2 + 1) * 1024]
                    em.wait("tensor", psy_busy[psl2])
                    for j in range(NJ):
                        for fc in range(2):
                            em.op("tensor", "matmul",
                                  py[:, fc * 512:(fc + 1) * 512],
                                  a_sb[:, j, tt * 128:(tt + 1) * 128],
                                  wdA[:, j, fc * 512:(fc + 1) * 512],
                                  start=(j == 0), stop=(j == NJ - 1))
                    ytok = em.mark("tensor")
                    ysl = gtt % 2
                    em.wait("vector", ytok)
                    em.wait("vector", y_busy[ysl])
                    if e == NE:
                        em.wait("sync", xt_busy[ysl])
                        xttok = em.dma("sync", xti[:, ysl, :],
                                       xin[gtt * 128:(gtt + 1) * 128, :],
                                       stream=f"xt{ysl}")
                        em.wait("vector", xttok)
                        em.op("vector", "tensor_tensor", ystg[:, ysl, :],
                              xti[:, ysl, :], py, ALU.add)
                        xt_busy[ysl] = em.mark("vector")
                        wtok = em.mark("vector")
                    else:
                        em.op("vector", "tensor_scalar", ystg[:, ysl, :], py,
                              cw_sb[:, gtt, e:e + 1], None, ALU.mult)
                        wtok = em.mark("vector")
                    psy_busy[psl2] = wtok
                    q = "gpsimd"
                    em.wait(q, wtok)
                    ystream = "ya%d_%d" % (s, ysl)
                    if tt < 2:
                        for st2 in ("ya%d_0" % s, "ya%d_1" % s):
                            if "D_" + st2 in em.streams:
                                em.wait(q, ("D_" + st2, em.streams["D_" + st2]))
                    if e == NE:
                        y_busy[ysl] = em.dma("gpsimd", xout[gtt * 128:(gtt + 1) * 128, :],
                                             ystg[:, ysl, :], stream=ystream)
                    else:
                        y_busy[ysl] = em.dma("gpsimd", xout[gtt * 128:(gtt + 1) * 128, :],
                                             ystg[:, ysl, :], stream=ystream,
                                             accum_op=ALU.add)
                a_busy = ytok
            if ei + 1 < len(order):
                em.wait("gpsimd", a_busy)
                wd_tok = load_wd(order[ei + 1])
        em.barrier()


PHASE_FNS["l0ffn"] = ph_l0_ffn


def ph_l0_attn_b(P, T, xin, xout, W):
    em, nc = P.em, P.nc
    NTT = T // 128
    NH, DH, CH = 16, 64, 80
    RQ, RK, RV = 6, 2, 2

    with contextlib.ExitStack() as stA:
        outT_all = P.sbuf([128, 8, T], BF16, stA)
        abq_tm = P.sbuf([128, NTT, 480], BF16, stA)
        abkv_tm = P.sbuf([128, NTT, 320], BF16, stA)
        rcos = P.sbuf([128, NTT, 32], BF16, stA)
        rsin = P.sbuf([128, NTT, 32], BF16, stA)
        for tt in range(NTT):
            em.dma("gpsimd", rcos[:, tt, :], P.cdr["rope_cos"][tt * 128:(tt + 1) * 128, :],
                   stream="rope")
            em.dma("gpsimd", rsin[:, tt, :], P.cdr["rope_sin"][tt * 128:(tt + 1) * 128, :],
                   stream="rope")
        with contextlib.ExitStack() as stB:
            HW2 = min(2048, T)
            xlnT = P.sbuf([128, 8, HW2], BF16, stB)
            wabqT, _ = emit_weight_loadT(P, W["W_abq"], 480, 1024, stack=stB)
            wabkvT, tok2 = emit_weight_loadT(P, W["W_abkv"], 320, 1024, stack=stB)
            em.barrier()
            ps = P.psum([128, 1024], F32, stB)
            ps_t = P.psum([128, 2048], BF16, stB)
            otile = P.sbuf([128, 512], BF16, stB)
            for hh in range(T // HW2):
                h0c = hh * HW2
                emit_ln(P, xin, h0c, h0c + HW2, xlnT, 0)
                ps_busy = [None, None]
                pt_busy = [None, None]
                ot_busy = None
                for which, wT, fdim, dst in [(0, wabqT, 480, abq_tm), (1, wabkvT, 320, abkv_tm)]:
                    nft = ceil_div(fdim, 128)
                    for ft in range(nft):
                        r0, r1 = ft * 128, min(fdim, ft * 128 + 128)
                        rows = r1 - r0
                        for tc in range(HW2 // 512):
                            slp = tc % 2
                            em.wait("tensor", ps_busy[slp])
                            for d in range(8):
                                em.op("tensor", "matmul", ps[0:rows, slp * 512:slp * 512 + 512],
                                      wT[:, d, r0:r1], xlnT[:, d, tc * 512:(tc + 1) * 512],
                                      start=(d == 0), stop=(d == 7))
                            mtok = em.mark("tensor")
                            em.wait("vector", mtok)
                            em.wait("vector", ot_busy)
                            em.op("vector", "tensor_copy", otile[0:rows, :],
                                  ps[0:rows, slp * 512:slp * 512 + 512])
                            ps_busy[slp] = em.mark("vector")
                            ot_busy = None
                            for q4 in range(4):
                                tt = (h0c // 128) + tc * 4 + q4
                                slq = tt % 2
                                em.wait("tensor", ps_busy[slp])
                                em.wait("tensor", pt_busy[slq])
                                em.op("tensor", "transpose",
                                      ps_t[0:128, slq * 1024:slq * 1024 + rows],
                                      otile[0:rows, q4 * 128:(q4 + 1) * 128],
                                      P.ident_bf[0:rows, 0:rows])
                                tttok = em.mark("tensor")
                                em.wait("vector", tttok)
                                em.op("vector", "tensor_copy", dst[:, tt, r0:r1],
                                      ps_t[0:128, slq * 1024:slq * 1024 + rows])
                                pt_busy[slq] = em.mark("vector")
                em.barrier()
        with contextlib.ExitStack() as stC:
            rb = P.sbuf([128, 4, 32], F32, stC)
            for tt in range(NTT):
                for r in range(4):
                    b0 = r * 80 + 16
                    x1 = abkv_tm[:, tt, b0:b0 + 32]
                    x2 = abkv_tm[:, tt, b0 + 32:b0 + 64]
                    em.op("vector", "tensor_tensor", rb[:, 0, :], x1, rcos[:, tt, :], ALU.mult)
                    em.op("vector", "tensor_tensor", rb[:, 1, :], x2, rsin[:, tt, :], ALU.mult)
                    em.op("vector", "tensor_tensor", rb[:, 2, :], x1, rsin[:, tt, :], ALU.mult)
                    em.op("vector", "tensor_tensor", rb[:, 3, :], x2, rcos[:, tt, :], ALU.mult)
                    em.op("vector", "tensor_tensor", x1, rb[:, 0, :], rb[:, 1, :], ALU.add)
                    em.op("vector", "tensor_tensor", x2, rb[:, 3, :], rb[:, 2, :], ALU.subtract)
            em.barrier()
        with contextlib.ExitStack() as stD:
            maskT = P.sbuf([128, 4, 512], BF16, stD)
            for k4 in range(4):
                em.dma("gpsimd", maskT[:, k4, :], P.cdr["attnb_maskT"][k4, :, :], stream="mskb")
            em.wait("vector", ("D_mskb", em.streams["D_mskb"]))
            em.barrier()
            q_tm = P.sbuf([128, 2, NTT, 64], BF16, stD)
            v_aug = P.sbuf([128, 2, NTT, 65], BF16, stD)
            qT_h = P.sbuf([64, 2, T], BF16, stD)
            kT_h = P.sbuf([64, 2, T], BF16, stD)
            kb = P.sbuf([128, 64], BF16, stD)
            Pj = P.sbuf([128, 3, 512], BF16, stD)
            rrow = P.sbuf([1, 512], F32, stD)
            acq = P.sbuf([128, RQ], F32, stD)
            ack = P.sbuf([128, 4], F32, stD)
            rb_sb = P.sbuf([64, 2, 512], F32, stD)
            ps_sc = P.psum([128, 1536], F32, stD)
            ps_av = P.psum([128, 1024], F32, stD)
            ps_rb = P.psum([128, 512], F32, stD)
            ps_t2 = P.psum([128, 2048], BF16, stD)
            sc_busy = [None] * 3
            pj_war = [None] * 3
            av_busy = [None, None]
            rb_busy = [None, None]
            t2_busy = [None, None]
            flash_done = [None, None]
            build_done = [None, None]

            def build_head(h):
                hs = h % 2
                em.wait("vector", flash_done[hs])
                em.op("vector", "memset", v_aug[:, hs, :, 64:65], 1.0)
                for tt in range(NTT):
                    aq = abq_tm[:, tt, :]
                    akv = abkv_tm[:, tt, :]
                    em.op("vector", "tensor_copy", acq[:, 0:RQ],
                          bass.AP(abq_tm, tt * 480 + h, [[480 * NTT, 128], [80, RQ]]))
                    em.op("vector", "tensor_copy", ack[:, 0:4],
                          bass.AP(abkv_tm, tt * 320 + h, [[320 * NTT, 128], [80, 4]]))
                    em.op("vector", "tensor_scalar", q_tm[:, hs, tt, :],
                          aq[:, 16:80], acq[:, 0:1], None, ALU.mult)
                    for r in range(1, RQ):
                        em.op("vector", "scalar_tensor_tensor", q_tm[:, hs, tt, :],
                              aq[:, r * 80 + 16:r * 80 + 80], acq[:, r:r + 1],
                              q_tm[:, hs, tt, :], ALU.mult, ALU.add)
                    em.op("vector", "tensor_scalar", kb[:, :],
                          akv[:, 16:80], ack[:, 0:1], None, ALU.mult)
                    em.op("vector", "scalar_tensor_tensor", kb[:, :],
                          akv[:, 96:160], ack[:, 1:2], kb[:, :], ALU.mult, ALU.add)
                    ktok = em.mark("vector")
                    em.wait("tensor", ktok)
                    em.wait("tensor", t2_busy[tt % 2])
                    em.op("tensor", "transpose",
                          ps_t2[0:64, (tt % 2) * 1024:(tt % 2) * 1024 + 128],
                          kb[:, :], P.ident_bf)
                    tt2 = em.mark("tensor")
                    em.wait("vector", tt2)
                    em.op("vector", "tensor_copy", kT_h[:, hs, tt * 128:(tt + 1) * 128],
                          ps_t2[0:64, (tt % 2) * 1024:(tt % 2) * 1024 + 128])
                    t2_busy[tt % 2] = em.mark("vector")
                    em.op("vector", "tensor_scalar", v_aug[:, hs, tt, 0:64],
                          akv[:, 176:240], ack[:, 2:3], None, ALU.mult)
                    em.op("vector", "scalar_tensor_tensor", v_aug[:, hs, tt, 0:64],
                          akv[:, 256:320], ack[:, 3:4],
                          v_aug[:, hs, tt, 0:64], ALU.mult, ALU.add)
                    qtok = em.mark("vector")
                    em.wait("tensor", qtok)
                    em.wait("tensor", t2_busy[(tt + 1) % 2])
                    em.op("tensor", "transpose",
                          ps_t2[0:64, ((tt + 1) % 2) * 1024:((tt + 1) % 2) * 1024 + 128],
                          q_tm[:, hs, tt, :], P.ident_bf)
                    tt3 = em.mark("tensor")
                    em.wait("vector", tt3)
                    em.op("vector", "tensor_copy", qT_h[:, hs, tt * 128:(tt + 1) * 128],
                          ps_t2[0:64, ((tt + 1) % 2) * 1024:((tt + 1) % 2) * 1024 + 128])
                    t2_busy[(tt + 1) % 2] = em.mark("vector")
                    yield
                build_done[hs] = em.mark("vector")

            for _ in build_head(0):
                pass
            for h in range(16):
                hs = h % 2
                gnext = build_head(h + 1) if h + 1 < 16 else None
                em.wait("tensor", build_done[hs])
                for t1b in range(T // 512):
                    if gnext is not None:
                        for _ in range(4):
                            next(gnext, None)
                    avsl = t1b % 2
                    pav = ps_av[0:65, avsl * 512:avsl * 512 + 512]
                    em.wait("tensor", av_busy[avsl])
                    njq = 4 * t1b + 4
                    DEPTH = 2

                    def emit_scores(j):
                        ssl = j % 3
                        em.wait("tensor", sc_busy[ssl])
                        em.op("tensor", "matmul", ps_sc[:, ssl * 512:ssl * 512 + 512],
                              kT_h[:, hs, j * 128:(j + 1) * 128],
                              qT_h[:, hs, t1b * 512:(t1b + 1) * 512], start=True, stop=True)
                        return em.mark("tensor")

                    def emit_exp(j, mtok):
                        ssl = j % 3
                        em.wait("scalar", mtok)
                        em.wait("scalar", pj_war[ssl])
                        em.op("scalar", "activation", Pj[:, ssl, :],
                              ps_sc[:, ssl * 512:ssl * 512 + 512], AF.Exp, scale=1.0 / 96.0)
                        etok = em.mark("scalar")
                        sc_busy[ssl] = etok
                        if j >= 4 * t1b:
                            em.wait("vector", etok)
                            em.op("vector", "tensor_tensor", Pj[:, ssl, :], Pj[:, ssl, :],
                                  maskT[:, j - 4 * t1b, :], ALU.mult)
                            return em.mark("vector")
                        return etok

                    stoks = {}
                    ptoks = {}
                    for j in range(min(DEPTH, njq)):
                        stoks[j] = emit_scores(j)
                        ptoks[j] = emit_exp(j, stoks[j])
                    for j in range(njq):
                        if j + DEPTH < njq:
                            stoks[j + DEPTH] = emit_scores(j + DEPTH)
                            ptoks[j + DEPTH] = emit_exp(j + DEPTH, stoks[j + DEPTH])
                        em.wait("tensor", ptoks[j])
                        em.op("tensor", "matmul", pav, v_aug[:, hs, j, :], Pj[:, j % 3, :],
                              start=(j == 0), stop=(j == njq - 1))
                        pj_war[j % 3] = em.mark("tensor")
                    avtok = em.mark("tensor")
                    em.wait("vector", avtok)
                    em.op("vector", "reciprocal", rrow[0:1, :], pav[64:65, :])
                    rtok = em.mark("vector")
                    em.wait("tensor", rtok)
                    em.wait("tensor", rb_busy[avsl])
                    em.op("tensor", "matmul", ps_rb[0:64, 0:512],
                          P.csb["ones_f32"][0:1, 0:64], rrow[0:1, :], start=True, stop=True)
                    rbtok = em.mark("tensor")
                    em.wait("vector", rbtok)
                    em.op("vector", "tensor_copy", rb_sb[:, avsl, :],
                          ps_rb[0:64, 0:512])
                    cptok = em.mark("vector")
                    rb_busy[avsl] = cptok
                    em.op("vector", "scalar_tensor_tensor",
                          outT_all[(h % 2) * 64:(h % 2) * 64 + 64, h // 2,
                                   t1b * 512:(t1b + 1) * 512],
                          pav[0:64, :], 0.5, rb_sb[:, avsl, :], ALU.mult, ALU.mult)
                    av_busy[avsl] = em.mark("vector")
                if gnext is not None:
                    for _ in gnext:
                        pass
                flash_done[hs] = av_busy[(T // 512 - 1) % 2]
            em.barrier()
        with contextlib.ExitStack() as stE:
            woT, _ = emit_weight_loadT(P, W["W_o"], 1024, 1024, stack=stE)
            em.barrier()
            if True:
                xres = P.sbuf([128, 2, 1024], F32, stE)
                yout = P.sbuf([128, 2, 1024], F32, stE)
                ps_y = P.psum([128, 2048], F32, stE)
                x_busy = [None, None]
                y_busy = [None, None]
                psy_busy = [None, None]
                for tt in range(NTT):
                    sl = tt % 2
                    em.wait("sync", x_busy[sl])
                    xtok = em.dma("sync", xres[:, sl, :], xin[tt * 128:(tt + 1) * 128, :],
                                  stream=f"xr{sl}")
                    em.wait("tensor", psy_busy[sl])
                    for dt in range(8):
                        for fc in range(2):
                            em.op("tensor", "matmul",
                                  ps_y[:, sl * 1024 + fc * 512:sl * 1024 + (fc + 1) * 512],
                                  outT_all[:, dt, tt * 128:(tt + 1) * 128],
                                  woT[:, dt, fc * 512:(fc + 1) * 512],
                                  start=(dt == 0), stop=(dt == 7))
                    ytok = em.mark("tensor")
                    em.wait("vector", ytok)
                    em.wait("vector", xtok)
                    em.wait("vector", y_busy[sl])
                    em.op("vector", "tensor_tensor", yout[:, sl, :], xres[:, sl, :],
                          ps_y[:, sl * 1024:(sl + 1) * 1024], ALU.add)
                    vtok = em.mark("vector")
                    psy_busy[sl] = vtok
                    x_busy[sl] = vtok
                    em.wait("sync", vtok)
                    y_busy[sl] = em.dma("sync", xout[tt * 128:(tt + 1) * 128, :],
                                        yout[:, sl, :], stream=f"yo{sl}")
                em.barrier()


PHASE_FNS["l0ab"] = ph_l0_attn_b


def ph_l1_ffn(P, T, xin, xout, W):
    em, nc = P.em, P.nc
    NE = 16
    SW = min(512, T)
    NS = T // SW
    NTT = SW // 128
    xlnT_d = nc.dram_tensor(P.name("ffn1_xln"), [8, 128, T], BF16).ap()

    with contextlib.ExitStack() as stA:
        c_sb = P.sbuf([128, T // 128, NE], F32, stA)
        with contextlib.ExitStack() as stL:
            gwT, _ = emit_weight_loadT(P, W["gate_w"], NE, 1024, stack=stL)
            stats = P.sbuf([128, 8], F32, stL)
            prob = P.sbuf([128, 2, NE], F32, stL)
            gsb = P.sbuf([128, 8], F32, stL)
            ps_l = P.psum([128, 1024], F32, stL)
            psl_busy = [None, None]
            em.barrier()
            xlnT = P.sbuf([128, 8, 512], BF16, stL)
            nblk = T // 512
            for b in range(nblk):
                emit_ln(P, xin, b * 512, (b + 1) * 512, xlnT, 0)
                for d in range(8):
                    em.dma("sync", xlnT_d[d, :, b * 512:(b + 1) * 512], xlnT[:, d, :],
                           stream="xlw")
                for it in range(4):
                    tt = b * 4 + it
                    psl = ps_l[:, (it % 2) * 512:(it % 2) * 512 + NE]
                    em.wait("tensor", psl_busy[it % 2])
                    for d in range(8):
                        em.op("tensor", "matmul", psl, xlnT[:, d, it * 128:(it + 1) * 128],
                              gwT[:, d, :], start=(d == 0), stop=(d == 7))
                    mtok = em.mark("tensor")
                    pr = prob[:, it % 2, :]
                    m1 = stats[:, 0:1]
                    den = stats[:, 1:2]
                    rden = stats[:, 2:3]
                    mg = stats[:, 3:4]
                    thr = stats[:, 4:5]
                    gs = gsb[:, 0:4]
                    gmask = gsb[:, 4:8]
                    tmp16 = prob[:, (it + 1) % 2, :]
                    em.wait("vector", mtok)
                    em.op("vector", "reduce_max", m1, psl, AX)
                    em.op("vector", "tensor_scalar", m1, m1, -1.0, None, ALU.mult)
                    em.wait("scalar", em.mark("vector"))
                    em.op("scalar", "activation", pr, psl, AF.Exp, bias=m1, accum_out=den)
                    stok = em.mark("scalar")
                    psl_busy[it % 2] = stok
                    em.wait("vector", stok)
                    em.op("vector", "reciprocal", rden, den)
                    em.op("vector", "tensor_scalar", pr, pr, rden, None, ALU.mult)
                    em.op("vector", "reduce_max", gs,
                          pr.rearrange("p (g q) -> p g q", g=4), AX)
                    em.op("vector", "reduce_max", mg, gs, AX)
                    em.op("vector", "tensor_scalar", gmask, gs, mg, None, ALU.is_ge)
                    em.op("vector", "scalar_tensor_tensor", gmask, gmask, -1e9, gs,
                          ALU.mult, ALU.add)
                    em.op("vector", "reduce_max", mg, gmask, AX)
                    em.op("vector", "tensor_scalar", gmask, gs, mg, None, ALU.is_ge)
                    em.op("vector", "tensor_tensor",
                          tmp16.rearrange("p (g q) -> p g q", g=4),
                          pr.rearrange("p (g q) -> p g q", g=4),
                          bass.AP(gsb, 4, [[8, 128], [1, 4], [0, 4]]), ALU.mult)
                    c_out = c_sb[:, tt, :]
                    em.op("vector", "tensor_copy", c_out, tmp16)
                    for _ in range(3):
                        em.op("vector", "reduce_max", thr, tmp16, AX)
                        em.op("vector", "tensor_scalar", tmp16,
                              tmp16, thr, None, ALU.is_ge, accum_out=None)
                        em.op("vector", "scalar_tensor_tensor", tmp16, tmp16, -1e9,
                              c_out, ALU.mult, ALU.add)
                        em.op("vector", "tensor_copy", c_out, tmp16)
                    em.op("vector", "reduce_max", thr, tmp16, AX)
                    em.op("vector", "tensor_tensor",
                          tmp16.rearrange("p (g q) -> p g q", g=4),
                          pr.rearrange("p (g q) -> p g q", g=4),
                          bass.AP(gsb, 4, [[8, 128], [1, 4], [0, 4]]), ALU.mult)
                    em.op("vector", "tensor_scalar", tmp16, tmp16, thr, None, ALU.is_ge)
                    em.op("vector", "tensor_tensor", c_sb[:, tt, :], tmp16, pr, ALU.mult)
                    em.mark("vector")
            em.barrier()
        NP = max(1, NS // 2)
        a_sb = P.sbuf([128, 8, 2, SW], BF16, stA)
        sg = P.sbuf([128, 512], BF16, stA)
        xlt = P.sbuf([128, NS, 8, SW], BF16, stA)
        xti = P.sbuf([128, 2, 1024], F32, stA)
        ystg = P.sbuf([128, 2, 1024], F32, stA)
        guA = P.sbuf([128, 8, 2048], BF16, stA)
        wdA = P.sbuf([128, 8, 1024], BF16, stA)
        ps_all = P.psum([128, 4096], F32, stA)

        def load_w(e):
            if e == NE:
                inter = 1024
                for d in range(8):
                    em.dma("gpsimd", guA[:, d, 0:1024], W["ws1"][d * 128:(d + 1) * 128, :],
                           stream="gu")
                    em.dma("gpsimd", guA[:, d, 1024:2048], W["ws3"][d * 128:(d + 1) * 128, :],
                           stream="gu")
            else:
                inter = 512
                for d in range(8):
                    em.dma("gpsimd", guA[:, d, 0:512], W["w1"][e, d * 128:(d + 1) * 128, :],
                           stream="gu")
                    em.dma("gpsimd", guA[:, d, 512:1024], W["w3"][e, d * 128:(d + 1) * 128, :],
                           stream="gu")
            return inter, ("D_gu", em.streams["D_gu"])

        def load_wd2(e):
            if e == NE:
                for j in range(8):
                    em.dma("gpsimd", wdA[:, j, :], W["ws2"][j * 128:(j + 1) * 128, :],
                           stream="wd")
            else:
                for j in range(4):
                    em.dma("gpsimd", wdA[:, j, :], W["w2"][e, j * 128:(j + 1) * 128, :],
                           stream="wd")
            return ("D_wd", em.streams["D_wd"])

        order = [NE] + list(range(NE))
        inter, gu_tok = load_w(order[0])
        wd_tok = load_wd2(order[0])
        for s0 in range(NS):
            for d in range(8):
                em.dma("sync", xlt[:, s0, d, :],
                       xlnT_d[d, :, s0 * SW:(s0 + 1) * SW], stream="xla1")
        xl_tok = ("D_xla1", em.streams["D_xla1"])
        xt_busy = [None, None]
        y_busy = [None, None]
        bank_busy = {}
        a_busy = None
        sgw = None
        for ei, e in enumerate(order):
            nj = inter // 128
            uoff = inter
            em.wait("tensor", gu_tok)
            em.wait("tensor", xl_tok)
            for sp in range(NP):
                s0 = sp * 2
                sweeps = [s0] if NS == 1 else [s0, s0 + 1]
                em.wait("tensor", a_busy)
                for j in range(nj):
                    jp = j % 2
                    gbase = jp * 1024
                    ubase = 2048 + jp * 1024
                    for si in range(len(sweeps)):
                        em.wait("tensor", bank_busy.get(("g", jp, si)))
                        em.wait("tensor", bank_busy.get(("u", jp, si)))
                    for d in range(8):
                        for si, s in enumerate(sweeps):
                            em.op("tensor", "matmul",
                                  ps_all[:, gbase + si * 512:gbase + (si + 1) * 512],
                                  guA[:, d, j * 128:(j + 1) * 128],
                                  xlt[:, s, d, :], start=(d == 0), stop=(d == 7))
                    for d in range(8):
                        for si, s in enumerate(sweeps):
                            em.op("tensor", "matmul",
                                  ps_all[:, ubase + si * 512:ubase + (si + 1) * 512],
                                  guA[:, d, uoff + j * 128:uoff + (j + 1) * 128],
                                  xlt[:, s, d, :], start=(d == 0), stop=(d == 7))
                    mtok = em.mark("tensor")
                    for si, s in enumerate(sweeps):
                        em.wait("scalar", mtok)
                        em.wait("scalar", sgw)
                        em.op("scalar", "activation", sg[:, :],
                              ps_all[:, gbase + si * 512:gbase + (si + 1) * 512], AF.Silu)
                        stk = em.mark("scalar")
                        bank_busy[("g", jp, si)] = stk
                        em.wait("vector", stk)
                        em.op("vector", "tensor_tensor", a_sb[:, j, si, :], sg[:, :],
                              ps_all[:, ubase + si * 512:ubase + (si + 1) * 512],
                              ALU.mult)
                        vtk = em.mark("vector")
                        bank_busy[("u", jp, si)] = vtk
                        sgw = vtk
                a_ready = sgw
                if sp == 0:
                    em.wait("tensor", wd_tok)
                if sp == NP - 1 and ei + 1 < len(order):
                    em.wait("gpsimd", mtok)
                    inter_next, gu_tok = load_w(order[ei + 1])
                else:
                    inter_next = None
                em.wait("tensor", a_ready)
                for si, s in enumerate(sweeps):
                    for tt in range(NTT):
                        gtt = s * NTT + tt
                        ysl2 = (si * NTT + tt) % 2
                        py = ps_all[:, ysl2 * 1024:(ysl2 + 1) * 1024]
                        em.wait("tensor", bank_busy.get(("y", ysl2)))
                        for jpar in range(2):
                            em.wait("tensor", bank_busy.get(("g", jpar, 0)))
                            em.wait("tensor", bank_busy.get(("g", jpar, 1)))
                        for j in range(nj):
                            for fc in range(2):
                                em.op("tensor", "matmul",
                                      py[:, fc * 512:(fc + 1) * 512],
                                      a_sb[:, j, si, tt * 128:(tt + 1) * 128],
                                      wdA[:, j, fc * 512:(fc + 1) * 512],
                                      start=(j == 0), stop=(j == nj - 1))
                        ytok = em.mark("tensor")
                        ysl = gtt % 2
                        em.wait("vector", ytok)
                        em.wait("vector", y_busy[ysl])
                        if e == NE:
                            em.wait("sync", xt_busy[ysl])
                            xttok = em.dma("sync", xti[:, ysl, :],
                                           xin[gtt * 128:(gtt + 1) * 128, :],
                                           stream=f"xt{ysl}")
                            em.wait("vector", xttok)
                            em.op("vector", "tensor_tensor", ystg[:, ysl, :],
                                  xti[:, ysl, :], py, ALU.add)
                            xt_busy[ysl] = em.mark("vector")
                            wtok = em.mark("vector")
                        else:
                            em.op("vector", "tensor_scalar", ystg[:, ysl, :], py,
                                  c_sb[:, gtt, e:e + 1], None, ALU.mult)
                            wtok = em.mark("vector")
                        bank_busy[("y", ysl2)] = wtok
                        bank_busy[("g", ysl2, 0)] = wtok
                        bank_busy[("g", ysl2, 1)] = wtok
                        em.wait("gpsimd", wtok)
                        ystream = "ya%d_%d" % (s, ysl)
                        if tt < 2:
                            for st2 in ("ya%d_0" % s, "ya%d_1" % s):
                                if "D_" + st2 in em.streams:
                                    em.wait("gpsimd", ("D_" + st2, em.streams["D_" + st2]))
                        if e == NE:
                            y_busy[ysl] = em.dma("gpsimd", xout[gtt * 128:(gtt + 1) * 128, :],
                                                 ystg[:, ysl, :], stream=ystream)
                        else:
                            y_busy[ysl] = em.dma("gpsimd", xout[gtt * 128:(gtt + 1) * 128, :],
                                                 ystg[:, ysl, :], stream=ystream,
                                                 accum_op=ALU.add)
                    a_busy = ytok
            if inter_next is not None:
                inter = inter_next
            if ei + 1 < len(order):
                em.wait("gpsimd", a_busy)
                wd_tok = load_wd2(order[ei + 1])
        em.barrier()


PHASE_FNS["l1ffn"] = ph_l1_ffn


def ph_l1_attn_b(P, T, xin, xout, W):
    em, nc = P.em, P.nc
    NTT = T // 128
    W_L = 0.9 ** 128
    xb_d = nc.dram_tensor(P.name("l1bh0_"), [T, 1024], F32).ap()
    if DBG.get("l1ab"):
        dbg_kk = nc.dram_tensor("dbg_kk", [T, 1024], BF16, kind="ExternalOutput").ap()
        dbg_v = nc.dram_tensor("dbg_v", [T, 1024], BF16, kind="ExternalOutput").ap()
        dbg_r = nc.dram_tensor("dbg_r", [T, 1024], BF16, kind="ExternalOutput").ap()
        dbg_y = nc.dram_tensor("dbg_y", [T, 1024], BF16, kind="ExternalOutput").ap()
    xb1_d = nc.dram_tensor(P.name("l1bout_"), [T, 1024], F32).ap()
    xr2_d = nc.dram_tensor(P.name("l1brr_"), [8, 128, T], BF16).ap()

    emit_ln(P, xin, 0, T, None, 0, out_dram=xb_d)

    with contextlib.ExitStack() as stA:
        w1col, _ = emit_load_cols(P, W["norm1_w"], 1024, stack=stA)
        WrA = P.sbuf([128, 8, 1024], BF16, stA)
        WkA = P.sbuf([128, 8, 1024], F32, stA)
        WvA = P.sbuf([128, 8, 1024], BF16, stA)
        WoA = P.sbuf([128, 8, 1024], BF16, stA)
        for d in range(8):
            em.dma("gpsimd", WrA[:, d, :], W["Wr"][d * 128:(d + 1) * 128, :], stream="wls")
            em.dma("sync", WkA[:, d, :], W["Wk"][d * 128:(d + 1) * 128, :], stream="wlsf")
            em.dma("gpsimd", WvA[:, d, :], W["Wv"][d * 128:(d + 1) * 128, :], stream="wls")
            em.dma("gpsimd", WoA[:, d, :], W["Wo"][d * 128:(d + 1) * 128, :], stream="wls")
        em.wait("vector", ("D_wls", em.streams["D_wls"]))
        em.wait("vector", ("D_wlsf", em.streams["D_wlsf"]))
        em.barrier()
        for Wx in (WrA, WkA, WvA):
            for d in range(8):
                em.op("vector", "tensor_scalar", Wx[:, d, :], Wx[:, d, :],
                      w1col[:, d:d + 1], None, ALU.mult)
        em.mark("vector")
        em.barrier()
        mskT = P.sbuf([128, 128], F32, stA)
        winb = P.sbuf([128, 128], F32, stA)
        woutc = P.sbuf([128, 1], F32, stA)
        em.dma("sync", mskT[:, :], P.cdr["l1b_maskT"][:, :], stream="mskc")
        em.dma("sync", winb[:, :], P.cdr["l1b_w_in"][:, :], stream="mskc")
        em.dma("sync", woutc[:, :], P.cdr["l1b_w_out"][:, :], stream="mskc")
        for e in ("tensor", "vector", "scalar"):
            em.wait(e, ("D_mskc", em.streams["D_mskc"]))
        S_sb = P.sbuf([64, 16, 64], BF16, stA)
        h0t = P.sbuf([128, 2, 1024], F32, stA)
        stats = P.sbuf([128, 8], F32, stA)
        sq = P.sbuf([128, 1024], F32, stA)
        xr = P.sbuf([128, 1024], BF16, stA)
        xr32 = P.sbuf([128, 1024], F32, stA)
        xrT = P.sbuf([128, 8, 128], BF16, stA)
        xrT32 = P.sbuf([128, 8, 128], F32, stA)
        rT = P.sbuf([128, 8, 128], BF16, stA)
        kkf = P.sbuf([128, 1024], F32, stA)
        cur = P.sbuf([128, 1024], F32, stA)
        thr = P.sbuf([128, 16], F32, stA)
        kk = P.sbuf([128, 1024], BF16, stA)
        kkT = P.sbuf([128, 8, 128], BF16, stA)
        v_tm = P.sbuf([128, 1024], BF16, stA)
        atm = P.sbuf([128, 2, 128], BF16, stA)
        rw = P.sbuf([64, 2, 128], BF16, stA)
        kw = P.sbuf([128, 2, 64], BF16, stA)
        yT = P.sbuf([128, 8, 128], BF16, stA)
        yo = P.sbuf([128, 2, 1024], F32, stA)
        psA = P.psum([128, 1024], F32, stA)
        psB = P.psum([128, 2048], F32, stA)
        ps_t = P.psum([128, 2048], BF16, stA)
        h_busy = [None, None]
        yo_busy = [None, None]
        pst_busy = [None, None]
        psA_free = None
        at_busy = {}
        atm_busy = {}
        y_busy = {}
        ds_busy = {}
        wo_evac = None
        prev_head_tok = None
        for ci in range(NTT):
            sl = ci % 2
            em.wait("sync", h_busy[sl])
            htok = em.dma("sync", h0t[:, sl, :], xb_d[ci * 128:(ci + 1) * 128, :],
                          stream=f"h0{sl}")
            h0 = h0t[:, sl, :]
            em.wait("scalar", htok)
            em.wait("scalar", prev_head_tok)
            em.op("scalar", "activation", sq[:, :], h0, AF.Square)
            sqtok = em.mark("scalar")
            msq = stats[:, 0:1]
            lnv = stats[:, 1:2]
            rstd = stats[:, 2:3]
            em.wait("vector", sqtok)
            em.op("vector", "reduce_sum", msq, sq[:, :], AX)
            em.op("vector", "tensor_scalar", msq, msq, 1.0 / 1024.0, 1e-5, ALU.mult,
                  ALU.add)
            vt = em.mark("vector")
            em.wait("scalar", vt)
            em.op("scalar", "activation", lnv, msq, AF.Ln)
            em.op("scalar", "activation", rstd, lnv, AF.Exp, scale=-0.5)
            stok = em.mark("scalar")
            em.wait("vector", stok)
            em.wait("vector", htok)
            em.op("vector", "tensor_scalar", xr32[:, :], h0, rstd, None, ALU.mult)
            em.op("vector", "tensor_copy", xr[:, :], xr32[:, :])
            xrtok = em.mark("vector")
            for d in range(8):
                slp = d % 2
                em.wait("tensor", xrtok)
                em.wait("tensor", pst_busy[slp])
                em.op("tensor", "transpose", ps_t[:, slp * 1024:slp * 1024 + 128],
                      xr[:, d * 128:(d + 1) * 128], P.ident_bf)
                tk = em.mark("tensor")
                em.wait("vector", tk)
                em.op("vector", "tensor_copy", xrT[:, d, :],
                      ps_t[:, slp * 1024:slp * 1024 + 128])
                pst_busy[slp] = em.mark("vector")
            xrT_tok = pst_busy[1]
            em.wait("tensor", pst_busy[0])
            em.wait("tensor", pst_busy[1])
            em.wait("tensor", psA_free)
            for h2 in ds_busy.values():
                em.wait("tensor", h2)
            ps32_busy = [None, None]
            for d in range(8):
                slp = d % 2
                em.wait("tensor", xrtok)
                em.wait("tensor", ps32_busy[slp])
                em.op("tensor", "transpose", psA[:, slp * 512:slp * 512 + 128],
                      xr32[:, d * 128:(d + 1) * 128], P.ident_f32)
                tk = em.mark("tensor")
                em.wait("vector", tk)
                em.op("vector", "tensor_copy", xrT32[:, d, :],
                      psA[:, slp * 512:slp * 512 + 128])
                ps32_busy[slp] = em.mark("vector")
            em.wait("tensor", ps32_busy[0])
            em.wait("tensor", ps32_busy[1])
            rsig = None
            for ft in range(8):
                slp = ft % 2
                em.wait("tensor", rsig_busy[slp] if False else None)
                for d in range(8):
                    em.op("tensor", "matmul", psA[:, slp * 512:slp * 512 + 128],
                          WrA[:, d, ft * 128:(ft + 1) * 128], xrT[:, d, :],
                          start=(d == 0), stop=(d == 7))
                mt = em.mark("tensor")
                if ft >= 2:
                    pass
                em.wait("scalar", mt)
                em.wait("scalar", rsig)
                em.op("scalar", "activation", rT[:, ft, :],
                      psA[:, slp * 512:slp * 512 + 128], AF.Sigmoid)
                rsig = em.mark("scalar")
                if ft < 6:
                    em.wait("tensor", rsig)
            em.wait("tensor", rsig)
            for d in range(8):
                for fc in range(2):
                    em.op("tensor", "matmul", psA[:, fc * 512:(fc + 1) * 512],
                          xrT32[:, d, :], WkA[:, d, fc * 512:(fc + 1) * 512],
                          start=(d == 0), stop=(d == 7))
            ktok = em.mark("tensor")
            kps = psA[:, :]
            em.wait("vector", ktok)
            em.op("vector", "tensor_scalar", kkf[:, :], kps, 0.5, None, ALU.is_gt)
            em.op("vector", "tensor_tensor", kkf[:, :], kkf[:, :], kps, ALU.mult)
            em.op("vector", "tensor_copy", cur[:, :], kkf[:, :])
            kv = kkf[:, :].rearrange("p (h q) -> p h q", h=16)
            cv = cur[:, :].rearrange("p (h q) -> p h q", h=16)
            for it in range(3):
                em.op("vector", "reduce_max", thr[:, :], cv, AX)
                em.op("vector", "tensor_tensor", sq[:, :].rearrange("p (h q) -> p h q", h=16),
                      cv, bass.AP(thr, 0, [[16, 128], [1, 16], [0, 64]]), ALU.is_ge)
                em.op("vector", "scalar_tensor_tensor", cur[:, :], sq[:, :], -1e9,
                      cur[:, :], ALU.mult, ALU.add)
            em.op("vector", "reduce_max", thr[:, :], cv, AX)
            em.op("vector", "tensor_tensor", sq[:, :].rearrange("p (h q) -> p h q", h=16),
                  kv, bass.AP(thr, 0, [[16, 128], [1, 16], [0, 64]]), ALU.is_ge)
            em.op("vector", "tensor_tensor", kk[:, :], sq[:, :], kkf[:, :], ALU.mult)
            kk_tok = em.mark("vector")
            em.wait("tensor", kk_tok)
            for d in range(8):
                for fc in range(2):
                    em.op("tensor", "matmul", psA[:, fc * 512:(fc + 1) * 512],
                          xrT[:, d, :], WvA[:, d, fc * 512:(fc + 1) * 512],
                          start=(d == 0), stop=(d == 7))
            vtok = em.mark("tensor")
            em.wait("vector", vtok)
            em.op("vector", "tensor_copy", v_tm[:, :], psA[:, :])
            v_tok = em.mark("vector")
            psA_free = v_tok
            for d in range(8):
                slp = d % 2
                em.wait("tensor", kk_tok)
                em.wait("tensor", pst_busy[slp])
                em.op("tensor", "transpose", ps_t[:, slp * 1024:slp * 1024 + 128],
                      kk[:, d * 128:(d + 1) * 128], P.ident_bf)
                tk = em.mark("tensor")
                em.wait("vector", tk)
                em.op("vector", "tensor_copy", kkT[:, d, :],
                      ps_t[:, slp * 1024:slp * 1024 + 128])
                pst_busy[slp] = em.mark("vector")
            em.wait("tensor", pst_busy[0])
            em.wait("tensor", pst_busy[1])
            em.wait("tensor", v_tok)
            ds_busy = {}
            for h in range(16):
                hp = (h % 2) * 64
                ht = h // 2
                hsl = h % 2
                rcT = rT[hp:hp + 64, ht, :]
                kcT = kkT[hp:hp + 64, ht, :]
                vh = v_tm[:, h * 64:(h + 1) * 64]
                kh = kk[:, h * 64:(h + 1) * 64]
                pat = psB[:, hsl * 512:hsl * 512 + 128]
                em.wait("tensor", atm_busy.get(hsl))
                em.op("tensor", "matmul", pat, kcT, rcT, start=True, stop=True)
                at_tok = em.mark("tensor")
                em.wait("vector", at_tok)
                em.op("vector", "tensor_tensor", atm[:, hsl, :], pat, mskT[:, :], ALU.mult)
                em.op("vector", "tensor_tensor", rw[:, hsl, :], rcT, winb[hp:hp + 64, :], ALU.mult)
                em.op("vector", "tensor_scalar", kw[:, hsl, :], kh, woutc[:, 0:1], None,
                      ALU.mult)
                a_tok = em.mark("vector")
                pyh = psB[:, 1024 + hsl * 512:1024 + hsl * 512 + 128]
                em.wait("tensor", a_tok)
                em.wait("tensor", y_busy.get(hsl))
                em.wait("tensor", wo_evac)
                em.op("tensor", "matmul", pyh[0:64, :], vh, atm[:, hsl, :],
                      start=True, stop=(ci == 0))
                if ci > 0:
                    em.op("tensor", "matmul", pyh[0:64, :], S_sb[:, h, :], rw[:, hsl, :],
                          start=False, stop=True)
                y_tok = em.mark("tensor")
                atm_busy[hsl] = y_tok
                em.wait("vector", y_tok)
                em.op("vector", "tensor_copy", yT[hp:hp + 64, ht, :], pyh[0:64, :])
                yc_tok = em.mark("vector")
                y_busy[hsl] = yc_tok
                pds = psA[0:64, hsl * 512:hsl * 512 + 64]
                em.wait("tensor", ds_busy.get(hsl))
                em.wait("tensor", psA_free)
                em.op("tensor", "matmul", pds, kw[:, hsl, :], vh,
                      start=True, stop=True)
                ds_tok = em.mark("tensor")
                em.wait("vector", ds_tok)
                if ci == 0:
                    em.op("vector", "tensor_copy", S_sb[:, h, :], pds)
                else:
                    em.op("vector", "scalar_tensor_tensor", S_sb[:, h, :],
                          S_sb[:, h, :], W_L, pds, ALU.mult, ALU.add)
                ds_busy[hsl] = em.mark("vector")
            head_tok = em.mark("vector")
            prev_head_tok = head_tok
            if DBG.get("l1ab"):
                em.wait("sync", kk_tok)
                em.wait("sync", v_tok)
                em.wait("sync", head_tok)
                d1 = em.dma("sync", dbg_kk[ci * 128:(ci + 1) * 128, :], kk[:, :], stream="dbg1")
                d2 = em.dma("sync", dbg_v[ci * 128:(ci + 1) * 128, :], v_tm[:, :], stream="dbg1")
                for dd in range(8):
                    d3 = em.dma("sync", dbg_r[ci * 128:(ci + 1) * 128, dd * 128:(dd + 1) * 128].rearrange("a b -> b a"), rT[:, dd, :], stream="dbg1", allow_slow_non_contiguous=True)
                    d4 = em.dma("sync", dbg_y[ci * 128:(ci + 1) * 128, dd * 128:(dd + 1) * 128].rearrange("a b -> b a"), yT[:, dd, :], stream="dbg1", allow_slow_non_contiguous=True)
                for e2 in ("vector", "scalar", "tensor"):
                    em.wait(e2, ("D_dbg1", em.streams["D_dbg1"]))
            em.wait("tensor", head_tok)
            for dt in range(8):
                for fc in range(2):
                    em.op("tensor", "matmul",
                          psB[:, 1024 + fc * 512:1024 + (fc + 1) * 512],
                          yT[:, dt, :], WoA[:, dt, fc * 512:(fc + 1) * 512],
                          start=(dt == 0), stop=(dt == 7))
            wo_tok = em.mark("tensor")
            wsl = ci % 2
            em.wait("vector", wo_tok)
            em.wait("vector", yo_busy[wsl])
            em.op("vector", "tensor_tensor", yo[:, wsl, :], h0,
                  psB[:, 1024:2048], ALU.add)
            yov = em.mark("vector")
            wo_evac = yov
            y_busy = {}
            h_busy[sl] = yov
            em.wait("sync", yov)
            yo_busy[wsl] = em.dma("sync", xb1_d[ci * 128:(ci + 1) * 128, :],
                                  yo[:, wsl, :], stream=f"x1{wsl}")
        em.barrier()

    with contextlib.ExitStack() as stB:
        xlnT = P.sbuf([128, 8, 512], BF16, stB)
        for b in range(T // 512):
            emit_ln(P, xb1_d, b * 512, (b + 1) * 512, xlnT, 0, rms=True)
            for d in range(8):
                em.dma("sync", xr2_d[d, :, b * 512:(b + 1) * 512], xlnT[:, d, :],
                       stream="xlw2")
        em.barrier()

    with contextlib.ExitStack() as stC:
        w2col, _ = emit_load_cols(P, W["norm2_w"], 1024, stack=stC)
        NJ = 32
        SW = min(1024, T)
        NQ = T // SW
        a_sb = P.sbuf([128, NJ, SW], BF16, stC)
        w12A = P.sbuf([128, 2, 8, 2, 1024], BF16, stC)
        w3A = P.sbuf([128, NJ, 512], BF16, stC)
        xlt = P.sbuf([128, 8, SW], BF16, stC)
        sgb = P.sbuf([128, 512], BF16, stC)
        x0t = P.sbuf([128, 2, 1024], F32, stC)
        x1t = P.sbuf([128, 2, 1024], F32, stC)
        ystg = P.sbuf([128, 2, 1024], F32, stC)
        ps_g = P.psum([128, 1024], F32, stC)
        ps_u = P.psum([128, 1024], F32, stC)
        ps_y = P.psum([128, 2048], F32, stC)
        psg_busy = [None, None]
        psu_busy = [None, None]
        psy_busy = {}
        sgw = None
        x0_busy = [None, None]
        x1_busy = [None, None]
        yo_busy = [None, None]
        a_busy = None
        xl_busy = None
        w12_fold = [None, None]
        fold_done = [None, None]
        w3_busy = None
        for q in range(NQ):
            em.wait("sync", xl_busy)
            for d in range(8):
                em.dma("sync", xlt[:, d, :], xr2_d[d, :, q * SW:(q + 1) * SW], stream="xl2")
            xltok = ("D_xl2", em.streams["D_xl2"])
            em.wait("tensor", xltok)
            em.wait("tensor", a_busy)
            def load_w12(fh2, slot):
                for d in range(8):
                    em.dma("gpsimd", w12A[:, slot, d, 0, :],
                           W["w1"][d * 128:(d + 1) * 128, fh2 * 1024:(fh2 + 1) * 1024],
                           stream=f"w12{slot}")
                    em.dma("gpsimd", w12A[:, slot, d, 1, :],
                           W["w2"][d * 128:(d + 1) * 128, fh2 * 1024:(fh2 + 1) * 1024],
                           stream=f"w12{slot}b")
                return (("D_w12" + str(slot), em.streams["D_w12" + str(slot)]),
                        ("D_w12" + str(slot) + "b", em.streams["D_w12" + str(slot) + "b"]))

            def fold_w12(slot, wtoks):
                em.wait("vector", wtoks[0])
                em.wait("vector", wtoks[1])
                for d in range(8):
                    for h2 in range(2):
                        em.op("vector", "tensor_scalar", w12A[:, slot, d, h2, :],
                              w12A[:, slot, d, h2, :], w2col[:, d:d + 1], None, ALU.mult)
                        yield
                fold_done[slot] = em.mark("vector")

            if q == 0:
                for _ in fold_w12(0, load_w12(0, 0)):
                    pass
            for fh in range(4):
                wsl2 = fh % 2
                em.wait("tensor", fold_done[wsl2])
                nxt = fh + 1 if fh + 1 < 4 else (0 if q + 1 < NQ else None)
                pend_fold = None
                if nxt is not None:
                    osl = (fh + 1) % 2
                    em.wait("gpsimd", w12_fold[osl])
                    pend_fold = fold_w12(osl, load_w12(nxt, osl))
                for jj in range(8):
                    j = fh * 8 + jj
                    for tc in range(SW // 512):
                        gslot = (jj * (SW // 512) + tc) % 2
                        pg = ps_g[:, gslot * 512:gslot * 512 + 512]
                        pu = ps_u[:, gslot * 512:gslot * 512 + 512]
                        em.wait("tensor", psg_busy[gslot])
                        em.wait("tensor", psu_busy[gslot])
                        for d in range(8):
                            em.op("tensor", "matmul", pg[:, :],
                                  w12A[:, wsl2, d, 0, jj * 128:(jj + 1) * 128],
                                  xlt[:, d, tc * 512:(tc + 1) * 512],
                                  start=(d == 0), stop=(d == 7))
                        for d in range(8):
                            em.op("tensor", "matmul", pu[:, :],
                                  w12A[:, wsl2, d, 1, jj * 128:(jj + 1) * 128],
                                  xlt[:, d, tc * 512:(tc + 1) * 512],
                                  start=(d == 0), stop=(d == 7))
                        mt = em.mark("tensor")
                        em.wait("scalar", mt)
                        em.wait("scalar", sgw)
                        em.op("scalar", "activation", sgb[:, :], pg, AF.Silu)
                        psg_busy[gslot] = em.mark("scalar")
                        em.wait("vector", psg_busy[gslot])
                        em.op("vector", "tensor_tensor",
                              a_sb[:, j, tc * 512:(tc + 1) * 512], sgb[:, :], pu,
                              ALU.mult)
                        psu_busy[gslot] = em.mark("vector")
                        sgw = psu_busy[gslot]
                    if pend_fold is not None and jj >= 2:
                        for _ in range(3):
                            next(pend_fold, None)
                if pend_fold is not None:
                    for _ in pend_fold:
                        pass
                w12_fold[wsl2] = mt
            a_ready = sgw
            em.wait("tensor", a_ready)
            for oh in range(2):
                em.wait("gpsimd", w3_busy)
                for j in range(NJ):
                    em.dma("gpsimd", w3A[:, j, :],
                           W["w3"][j * 128:(j + 1) * 128, oh * 512:(oh + 1) * 512],
                           stream="w3")
                w3tok = ("D_w3", em.streams["D_w3"])
                em.wait("tensor", w3tok)
                for tt in range(SW // 128):
                    gtt = q * (SW // 128) + tt
                    ysl = tt % 2
                    py = ps_y[:, ysl * 1024 + oh * 512:ysl * 1024 + (oh + 1) * 512]
                    em.wait("tensor", psy_busy.get((ysl, oh)))
                    for j in range(NJ):
                        em.op("tensor", "matmul", py,
                              a_sb[:, j, tt * 128:(tt + 1) * 128],
                              w3A[:, j, :], start=(j == 0), stop=(j == NJ - 1))
                    ytok = em.mark("tensor")
                    em.wait("sync", x0_busy[ysl])
                    x0tok = em.dma("sync", x0t[:, ysl, oh * 512:(oh + 1) * 512],
                                   xin[gtt * 128:(gtt + 1) * 128, oh * 512:(oh + 1) * 512],
                                   stream=f"xr{ysl}")
                    em.wait("sync", x1_busy[ysl])
                    x1tok = em.dma("sync", x1t[:, ysl, oh * 512:(oh + 1) * 512],
                                   xb1_d[gtt * 128:(gtt + 1) * 128, oh * 512:(oh + 1) * 512],
                                   stream=f"x1b{ysl}")
                    em.wait("vector", ytok)
                    em.wait("vector", x0tok)
                    em.wait("vector", x1tok)
                    em.wait("vector", yo_busy[ysl])
                    em.op("vector", "tensor_tensor", ystg[:, ysl, oh * 512:(oh + 1) * 512],
                          x1t[:, ysl, oh * 512:(oh + 1) * 512], py, ALU.add)
                    em.op("vector", "tensor_tensor", ystg[:, ysl, oh * 512:(oh + 1) * 512],
                          ystg[:, ysl, oh * 512:(oh + 1) * 512],
                          x0t[:, ysl, oh * 512:(oh + 1) * 512], ALU.add)
                    ev = em.mark("vector")
                    psy_busy[(ysl, oh)] = ev
                    x0_busy[ysl] = ev
                    x1_busy[ysl] = ev
                    em.wait("sync", ev)
                    yo_busy[ysl] = em.dma("sync",
                                          xout[gtt * 128:(gtt + 1) * 128, oh * 512:(oh + 1) * 512],
                                          ystg[:, ysl, oh * 512:(oh + 1) * 512],
                                          stream=f"yo{ysl}")
                w3_busy = ytok
            a_busy = ytok
            xl_busy = a_ready
        em.barrier()


PHASE_FNS["l1ab"] = ph_l1_attn_b


def ph_l1_attn_a(P, T, xin, xout, W):
    em, nc = P.em, P.nc
    HALF_PI = PI / 2.0
    INV2PI = 1.0 / TWO_PI
    QW = min(1024, T)
    NQ = T // QW
    NTTQ = QW // 128

    if DBG.get("l1aa"):
        dbg_zc = nc.dram_tensor("dbg_zc", [17, 128, T], BF16, kind="ExternalOutput").ap()
        dbg_pw = nc.dram_tensor("dbg_pw", [32, T], F32, kind="ExternalOutput").ap()
        dbg_den = nc.dram_tensor("dbg_den", [32, T], F32, kind="ExternalOutput").ap()
        dbg_cre = nc.dram_tensor("dbg_cre", [128, T], BF16, kind="ExternalOutput").ap()
        dbg_mrg = nc.dram_tensor("dbg_mrg", [128, T], BF16, kind="ExternalOutput").ap()
        dbg_yr = nc.dram_tensor("dbg_yr", [T, 1024], BF16, kind="ExternalOutput").ap()
        dbg_s = nc.dram_tensor("dbg_s", [32, T], F32, kind="ExternalOutput").ap()
    with contextlib.ExitStack() as stA:
        winA = P.sbuf([128, 8, 2080], BF16, stA)
        for d in range(8):
            em.dma("gpsimd", winA[:, d, :], W["W_in"][d * 128:(d + 1) * 128, :], stream="wia")
        WoutA = P.sbuf([128, 8, 1024], BF16, stA)
        for d in range(8):
            em.dma("gpsimd", WoutA[:, d, :], W["W_out"][d * 128:(d + 1) * 128, :], stream="wia")
        wre = P.sbuf([128, 32], BF16, stA)
        wim = P.sbuf([128, 32], BF16, stA)
        em.dma("gpsimd", wre[:, :], W["W_re"][:, :], stream="wia")
        em.dma("gpsimd", wim[:, :], W["W_im"][:, :], stream="wia")
        e3b = P.sbuf([128, 1024], BF16, stA)
        em.dma("gpsimd", e3b[:, :], P.cdr["l1a_e3"][:, :], stream="wia")
        dvec = P.sbuf([32, 1024], F32, stA)
        bconv_col, _ = emit_load_cols(P, W["b_conv"], 2080, stack=stA)
        ns_col, _ = emit_load_cols(P, W["norm_scale"], 256, stack=stA)
        ssc_col, _ = emit_load_cols(P, W["score_scale"], 32, stack=stA)
        em.barrier()
        em.op("vector", "tensor_scalar", wre[:, :], wre[:, :], ns_col[:, 0:1], None, ALU.mult)
        em.op("vector", "tensor_scalar", wim[:, :], wim[:, :], ns_col[:, 1:2], None, ALU.mult)
        wc_col = P.sbuf([128, 17, 4], F32, stA)
        sp_col = P.sbuf([32, 1], F32, stA)
        wpoly = P.sbuf([128, 4], F32, stA)
        theta_bc = P.sbuf([128, 4096], F32, stA)
        with contextlib.ExitStack() as st0:
            stg = P.sbuf([4, 2080], F32, st0)
            row = P.sbuf([1, 4096], F32, st0)
            ps0 = P.psum([128, 1024], F32, st0)
            t1 = em.dma("sync", stg[:, :], W["w_conv"].rearrange("a b c -> (a b) c"), stream="wib_wc")
            em.wait("tensor", t1)
            pb = [None, None]
            for ft in range(17):
                r0, r1 = ft * 128, min(2080, ft * 128 + 128)
                slp = ft % 2
                em.wait("tensor", pb[slp])
                em.op("tensor", "transpose", ps0[0:r1 - r0, slp * 512:slp * 512 + 4],
                      stg[:, r0:r1], P.ident_f32[0:4, 0:4])
                tk = em.mark("tensor")
                em.wait("vector", tk)
                em.op("vector", "tensor_copy", wc_col[0:r1 - r0, ft, :],
                      ps0[0:r1 - r0, slp * 512:slp * 512 + 4])
                pb[slp] = em.mark("vector")
            t2 = em.dma("sync", row[0:1, 0:28],
                        W["decay_slopes"].rearrange("(a b) -> a b", a=1), stream="wib_sl")
            em.wait("sync", t2)
            t3 = em.dma("sync", row[0:1, 28:32],
                        W["anchor_slopes"].rearrange("(a b) -> a b", a=1), stream="wib_sl")
            em.wait("tensor", t3)
            em.wait("tensor", pb[0])
            em.wait("tensor", pb[1])
            em.op("tensor", "transpose", ps0[0:32, 0:1], row[0:1, 0:32],
                  P.ident_f32[0:1, 0:1])
            tk = em.mark("tensor")
            slopes_tr = tk
            em.wait("vector", tk)
            em.op("vector", "tensor_copy", sp_col[:, 0:1], ps0[0:32, 0:1])
            sv = em.mark("vector")
            em.wait("scalar", sv)
            em.op("scalar", "activation", dvec[0:32, 0:1] if False else sp_col[:, 0:1],
                  sp_col[:, 0:1], AF.Abs)
            em.mark("scalar")
            em.op("vector", "tensor_copy", wpoly[0:32, 3:4], ps0[0:32, 0:1])
            sv2 = em.mark("vector")
            em.wait("scalar", sv2)
            em.op("scalar", "activation", sp_col[:, 0:1], wpoly[0:32, 3:4], AF.Abs)
            ab = em.mark("scalar")
            em.op("scalar", "activation", sp_col[:, 0:1], sp_col[:, 0:1], AF.Exp, scale=-1.0)
            em.op("scalar", "activation", sp_col[:, 0:1], sp_col[:, 0:1], AF.Ln, bias=1.0)
            lg = em.mark("scalar")
            em.wait("vector", lg)
            em.op("vector", "tensor_scalar", wpoly[0:32, 3:4], wpoly[0:32, 3:4], 0.0,
                  None, ALU.max)
            em.op("vector", "tensor_tensor", sp_col[:, 0:1], sp_col[:, 0:1],
                  wpoly[0:32, 3:4], ALU.add)
            em.mark("vector")
            em.wait("sync", slopes_tr)
            t4 = em.dma("sync", row[0:1, 0:3],
                        W["deriv_logits"].rearrange("(a b) -> a b", a=1), stream="wib_dl")
            em.wait("vector", t4)
            em.op("vector", "reduce_max", row[0:1, 8:9], row[0:1, 0:3], AX)
            rv = em.mark("vector")
            em.wait("vector", rv)
            em.op("vector", "tensor_scalar", row[0:1, 4:7], row[0:1, 0:3], row[0:1, 8:9],
                  None, ALU.subtract)
            rv = em.mark("vector")
            em.wait("scalar", rv)
            em.op("scalar", "activation", row[0:1, 4:7], row[0:1, 4:7], AF.Exp,
                  accum_out=row[0:1, 9:10])
            se = em.mark("scalar")
            em.wait("vector", se)
            em.op("vector", "reciprocal", row[0:1, 10:11], row[0:1, 9:10])
            em.op("vector", "tensor_scalar", row[0:1, 4:7], row[0:1, 4:7],
                  row[0:1, 10:11], None, ALU.mult)
            em.op("vector", "tensor_scalar", row[0:1, 6:7], row[0:1, 6:7], -1.0, None,
                  ALU.mult)
            rv = em.mark("vector")
            em.wait("tensor", rv)
            em.op("tensor", "matmul", ps0[:, 0:3], P.csb["ones_f32"][0:1, :],
                  row[0:1, 4:7], start=True, stop=True)
            tk = em.mark("tensor")
            em.wait("vector", tk)
            em.op("vector", "tensor_copy", wpoly[:, 0:3], ps0[:, 0:3])
            em.mark("vector")
            em.wait("sync", tk)
            t5 = em.dma("sync", row[0:1, :],
                        W["theta"].rearrange("a b c d e -> (a b) (c d e)"), stream="wib_th")
            em.wait("tensor", t5)
            for cc in range(8):
                em.op("tensor", "matmul", ps0[:, 512:1024], P.csb["ones_f32"][0:1, :],
                      row[0:1, cc * 512:(cc + 1) * 512], start=True, stop=True)
                tk = em.mark("tensor")
                em.wait("vector", tk)
                em.op("vector", "tensor_copy", theta_bc[:, cc * 512:(cc + 1) * 512],
                      ps0[:, 512:1024])
                vv = em.mark("vector")
                em.wait("tensor", vv)
            em.barrier()
        carry_re = P.sbuf([128, 32], F32, stA)
        carry_im = P.sbuf([128, 32], F32, stA)
        carry_den = P.sbuf([32, 1], F32, stA)
        em.op("vector", "memset", carry_re[:, :], 0.0)
        em.op("vector", "memset", carry_im[:, :], 0.0)
        em.op("vector", "memset", carry_den[:, :], 0.0)
        ones_f = P.sbuf([128, QW], F32, stA)
        em.op("vector", "memset", ones_f[:, :], 1.0)
        em.mark("vector")
        em.barrier()
        for q in range(NQ):
            E = 128 if q > 0 else 0
            q0 = q * QW
            with contextlib.ExitStack() as stQ:
                dv_tok = em.dma("sync", dvec[:, 0:QW], P.cdr["l1a_dvec"][:, q0:q0 + QW],
                                stream="wib_dv")
                for e2 in ("vector", "scalar"):
                    em.wait(e2, dv_tok)
                zc = P.sbuf([128, 17, QW], BF16, stQ)
                stP = contextlib.ExitStack()
                xlnT = P.sbuf([128, 8, E + QW], BF16, stP)
                emit_ln(P, xin, q0 - E, q0 + QW, xlnT, 0)
                ztile = P.sbuf([128, 2, E + QW], F32, stP)
                ps = P.psum([128, 1024], F32, stP)
                psb = [None, None]
                zt_busy = [None, None]
                for ft in range(17):
                    r0, r1 = ft * 128, min(2080, ft * 128 + 128)
                    rows = r1 - r0
                    sl = ft % 2
                    em.wait("vector", zt_busy[sl])
                    for tc in range(ceil_div(E + QW, 512)):
                        c0, c1 = tc * 512, min(E + QW, (tc + 1) * 512)
                        slp = tc % 2
                        em.wait("tensor", psb[slp])
                        for d in range(8):
                            em.op("tensor", "matmul", ps[0:rows, slp * 512:slp * 512 + (c1 - c0)],
                                  winA[:, d, r0:r1], xlnT[:, d, c0:c1],
                                  start=(d == 0), stop=(d == 7))
                        mt = em.mark("tensor")
                        em.wait("vector", mt)
                        em.op("vector", "tensor_copy", ztile[0:rows, sl, c0:c1],
                              ps[0:rows, slp * 512:slp * 512 + (c1 - c0)])
                        psb[slp] = em.mark("vector")
                    em.op("vector", "tensor_scalar", zc[0:rows, ft, :],
                          ztile[0:rows, sl, E:E + QW], wc_col[0:rows, ft, 3:4],
                          bconv_col[0:rows, ft:ft + 1], ALU.mult, ALU.add)
                    for dd in range(1, 4):
                        lo = E - dd
                        if lo >= 0:
                            em.op("vector", "scalar_tensor_tensor", zc[0:rows, ft, :],
                                  ztile[0:rows, sl, lo:lo + QW], wc_col[0:rows, ft, 3 - dd:4 - dd],
                                  zc[0:rows, ft, :], ALU.mult, ALU.add)
                        else:
                            em.op("vector", "scalar_tensor_tensor", zc[0:rows, ft, dd:],
                                  ztile[0:rows, sl, 0:QW - dd], wc_col[0:rows, ft, 3 - dd:4 - dd],
                                  zc[0:rows, ft, dd:], ALU.mult, ALU.add)
                    zt_busy[sl] = em.mark("vector")
                zc_tok = em.mark("vector")
                em.barrier()
                stP.close()
                pw = P.sbuf([32, QW], F32, stQ)
                tw = P.sbuf([32, QW], F32, stQ)
                denc = P.sbuf([32, QW], F32, stQ)
                em.op("vector", "tensor_scalar", pw[:, :], zc[0:32, 16, :],
                      ssc_col[0:32, 0:1], None, ALU.mult)
                em.op("vector", "tensor_scalar", pw[:, :], pw[:, :], 20.0, None, ALU.min)
                em.op("vector", "tensor_scalar", pw[:, :], pw[:, :], -20.0, None, ALU.max)
                em.op("vector", "tensor_scalar", tw[:, :], dvec[0:32, 0:QW],
                      sp_col[0:32, 0:1], None, ALU.mult)
                pv = em.mark("vector")
                em.wait("scalar", pv)
                em.op("scalar", "activation", pw[:, :], pw[:, :], AF.Exp)
                em.op("scalar", "activation", tw[:, :], tw[:, :], AF.Exp, scale=-1.0)
                pe = em.mark("scalar")
                em.wait("vector", pe)
                em.op("vector", "tensor_tensor", pw[:, :], pw[:, :], tw[:, :], ALU.mult)
                em.op("vector", "tensor_tensor_scan", denc[:, :], ones_f[0:32, :],
                      pw[:, :], carry_den[:, 0:1], ALU.mult, ALU.add)
                em.op("vector", "tensor_copy", carry_den[:, 0:1], denc[:, QW - 1:QW])
                em.op("vector", "tensor_scalar", tw[:, :], denc[:, :], 1e-4, None, ALU.max)
                em.op("vector", "reciprocal", tw[:, :], tw[:, :])
                pw_tok = em.mark("vector")
                pwT = P.sbuf([128, NTTQ, 32], F32, stQ)
                ps_f = P.psum([128, 1024], F32, stQ)
                pfb = [None, None]
                for tt in range(NTTQ):
                    slp = tt % 2
                    em.wait("tensor", pw_tok)
                    em.wait("tensor", pfb[slp])
                    em.op("tensor", "transpose", ps_f[0:128, slp * 512:slp * 512 + 32],
                          pw[:, tt * 128:(tt + 1) * 128], P.ident_f32[0:32, 0:32])
                    tk = em.mark("tensor")
                    em.wait("vector", tk)
                    em.op("vector", "tensor_copy", pwT[:, tt, :],
                          ps_f[0:128, slp * 512:slp * 512 + 32])
                    pfb[slp] = em.mark("vector")
                em.wait("tensor", pfb[0])
                em.wait("tensor", pfb[1])
                if DBG.get("l1aa"):
                    em.wait("sync", zc_tok)
                    em.wait("sync", pw_tok)
                    for ft in range(17):
                        rz = 128 if ft < 16 else 32
                        dz = em.dma("sync", dbg_zc[ft, 0:rz, q0:q0 + QW], zc[0:rz, ft, :], stream="dbgz")
                    dz = em.dma("sync", dbg_pw[:, q0:q0 + QW], pw[:, :], stream="dbgz")
                    dz = em.dma("sync", dbg_den[:, q0:q0 + QW], denc[:, :], stream="dbgz")
                    for e2 in ("vector", "scalar", "tensor", "gpsimd"):
                        em.wait(e2, dz)
                ps_t = P.psum([128, 2048], BF16, stQ)
                yraw = P.sbuf([128, NTTQ, 1024], BF16, stQ)
                ps_sn = P.psum([128, 1024], F32, stQ)
                ps_y = P.psum([128, 1024], F32, stQ)
                stK = contextlib.ExitStack()
                xv_tm = P.sbuf([128, NTTQ, 8, 128], BF16, stK)
                ptb = [None, None]
                for tt in range(NTTQ):
                    for ft in range(8):
                        slp = (tt * 8 + ft) % 2
                        em.wait("tensor", zc_tok)
                        em.wait("tensor", ptb[slp])
                        em.op("tensor", "transpose",
                              ps_t[0:128, slp * 1024:slp * 1024 + 128],
                              zc[:, ft, tt * 128:(tt + 1) * 128], P.ident_bf)
                        tk = em.mark("tensor")
                        em.wait("vector", tk)
                        em.op("vector", "tensor_copy", xv_tm[:, tt, ft, :],
                              ps_t[0:128, slp * 1024:slp * 1024 + 128])
                        ptb[slp] = em.mark("vector")
                em.wait("tensor", ptb[0])
                em.wait("tensor", ptb[1])
                phi2 = P.sbuf([128, 2, QW], F32, stK)
                phc2 = P.sbuf([128, 2, QW], F32, stK)
                rrf = P.sbuf([128, QW], F32, stK)
                sinv = P.sbuf([128, QW], F32, stK)
                cosv = P.sbuf([128, QW], F32, stK)
                poly = P.sbuf([128, NTTQ, 32], F32, stK)
                pp2 = P.sbuf([128, 2, NTTQ, 32], F32, stK)
                mreT2 = P.sbuf([128, 2, QW], BF16, stK)
                mimT2 = P.sbuf([128, 2, QW], BF16, stK)
                mre = P.sbuf([128, QW], BF16, stK)
                mim = P.sbuf([128, QW], BF16, stK)
                cre = P.sbuf([128, QW], BF16, stK)
                cim = P.sbuf([128, QW], BF16, stK)
                sqb = P.sbuf([128, QW], BF16, stK)
                sin_done = [None, None]
                mg_done = [None, None]
                tp_done = [None, None]
                scan_tok = None
                ym_tok = None
                yb = [None, None]
                sn_last = None
                BIGRND = 12582912.0
                def stageA(k):
                    s = k % 2
                    off = (k // 4) * 128 + (k % 4) * 32
                    xv3 = bass.AP(xv_tm, off, [[NTTQ * 1024, 128], [1024, NTTQ], [1, 32]])
                    xv_bc = bass.AP(xv_tm, off,
                                    [[NTTQ * 1024, 128], [1024, NTTQ], [1, 32], [0, 4]])
                    th_sl = bass.AP(theta_bc, k * 128,
                                    [[4096, 128], [0, NTTQ], [4, 32], [1, 4]])
                    phi = phi2[:, s, :]
                    phc = phc2[:, s, :]
                    em.wait("vector", sin_done[s])
                    em.op("vector", "tensor_tensor",
                          phi.rearrange("p (a b c) -> p a b c", a=NTTQ, c=4),
                          xv_bc, th_sl, ALU.mult)
                    em.op("vector", "tensor_scalar", phc, phi, HALF_PI, None,
                          ALU.add)
                    for srcb in (phi, phc):
                        em.op("vector", "tensor_scalar", rrf[:, :], srcb, INV2PI,
                              BIGRND, ALU.mult, ALU.add)
                        em.op("vector", "tensor_scalar", rrf[:, :], rrf[:, :], -BIGRND,
                              None, ALU.add)
                        em.op("vector", "scalar_tensor_tensor", srcb, rrf[:, :],
                              -TWO_PI, srcb, ALU.mult, ALU.add)
                    tr = em.mark("vector")
                    em.wait("scalar", tr)
                    em.wait("scalar", mg_done[0])
                    em.wait("scalar", mg_done[1])
                    em.op("scalar", "activation", sinv[:, :], phi, AF.Sin)
                    em.op("scalar", "activation", cosv[:, :], phc, AF.Sin)
                    sin_done[s] = em.mark("scalar")
                    em.wait("vector", mg_done[s])
                    em.op("vector", "tensor_tensor", poly[:, :, :], xv3, xv3, ALU.mult)
                    em.op("vector", "tensor_scalar", poly[:, :, :], poly[:, :, :],
                          wpoly[:, 2:3], None, ALU.mult)
                    em.op("vector", "scalar_tensor_tensor", pp2[:, s, :, :], xv3,
                          wpoly[:, 1:2], poly[:, :, :], ALU.mult, ALU.add)
                    em.op("vector", "tensor_scalar", poly[:, :, :], pp2[:, s, :, :],
                          wpoly[:, 0:1], None, ALU.add)
                    em.op("vector", "tensor_tensor", pp2[:, s, :, :], poly[:, :, :],
                          bass.AP(pwT, k, [[NTTQ * 32, 128], [32, NTTQ], [0, 32]]),
                          ALU.mult)
                    pp_tok = em.mark("vector")
                    pp_bc = bass.AP(pp2, s * NTTQ * 32,
                                    [[2 * NTTQ * 32, 128], [32, NTTQ], [1, 32], [0, 4]])
                    em.wait("gpsimd", pp_tok)
                    em.wait("gpsimd", sin_done[s])
                    em.wait("gpsimd", tp_done[s])
                    em.op("gpsimd", "tensor_tensor",
                          mreT2[:, s, :].rearrange("p (a b c) -> p a b c", a=NTTQ, c=4),
                          cosv[:, :].rearrange("p (a b c) -> p a b c", a=NTTQ, c=4),
                          pp_bc, ALU.mult)
                    em.op("gpsimd", "tensor_tensor",
                          mimT2[:, s, :].rearrange("p (a b c) -> p a b c", a=NTTQ, c=4),
                          sinv[:, :].rearrange("p (a b c) -> p a b c", a=NTTQ, c=4),
                          pp_bc, ALU.mult)
                    mg_done[s] = em.mark("gpsimd")

                def stageB(k):
                    s = k % 2
                    first_copy = True
                    for tt in range(NTTQ):
                        for which, mt_, ct_ in ((0, mreT2, cre), (1, mimT2, cim)):
                            slp = which
                            em.wait("tensor", mg_done[s])
                            em.wait("tensor", ptb[slp])
                            em.op("tensor", "transpose",
                                  ps_t[0:128, slp * 1024 + 128:slp * 1024 + 256],
                                  mt_[:, s, tt * 128:(tt + 1) * 128], P.ident_bf)
                            tk = em.mark("tensor")
                            em.wait("scalar", tk)
                            if first_copy:
                                em.wait("scalar", state["scan_tok"])
                                first_copy = False
                            em.op("scalar", "copy", ct_[:, tt * 128:(tt + 1) * 128],
                                  ps_t[0:128, slp * 1024 + 128:slp * 1024 + 256])
                            ptb[slp] = em.mark("scalar")
                    tp_done[s] = em.mark("tensor")
                    mtr = em.mark("scalar")
                    em.wait("vector", mtr)
                    em.wait("vector", state["ym_tok"])
                    em.op("vector", "tensor_tensor_scan", mre[:, :], ones_f[:, :],
                          cre[:, :], carry_re[:, k:k + 1], ALU.mult, ALU.add)
                    em.op("vector", "tensor_copy", carry_re[:, k:k + 1], mre[:, QW - 1:QW])
                    em.op("vector", "tensor_tensor_scan", mim[:, :], ones_f[:, :],
                          cim[:, :], carry_im[:, k:k + 1], ALU.mult, ALU.add)
                    em.op("vector", "tensor_copy", carry_im[:, k:k + 1], mim[:, QW - 1:QW])
                    state["scan_tok"] = em.mark("vector")
                    em.op("vector", "tensor_tensor", sqb[:, :], mre[:, :], mre[:, :],
                          ALU.mult)
                    sq1 = em.mark("vector")
                    em.wait("tensor", sq1)
                    for cc in range(QW // 512):
                        em.op("tensor", "matmul", ps_sn[0:32, cc * 512:(cc + 1) * 512],
                              e3b[:, k * 32:(k + 1) * 32], sqb[:, cc * 512:(cc + 1) * 512],
                              start=(k == 0), stop=False)
                    sqm = em.mark("tensor")
                    em.wait("vector", sqm)
                    em.op("vector", "tensor_tensor", sqb[:, :], mim[:, :], mim[:, :],
                          ALU.mult)
                    sq2 = em.mark("vector")
                    em.wait("tensor", sq2)
                    for cc in range(QW // 512):
                        em.op("tensor", "matmul", ps_sn[0:32, cc * 512:(cc + 1) * 512],
                              e3b[:, k * 32:(k + 1) * 32], sqb[:, cc * 512:(cc + 1) * 512],
                              start=False, stop=(k == 31))
                    state["sn_last"] = em.mark("tensor")
                    ysl = k % 2
                    em.wait("tensor", yb[ysl])
                    for tt in range(NTTQ):
                        em.op("tensor", "matmul",
                              ps_y[:, ysl * 512 + tt * 32:ysl * 512 + (tt + 1) * 32],
                              mre[:, tt * 128:(tt + 1) * 128], wre[:, :],
                              start=True, stop=False)
                        em.op("tensor", "matmul",
                              ps_y[:, ysl * 512 + tt * 32:ysl * 512 + (tt + 1) * 32],
                              mim[:, tt * 128:(tt + 1) * 128], wim[:, :],
                              start=False, stop=True)
                    ym = em.mark("tensor")
                    state["ym_tok"] = ym
                    em.wait("vector", ym)
                    em.op("vector", "tensor_copy",
                          bass.AP(yraw, k * 32, [[NTTQ * 1024, 128], [1024, NTTQ], [1, 32]]),
                          ps_y[:, ysl * 512:ysl * 512 + NTTQ * 32]
                          .rearrange("p (a b) -> p a b", b=32))
                    yb[ysl] = em.mark("vector")
                    if DBG.get("l1aa") and k == 0:
                        em.wait("sync", state["scan_tok"])
                        dz = em.dma("sync", dbg_cre[:, q0:q0 + QW], mre[:, :], stream="dbgz")
                        dz = em.dma("sync", dbg_mrg[:, q0:q0 + QW], cre[:, :], stream="dbgz")
                        for e2 in ("vector", "scalar", "tensor", "gpsimd"):
                            em.wait(e2, dz)

                state = {"scan_tok": None, "ym_tok": None, "sn_last": None}
                stageA(0)
                for k in range(32):
                    if k + 1 < 32:
                        stageA(k + 1)
                    stageB(k)
                sn_last = state["sn_last"]
                em.barrier()
                if not DBG.get("l1aa"):
                    stK.close()
                if DBG.get("l1aa"):
                    for tt in range(NTTQ):
                        dz = em.dma("sync", dbg_yr[(q * NTTQ + tt) * 128:(q * NTTQ + tt + 1) * 128, :],
                                    yraw[:, tt, :], stream="dbgz")
                    for e2 in ("vector", "scalar", "tensor", "gpsimd"):
                        em.wait(e2, dz)
                sT = P.sbuf([128, NTTQ, 32], F32, stQ)
                em.op("vector", "tensor_tensor", denc[:, :], tw[:, :], tw[:, :], ALU.mult)
                em.op("vector", "tensor_tensor", denc[:, :], ps_sn[0:32, 0:QW], denc[:, :],
                      ALU.mult)
                em.op("vector", "tensor_scalar", denc[:, :], denc[:, :], 1.0 / 256.0,
                      1e-5, ALU.mult, ALU.add)
                sv = em.mark("vector")
                em.wait("scalar", sv)
                em.op("scalar", "activation", denc[:, :], denc[:, :], AF.Ln)
                em.op("scalar", "activation", denc[:, :], denc[:, :], AF.Exp, scale=-0.5)
                sv = em.mark("scalar")
                em.wait("vector", sv)
                em.op("vector", "tensor_tensor", denc[:, :], denc[:, :], tw[:, :], ALU.mult)
                s_tok = em.mark("vector")
                if DBG.get("l1aa"):
                    em.wait("sync", s_tok)
                    dz = em.dma("sync", dbg_s[:, q0:q0 + QW], denc[:, :], stream="dbgz")
                    for e2 in ("vector", "scalar", "tensor", "gpsimd"):
                        em.wait(e2, dz)
                pfb = [None, None]
                for tt in range(NTTQ):
                    slp = tt % 2
                    em.wait("tensor", s_tok)
                    em.wait("tensor", pfb[slp])
                    em.op("tensor", "transpose", ps_f[0:128, slp * 512:slp * 512 + 32],
                          denc[:, tt * 128:(tt + 1) * 128], P.ident_f32[0:32, 0:32])
                    tk = em.mark("tensor")
                    em.wait("vector", tk)
                    em.op("vector", "tensor_copy", sT[:, tt, :],
                          ps_f[0:128, slp * 512:slp * 512 + 32])
                    pfb[slp] = em.mark("vector")
                ygT = P.sbuf([128, 8, QW], BF16, stQ)
                ytmp = P.sbuf([128, 1024], BF16, stQ)
                gtile = P.sbuf([128, 2, 128], BF16, stQ)
                ptb2 = [None, None]
                yt_tok = None
                for tt in range(NTTQ):
                    s_bc = bass.AP(sT, tt * 32, [[NTTQ * 32, 128], [1, 32], [0, 32]])
                    em.wait("vector", pfb[0])
                    em.wait("vector", pfb[1])
                    em.wait("vector", yt_tok)
                    em.op("vector", "tensor_tensor",
                          ytmp[:, :].rearrange("p (a b) -> p a b", b=32),
                          yraw[:, tt, :].rearrange("p (a b) -> p a b", b=32),
                          s_bc, ALU.mult)
                    yv = em.mark("vector")
                    for d in range(8):
                        slp = d % 2
                        em.wait("tensor", yv)
                        em.wait("tensor", ptb2[slp])
                        em.op("tensor", "transpose",
                              ps_t[0:128, slp * 1024 + 256:slp * 1024 + 384],
                              ytmp[:, d * 128:(d + 1) * 128], P.ident_bf)
                        tk = em.mark("tensor")
                        em.wait("scalar", ptb2[slp])
                        em.op("scalar", "activation", gtile[:, slp, :],
                              zc[:, 8 + d, tt * 128:(tt + 1) * 128], AF.Silu)
                        gtk = em.mark("scalar")
                        em.wait("vector", tk)
                        em.wait("vector", gtk)
                        em.op("vector", "tensor_tensor", ygT[:, d, tt * 128:(tt + 1) * 128],
                              gtile[:, slp, :],
                              ps_t[0:128, slp * 1024 + 256:slp * 1024 + 384], ALU.mult)
                        ptb2[slp] = em.mark("vector")
                    yt_tok = ptb2[1] if ptb2[1] is not None else ptb2[0]
                em.wait("tensor", ptb2[0])
                em.wait("tensor", ptb2[1])
                xres = P.sbuf([128, 2, 1024], F32, stQ)
                yout = P.sbuf([128, 2, 1024], F32, stQ)
                x_busy = [None, None]
                y_busy = [None, None]
                psy_busy = None
                for tt in range(NTTQ):
                    gtt = q * NTTQ + tt
                    sl = tt % 2
                    em.wait("sync", x_busy[sl])
                    xtok = em.dma("sync", xres[:, sl, :], xin[gtt * 128:(gtt + 1) * 128, :],
                                  stream=f"xr{sl}")
                    em.wait("tensor", psy_busy)
                    for fc in range(2):
                        for dt in range(8):
                            em.op("tensor", "matmul", ps_f[:, fc * 512:(fc + 1) * 512],
                                  ygT[:, dt, tt * 128:(tt + 1) * 128],
                                  WoutA[:, dt, fc * 512:(fc + 1) * 512],
                                  start=(dt == 0), stop=(dt == 7))
                    ym = em.mark("tensor")
                    em.wait("vector", ym)
                    em.wait("vector", xtok)
                    em.wait("vector", y_busy[sl])
                    em.op("vector", "tensor_tensor", yout[:, sl, :], xres[:, sl, :],
                          ps_f[:, :], ALU.add)
                    ev = em.mark("vector")
                    psy_busy = ev
                    x_busy[sl] = ev
                    em.wait("sync", ev)
                    y_busy[sl] = em.dma("sync", xout[gtt * 128:(gtt + 1) * 128, :],
                                        yout[:, sl, :], stream=f"yo{sl}")
                em.barrier()


PHASE_FNS["l1aa"] = ph_l1_attn_a
